# version 11
# baseline (speedup 1.0000x reference)
import numpy as np
from contextlib import ExitStack

import concourse.bass as bass
import concourse.mybir as mybir
from concourse.bass_utils import run_bass_kernel_spmd

F32 = mybir.dt.float32
F32R = mybir.dt.float32r
BF16 = mybir.dt.bfloat16
I32 = mybir.dt.int32
ALU = mybir.AluOpType
ACT = mybir.ActivationFunctionType
AX = mybir.AxisListType

ENGINES = ("sync", "vector", "scalar", "tensor", "gpsimd")
RG = [[0, 1, 2, 3, 4, 5, 6, 7]]

S, D, NH, DH, I3 = 2048, 1024, 4, 256, 3072
NTC = S // 128
NFC = D // 128
NIC = I3 // 128

N_PHASES = 99
FFN_STAGES = 99
NOG_LIM = 99


class Prog:

    N_DMA = {"sync": 8, "gpsimd": 8, "scalar": 2}

    def __init__(self, nc, es):
        self.nc = nc
        self.sems = {}
        for name in ("sv", "ss", "st_", "sg", "sy"):
            self.sems[name] = [es.enter_context(nc.semaphore(name)), 0]
        self.own = {"vector": "sv", "scalar": "ss", "tensor": "st_",
                    "gpsimd": "sg", "sync": "sy"}
        for eng, n in self.N_DMA.items():
            for i in range(n):
                name = "q_%s%d" % (eng, i)
                self.sems[name] = [es.enter_context(nc.semaphore(name)), 0]
        self.N_TOK = 8
        for e in ENGINES:
            for i in range(self.N_TOK):
                nm = "t_%s%d" % (e, i)
                self.sems[nm] = [es.enter_context(nc.semaphore(nm)), 0]
        self.tok_rr = {e: 0 for e in ENGINES}
        self.ops = {e: [] for e in ENGINES}
        self.guar = {e: {} for e in ENGINES}
        self.lastw = {}
        self.lastr = {}
        self.rec = []

    @staticmethod
    def _is_ap(a):
        return hasattr(a, "tensor") and hasattr(a, "offset") and hasattr(a, "ap")

    @staticmethod
    def _rng(ap):
        t = ap.tensor
        import concourse.mybir as _mb
        esz = _mb.dt.size(ap.dtype)
        shape = list(t.shape)
        row = shape[-1] if len(shape) > 1 else shape[0]
        tesz = _mb.dt.size(t.dtype)
        row_bytes = row * tesz
        off_b = ap.offset * esz
        col_lo = off_b % row_bytes if row_bytes else 0
        span = 0
        for step, count in list(ap.ap):
            sb = abs(int(step)) * esz
            if row_bytes and sb % row_bytes == 0 and sb != 0:
                continue
            span += (int(count) - 1) * sb
        return (t.name, col_lo, col_lo + span + esz)

    _WRITES = {
        "tensor_scalar": [0], "tensor_tensor": [0], "tensor_copy": [0],
        "tensor_sub": [0], "tensor_add": [0], "tensor_mul": [0],
        "tensor_max": [0], "scalar_tensor_tensor": [0],
        "tensor_tensor_scan": [0], "reduce_max": [0], "reduce_sum": [0],
        "tensor_reduce": [0], "reciprocal": [0], "memset": [0],
        "activation": [0], "transpose": [0], "matmul": [0], "iota": [0],
    }

    def _autodeps(self, eng, meth, args, kw, tokval_next):
        reads, writes = [], []
        wpos = self._WRITES.get(meth, [])
        for i, a in enumerate(args):
            if self._is_ap(a):
                (writes if i in wpos else reads).append(a)
        for k, a in kw.items():
            if self._is_ap(a):
                if k in ("out", "accum_out"):
                    writes.append(a)
                else:
                    reads.append(a)
        deps = []
        psum_pe = (eng == "tensor")
        for a in reads:
            r = self._rng(a)
            for (lo, hi, tk, teng) in self.lastw.get(r[0], []):
                if hi > r[1] and r[2] > lo:
                    if psum_pe and teng == "tensor" and r[0].startswith("pb"):
                        continue
                    deps.append(tk)
        for a in writes:
            r = self._rng(a)
            for m in (self.lastw, self.lastr):
                for (lo, hi, tk, teng) in m.get(r[0], []):
                    if hi > r[1] and r[2] > lo:
                        if psum_pe and teng == "tensor" and r[0].startswith("pb"):
                            continue
                        deps.append(tk)
        return deps, reads, writes

    def _note(self, eng, reads, writes, tokval):
        for a in writes:
            nm, lo, hi = self._rng(a)
            lst = self.lastw.setdefault(nm, [])
            lst[:] = [e for e in lst if not (lo <= e[0] and e[1] <= hi)]
            lst.append((lo, hi, tokval, eng))
            rl = self.lastr.setdefault(nm, [])
            rl[:] = [e for e in rl if not (lo <= e[0] and e[1] <= hi)]
        for a in reads:
            nm, lo, hi = self._rng(a)
            self.lastr.setdefault(nm, []).append((lo, hi, tokval, eng))

    @staticmethod
    def _flat(deps):
        out = []
        for d in deps:
            if d is None:
                continue
            if isinstance(d, list):
                out.extend(x for x in d if x is not None)
            else:
                out.append(d)
        return out

    def emit(self, eng, meth, *args, deps=(), tok=False, _inc=None,
             note_tok=None, _skip=None, **kw):
        if eng == "tensor" and meth == "nop":
            ps1, id1 = self.tensor_nop_args
            meth = "transpose"
            args = (ps1, id1, id1)
        adeps, rds, wrs = self._autodeps(eng, meth, args, kw, None)
        oid = len(self.rec)
        all_deps = []
        for d in self._flat(list(deps) + adeps):
            if _skip is not None and d == _skip:
                continue
            all_deps.append(d)
        qtok = None
        if _inc is not None:
            q = _inc
            self.sems[q][1] += 16
            qtok = (q, self.sems[q][1])
        self.rec.append({"eng": eng, "meth": meth, "args": args, "kw": kw,
                         "deps": all_deps, "qtok": qtok})
        tokval = oid if qtok is None else (note_tok if note_tok is not None
                                           else qtok)
        self._note(eng, rds, wrs, oid if qtok is None else tokval)
        return tokval

    def finalize(self):
        used = set()
        for r in self.rec:
            for d in r["deps"]:
                if isinstance(d, int):
                    used.add(d)
        assign = {}
        for oid, r in enumerate(self.rec):
            if r["qtok"] is not None:
                assign[oid] = ("__dma__", r["qtok"])
                continue
            if oid in used:
                eng = r["eng"]
                q = "t_%s%d" % (eng, self.tok_rr[eng] % self.N_TOK)
                self.tok_rr[eng] += 1
                drain = (q, self.sems[q][1])
                self.sems[q][1] += 1
                assign[oid] = (q, self.sems[q][1])
                r["drain"] = drain
        for oid, r in enumerate(self.rec):
            eng = r["eng"]
            waits = []
            wl = list(r["deps"])
            if "drain" in r:
                wl.append(r["drain"])
            for d in wl:
                if isinstance(d, int):
                    a = assign[d]
                    d = a[1] if a[0] == "__dma__" else a
                sname, val = d
                if self.guar[eng].get(sname, 0) >= val:
                    continue
                self.guar[eng][sname] = val
                waits.append((self.sems[sname][0], val))
            inc = None
            if r["qtok"] is not None:
                inc = (self.sems[r["qtok"][0]][0], 16)
            elif oid in assign:
                a = assign[oid]
                inc = (self.sems[a[0]][0], 1)

            def run(e, meth=r["meth"], args=r["args"], kw=r["kw"],
                    waits=waits, inc=inc):
                for h, v in waits:
                    e.wait_ge(h, v)
                ins = getattr(e, meth)(*args, **kw)
                if inc is not None:
                    ins.then_inc(*inc)
            self.ops[eng].append(run)

    def dma(self, eng, slot, pairs, deps=()):
        q = "q_%s%d" % (eng, slot % self.N_DMA[eng])
        drain = (q, self.sems[q][1])
        final = (q, self.sems[q][1] + 16 * len(pairs))
        for j, (o, i_) in enumerate(pairs):
            dd = list(deps) + [drain] if j == 0 else ()
            self.emit(eng, "dma_start", out=o, in_=i_, deps=dd, _inc=q,
                      note_tok=final, _skip=final)
        return final

    def qtot(self, eng):
        return [("q_%s%d" % (eng, i), self.sems["q_%s%d" % (eng, i)][1])
                for i in range(self.N_DMA[eng])]

    def barrier(self):
        toks = {}
        for e in ENGINES:
            toks[e] = self.emit(e, "nop")
        qvals = []
        for eng in self.N_DMA:
            qvals += self.qtot(eng)
        for e in ENGINES:
            deps = [toks[e2] for e2 in ENGINES if e2 != e] + qvals
            self.emit(e, "nop", deps=deps)


def build_program():
    nc = bass.Bass("TRN2", target_bir_lowering=False)

    def din(name, shape, dt=F32):
        return nc.dram_tensor(name, shape, dt, kind="ExternalInput").ap()

    def dint(name, shape, dt=F32):
        return nc.dram_tensor(name, shape, dt, kind="Internal").ap()

    x_d = din("x", [S, D])
    wup_d = din("wup", [I3, D])
    wgate_d = din("wgate", [I3, D])
    wdown_d = din("wdown", [D, I3])
    bdq_d = din("bdq", [NFC, 128, 128])
    bdk_d = din("bdk", [NFC, 128, 128])
    bdv_d = din("bdv", [NFC, 128, 128])
    wit_d = din("wit", [I3, 4])
    wft_d = din("wft", [I3, 4])
    bi_d = din("bi", [4, 1])
    bf_d = din("bf4", [4, 1])
    lnw_d = din("lnw", [1, D])
    ident_d = din("ident", [128, 128])
    maskd_d = din("maskd", [128, 2048])
    onesk_d = din("onesk", [1, 128])
    sel4_d = din("sel4", [4, 512])
    out_d = nc.dram_tensor("out", [S, D], F32, kind="ExternalOutput").ap()

    pu_d = dint("pu", [I3, S])
    pg_d = dint("pg", [I3, S])
    m1_d = dint("m1", [I3, S])
    op_d = dint("op", [I3, S])
    nohi_d = dint("nohi", [I3, S], BF16)
    nolo_d = dint("nolo", [I3, S], BF16)
    wuphi_d = dint("wuphi", [D, I3], BF16)
    wuplo_d = dint("wuplo", [D, I3], BF16)
    wghi_d = dint("wghi", [D, I3], BF16)
    wglo_d = dint("wglo", [D, I3], BF16)
    wdhi_d = dint("wdhi", [I3, D], BF16)
    wdlo_d = dint("wdlo", [I3, D], BF16)
    xfm_d = dint("xfmsp", [D, S])
    ccin_d = [dint("ccin%d" % k, [1, 2]) for k in range(8)]
    ccout_d = [dint("ccout%d" % k, [1, 2]) for k in range(8)]

    es = ExitStack()

    def sb(name, shape, dt=F32):
        return es.enter_context(nc.sbuf_tensor(name, shape, dt))

    with es:
        xres = sb("xres", [128, NTC * 1024])
        arA = sb("arA", [128, 12288])
        arB = sb("arB", [128, 6144])
        nhi = sb("nhi", [128, 16384], BF16)
        nlo = sb("nlo", [128, 16384], BF16)
        colA = sb("colA", [128, 80])
        gcols = sb("gcols", [128, 192])
        identt = sb("identt", [128, 128])
        onest = sb("onest", [1, 128])
        sel4t = sb("sel4t", [4, 512])
        scrow = sb("scrow", [1, 64])
        ccio = sb("ccio", [1, 4])
        psb = [es.enter_context(nc.psum_tensor("pb%d" % i, [128, 512], F32))
               for i in range(8)]

        arA_bf = arA[:].bitcast(BF16)
        arB_i32 = arB[:].bitcast(I32)
        nhi_f = nhi[:].bitcast(F32)
        nlo_f = nlo[:].bitcast(F32)

        p = Prog(nc, es)
        p.tensor_nop_args = (psb[7][0:1, 0:1], identt[0:1, 0:1])
        E = p.emit
        cc_ev = [0]

        def cA(i):
            return colA[:, i:i + 1]

        def srow(i, n=1):
            return scrow[0:1, i:i + n]

        p.dma("sync", 0, [(xres[:, 1024 * c:1024 * (c + 1)],
                           x_d[128 * c:128 * (c + 1), :]) for c in range(NTC)])
        p.dma("sync", 1, [(identt[:], ident_d[:, :]), (onest[:], onesk_d[:, :]),
                          (sel4t[:], sel4_d[:, :])])
        E("vector", "memset", colA[:, 79:80], 1e-5, tok=True)
        E("vector", "memset", colA[:, 78:79], 1.0, tok=True)
        p.barrier()

        wp_state = {"load": [None, None], "done": [None, None],
                    "stg": [None, None], "t5": None}

        def w_prep(w_nat, hi_dst, lo_dst, O, I):
            noch = O // 128
            nb = I // 32
            arB_bf = arB[:].bitcast(BF16)
            stg_hi = [arB_bf[:, 8704:9216], arB_bf[:, 9216:9728]]
            stg_lo = [arB_bf[:, 9728:10240], arB_bf[:, 10240:10752]]
            tok_load = wp_state["load"]
            tok_done = wp_state["done"]
            tok_stg = wp_state["stg"]
            bm = arB[:, 4000:4000 + nb]
            sA = arB[:, 4128:4128 + nb]
            rs = arB[:, 4256:4256 + nb]
            for och in range(noch):
                b = och % 2
                wnat = arA[:, I * b: I * (b + 1)]
                wdq = arA[:, (2 + b) * I: (3 + b) * I]
                dl = [tok_done[b], wp_state.get("t5_%d" % b)]
                if och < 2:
                    dl += [tok_done[0], tok_done[1], wp_state.get("t5_0"),
                           wp_state.get("t5_1"), tok_stg[0], tok_stg[1]]
                tok_load[b] = p.dma("sync", b, [(wnat,
                                w_nat[128 * och:128 * (och + 1), :])],
                                deps=dl)
                w3 = wnat.rearrange("p (g e) -> p g e", e=32)
                t1 = E("vector", "reduce_max", bm.rearrange("p (g o) -> p g o", o=1),
                       w3, axis=AX.X, apply_absolute_value=True,
                       deps=[tok_load[b]], tok=True)
                t2 = E("vector", "tensor_scalar", sA, bm, 1.0 / 7.0, 1e-12,
                       ALU.mult, ALU.max, deps=[t1], tok=True)
                t3 = E("vector", "reciprocal", rs, sA, deps=[t2], tok=True)
                q32 = arB_i32[:, 0:I]
                t4 = E("vector", "tensor_tensor",
                       q32.rearrange("p (g e) -> p g e", e=32), w3,
                       rs.rearrange("p (g o) -> p g o", o=1).broadcast_to((128, nb, 32)),
                       ALU.mult, deps=[t3], tok=True)
                t5 = E("vector", "tensor_tensor",
                       wdq.rearrange("p (g e) -> p g e", e=32),
                       q32.rearrange("p (g e) -> p g e", e=32),
                       sA.rearrange("p (g o) -> p g o", o=1).broadcast_to((128, nb, 32)),
                       ALU.mult, deps=[t4, tok_done[0], tok_done[1]], tok=True)
                wp_state["t5_%d" % b] = t5
                last_tp = None
                for ts_ in range(I // 512):
                    bank = psb[4 + (ts_ % 2)]
                    sb_ = ts_ % 2
                    for k in range(4):
                        last_tp = E("tensor", "transpose", bank[:, 128 * k:128 * (k + 1)],
                                    wdq[:, 512 * ts_ + 128 * k: 512 * ts_ + 128 * (k + 1)],
                                    identt[:],
                                    deps=[t5, tok_stg[sb_]] if k == 0 else [],
                                    tok=(k == 3))
                    th = E("vector", "tensor_copy", stg_hi[sb_], bank[:],
                           deps=[last_tp, tok_stg[sb_]], tok=True)
                    tl = E("vector", "tensor_sub", stg_lo[sb_], bank[:], stg_hi[sb_],
                           deps=[last_tp, th], tok=True)
                    if ts_ == I // 512 - 1:
                        tok_done[b] = tl
                    dst = slice(512 * ts_, 512 * (ts_ + 1))
                    hi_ap = hi_dst[dst, 128 * och:128 * (och + 1)] \
                        .rearrange("(k q) c -> q k c", q=128)
                    lo_ap = lo_dst[dst, 128 * och:128 * (och + 1)] \
                        .rearrange("(k q) c -> q k c", q=128)
                    tok_stg[sb_] = p.dma("gpsimd", sb_, [
                        (hi_ap, stg_hi[sb_].rearrange("p (k c) -> p k c", c=128)),
                        (lo_ap, stg_lo[sb_].rearrange("p (k c) -> p k c", c=128))],
                        deps=[th, tl])

        if N_PHASES >= 1:
            w_prep(wup_d, wuphi_d, wuplo_d, I3, D)
            w_prep(wgate_d, wghi_d, wglo_d, I3, D)
            w_prep(wdown_d, wdhi_d, wdlo_d, D, I3)
            p.barrier()

        def allreduce_max2(v0_ap, v1_ap, tok_in):
            k = cc_ev[0]
            cc_ev[0] += 1
            tc0 = E("vector", "tensor_copy", ccio[0:1, 0:1], v0_ap,
                    deps=tok_in, tok=True)
            tc1 = E("vector", "tensor_copy", ccio[0:1, 1:2], v1_ap,
                    deps=tok_in, tok=True)
            td = E("gpsimd", "dma_start", out=ccin_d[k][:, :], in_=ccio[0:1, 0:2],
                   deps=[tc0, tc1], dma=True)
            tcc = E("gpsimd", "collective_compute", "AllReduce", ALU.max,
                    RG, [ccin_d[k][:, :]], [ccout_d[k][:, :]],
                    deps=[td], tok=True)
            tr = E("gpsimd", "dma_start", out=ccio[0:1, 2:4], in_=ccout_d[k][:, :],
                   deps=[tcc], dma=True)
            return tr

        def bcast_cols(src_row_ap, n, dst_col0, deps):
            tm = E("tensor", "matmul", psb[6][:, 0:n], onest[:], src_row_ap,
                   start=True, stop=True, deps=deps, tok=True)
            return E("scalar", "activation", colA[:, dst_col0:dst_col0 + n],
                     psb[6][:, 0:n], ACT.Identity, deps=[tm], tok=True)

        last_bcast = [None]

        def bcast_cols2(src_row_ap, n, dst_col0, deps):
            tm = E("tensor", "matmul", psb[6][:, 0:n], onest[:], src_row_ap,
                   start=True, stop=True, deps=list(deps) + [last_bcast[0]],
                   tok=True)
            tev = E("vector", "tensor_copy", colA[:, dst_col0:dst_col0 + n],
                    psb[6][:, 0:n], deps=[tm], tok=True)
            last_bcast[0] = tev
            return tev

        def allred2(tok_deps, ccio_deps):
            k = cc_ev[0]
            cc_ev[0] += 1
            td = p.dma("gpsimd", 6, [(ccin_d[k][:, :], ccio[0:1, 0:2])],
                       deps=list(tok_deps) + list(ccio_deps))
            tcc = E("gpsimd", "collective_compute", "AllReduce", ALU.max,
                    RG, [ccin_d[k][:, :]], [ccout_d[k][:, :]], deps=[td], tok=True)
            return p.dma("gpsimd", 6, [(ccio[0:1, 2:4], ccout_d[k][:, :])],
                         deps=[tcc])

        def ffn_phase():
            def xln_ap(c):
                if c < 12:
                    return arA[:, 1024 * c:1024 * (c + 1)]
                return arB[:, 1024 * (c - 12):1024 * (c - 11)]
            tok_xln = []
            tok_absm = []
            junk = nhi_f[:, 0:1024]
            for c in range(NTC):
                xc = xres[:, 1024 * c:1024 * (c + 1)]
                ts_ = E("vector", "reduce_sum", cA(48 + c), xc, axis=AX.X, tok=True)
                tn = E("scalar", "activation", cA(c), cA(48 + c), ACT.Identity,
                       scale=-1.0 / D, deps=[ts_], tok=True)
                tq = E("scalar", "activation", junk, xc, ACT.Square,
                       bias=cA(c), accum_out=cA(48 + c), deps=[tn], tok=True)
                tr0 = E("scalar", "activation", cA(48 + c), cA(48 + c), ACT.Sqrt,
                        scale=1.0 / D, bias=cA(79), deps=[tq], tok=True)
                tr = E("vector", "reciprocal", cA(16 + c), cA(48 + c),
                       deps=[tr0], tok=True)
                tx = E("vector", "tensor_scalar", xln_ap(c), xc, cA(c), cA(16 + c),
                       ALU.add, ALU.mult, deps=[tn, tr], tok=True)
                tok_xln.append(tx)
                ta = E("vector", "reduce_max", cA(32 + c), xln_ap(c), axis=AX.X,
                       apply_absolute_value=True, deps=[tx], tok=True)
                tok_absm.append(ta)
            rowb = nhi_f[0:1, 1024:3072]
            tdm = p.dma("gpsimd", 7, [(rowb, colA[:, 32:48])], deps=tok_absm)
            trm = E("vector", "reduce_max", ccio[0:1, 0:1], rowb, axis=AX.X,
                    deps=[tdm], tok=True)
            tz = E("vector", "memset", ccio[0:1, 1:2], 0.0, tok=True)
            tr1 = allred2([trm, tz], [])
            ta1 = E("vector", "reciprocal", srow(0), ccio[0:1, 2:3], deps=[tr1], tok=True)
            ta2 = E("vector", "tensor_scalar", srow(1), srow(0), 32767.0, None,
                    ALU.mult, deps=[ta1], tok=True)
            E("vector", "tensor_scalar", srow(2), ccio[0:1, 2:3],
              1.0 / 32767.0, None, ALU.mult, deps=[tr1], tok=True)
            tbc1 = bcast_cols2(srow(1), 1, 64, [ta2])
            if FFN_STAGES <= 1:
                p.barrier()
                return

            ni32 = arB_i32[:, 4096:5120]
            nf32 = arB[:, 5120:6144]
            tok_ev = {}
            tp_done = [None, None]
            nhi3 = nhi[:].rearrange("p (f t) -> p f t", t=2048)
            nlo3 = nlo[:].rearrange("p (f t) -> p f t", t=2048)
            for c in range(NTC):
                t1 = E("scalar", "activation", ni32, xln_ap(c), ACT.Identity,
                       scale=cA(64), deps=[tok_xln[c], tbc1, tok_ev.get(c - 1)],
                       tok=True)
                t2 = E("scalar", "activation", nf32, ni32, ACT.Identity,
                       deps=[t1], tok=True)
                for half in range(2):
                    bank = psb[4 + half]
                    ltp = None
                    for kk in range(4):
                        f = 4 * half + kk
                        ltp = E("tensor", "transpose", bank[:, 128 * kk:128 * (kk + 1)],
                                nf32[:, 128 * f:128 * (f + 1)], identt[:],
                                deps=[t2, tp_done[half]] if kk == 0 else [],
                                tok=(kk == 3))
                    dsth = nhi3[:, 4 * half:4 * half + 4, 128 * c:128 * (c + 1)]
                    dstl = nlo3[:, 4 * half:4 * half + 4, 128 * c:128 * (c + 1)]
                    th = E("vector", "tensor_copy",
                           dsth, bank[:].rearrange("p (k c) -> p k c", c=128),
                           deps=[ltp], tok=True)
                    tl = E("vector", "tensor_tensor",
                           dstl, bank[:].rearrange("p (k c) -> p k c", c=128),
                           dsth, ALU.subtract, deps=[ltp, th], tok=True)
                    tp_done[half] = tl
                    if half == 1:
                        tok_ev[c] = tl
            tok_n = [tp_done[0], tp_done[1], tok_ev[NTC - 1]]
            if FFN_STAGES <= 2:
                p.barrier()
                return

            tmu = E("vector", "memset", cA(71), 0.0, deps=tok_absm, tok=True)
            tmg = E("vector", "memset", cA(72), 0.0, tok=True)
            tok_wload = {}
            tok_mmdone = {}
            tok_evac = {}
            tok_stgout = [None, None]
            srcs = [(wuphi_d, 0), (wuplo_d, 1), (wghi_d, 2), (wglo_d, 3)]
            NOG = min(12, NOG_LIM)
            stg_ctr = [0]
            tcp = None
            for og in range(NOG):
                b = og % 2
                dep_reuse = [tok_mmdone.get(og - 2)] + (tok_n if og < 2 else [])
                wpairs = []
                for (src_, s_) in srcs:
                    dst = arA_bf[:, 8192 * b + 2048 * s_: 8192 * b + 2048 * (s_ + 1)] \
                        .rearrange("p (f c) -> p f c", c=256)
                    sap = src_[:, 256 * og:256 * (og + 1)] \
                        .rearrange("(f q) c -> q f c", q=128)
                    wpairs.append((dst, sap))
                tok_wload[og] = p.dma("sync", 2 + b, wpairs, deps=dep_reuse)
                last_mm = None
                for mat in range(2):
                    wh3 = arA_bf[:, 8192 * b + 2048 * (2 * mat):
                                 8192 * b + 2048 * (2 * mat + 1)] \
                        .rearrange("p (f c) -> p f c", c=256)
                    wl3 = arA_bf[:, 8192 * b + 2048 * (2 * mat + 1):
                                 8192 * b + 2048 * (2 * mat + 2)] \
                        .rearrange("p (f c) -> p f c", c=256)
                    for o2 in range(2):
                        och = 2 * og + o2
                        sp = stg_ctr[0] % 2
                        stg_ctr[0] += 1
                        stg = arA[:, 8192 + 2048 * sp:8192 + 2048 * (sp + 1)]
                        for tt in range(4):
                            bank_i = tt % 4
                            bank = psb[bank_i]
                            first = True
                            for f in range(NFC):
                                whi = wh3[:, f, 128 * o2:128 * (o2 + 1)]
                                wlo = wl3[:, f, 128 * o2:128 * (o2 + 1)]
                                rhs_h = nhi3[:, f, 512 * tt:512 * (tt + 1)]
                                rhs_l = nlo3[:, f, 512 * tt:512 * (tt + 1)]
                                dd = ([tok_wload[og],
                                       tok_evac.get(bank_i)] + tok_n) \
                                    if first else []
                                E("tensor", "matmul", bank[:], whi, rhs_h,
                                  start=first, stop=False, deps=dd)
                                E("tensor", "matmul", bank[:], whi, rhs_l,
                                  start=False, stop=False)
                                last_mm = E("tensor", "matmul", bank[:], wlo, rhs_h,
                                            start=False, stop=(f == NFC - 1),
                                            tok=(f == NFC - 1))
                                first = False
                            tr_ = E("vector", "reduce_max", cA(73), bank[:],
                                    axis=AX.X, apply_absolute_value=True,
                                    deps=[last_mm], tok=True)
                            E("vector", "tensor_tensor", cA(71 + mat),
                              cA(71 + mat), cA(73), ALU.max,
                              deps=[tr_, tmu, tmg], tok=True)
                            tcp = E("vector", "tensor_copy",
                                    stg[:, 512 * tt:512 * (tt + 1)], bank[:],
                                    deps=[last_mm, tok_stgout[sp]], tok=True)
                            tok_evac[bank_i] = tcp
                        dst_d = (pu_d if mat == 0 else pg_d)
                        tok_stgout[sp] = p.dma(
                            "gpsimd", sp,
                            [(dst_d[128 * och:128 * (och + 1), :], stg)],
                            deps=[tcp])
                tok_mmdone[og] = last_mm
            tok_ug_end = [tok_mmdone[NOG - 1], tok_evac.get(0), tok_evac.get(1),
                          tok_evac.get(2), tok_evac.get(3), tok_stgout[0],
                          tok_stgout[1]]
            if FFN_STAGES <= 2.5:
                p.barrier()
                return
            t7a = p.dma("gpsimd", 7, [(nhi_f[0:1, 1024:1152], cA(71)),
                                      (nhi_f[0:1, 1152:1280], cA(72))],
                        deps=tok_ug_end)
            t7b = t7a
            tru = E("vector", "reduce_max", ccio[0:1, 0:1], nhi_f[0:1, 1024:1152],
                    axis=AX.X, deps=[t7a, tr1], tok=True)
            trg = E("vector", "reduce_max", ccio[0:1, 1:2], nhi_f[0:1, 1152:1280],
                    axis=AX.X, deps=[t7b], tok=True)
            tr2 = allred2([tru, trg], [])
            tb1 = E("vector", "reciprocal", srow(0), ccio[0:1, 2:3], deps=[tr2], tok=True)
            tb2 = E("vector", "tensor_scalar", srow(3), srow(0), 32767.0, None,
                    ALU.mult, deps=[tb1], tok=True)
            tb3 = E("vector", "tensor_scalar", srow(8), ccio[0:1, 2:3], 1.0 / 32767.0,
                    None, ALU.mult, deps=[tr2], tok=True)
            tb4 = E("vector", "tensor_tensor", srow(4), srow(8), srow(2), ALU.mult,
                    deps=[tb3], tok=True)
            tb5 = E("vector", "reciprocal", srow(0), ccio[0:1, 3:4], deps=[tr2, tb2],
                    tok=True)
            tb6 = E("vector", "tensor_scalar", srow(5), srow(0), 32767.0, None,
                    ALU.mult, deps=[tb5], tok=True)
            tb7 = E("vector", "tensor_scalar", srow(8), ccio[0:1, 3:4], 1.0 / 32767.0,
                    None, ALU.mult, deps=[tb5, tb4], tok=True)
            tb8 = E("vector", "tensor_tensor", srow(6), srow(8), srow(2), ALU.mult,
                    deps=[tb7], tok=True)
            tb9 = E("vector", "tensor_copy", srow(9), srow(3), deps=[tb2], tok=True)
            tba = E("vector", "tensor_copy", srow(10), srow(5), deps=[tb6], tok=True)
            tbb = E("vector", "tensor_copy", srow(11), srow(6), deps=[tb8], tok=True)
            tbc2 = bcast_cols2(srow(9, 3), 3, 65, [tb9, tba, tbb])
            if FFN_STAGES <= 3:
                p.barrier()
                return

            tmm = E("vector", "memset", cA(74), 0.0, tok=True)
            tok_use = {}
            tok_m1w = {}
            st_e2 = {}
            def e2_head(i):
                b = i % 4
                pgin = arA[:, 2048 * b:2048 * (b + 1)]
                ngi = arB_i32[:, 2048 * (i % 2):2048 * (i % 2 + 1)]
                sigb = arA[:, 8192 + 2048 * (i % 2):8192 + 2048 * (i % 2 + 1)]
                d0 = (tok_ug_end if i < 2 else [])
                if i == 0:
                    d0 = d0 + p.qtot("gpsimd")
                trd_ = p.dma("sync", i % 4, [(pgin, pg_d[128 * i:128 * (i + 1), :])],
                             deps=d0)
                tg = E("vector", "tensor_scalar", ngi, pgin, cA(66), None,
                       ALU.mult, deps=[trd_, tbc2], tok=True)
                tsg = E("scalar", "activation", sigb, ngi, ACT.Sigmoid,
                        scale=cA(67), deps=[tg], tok=True)
                st_e2[i] = (pgin, ngi, sigb)
            def e2_tail(i):
                pgin, ngi, sigb = st_e2[i]
                m1o = pgin
                ri32 = sigb.bitcast(I32)
                tri = E("vector", "tensor_scalar", ri32, sigb, 65536.0, None,
                        ALU.mult, tok=True)
                tm1 = E("vector", "scalar_tensor_tensor", m1o, ri32, 65535.0,
                        ngi, ALU.min, ALU.mult,
                        deps=[tri, tok_m1w.get(i - 2)], tok=True)
                trd2 = E("vector", "reduce_max", cA(73), m1o, axis=AX.X,
                         apply_absolute_value=True, deps=[tm1], tok=True)
                tok_use[i] = E("vector", "tensor_tensor", cA(74), cA(74),
                               cA(73), ALU.max, deps=[trd2, tmm], tok=True)
                tok_m1w[i] = p.dma("gpsimd", i % 4,
                                   [(m1_d[128 * i:128 * (i + 1), :], m1o)],
                                   deps=[tm1])
            e2_head(0)
            for i in range(1, NIC):
                e2_head(i)
                e2_tail(i - 1)
            e2_tail(NIC - 1)
            t8a = p.dma("gpsimd", 7, [(nhi_f[0:1, 1024:1152], cA(74))],
                        deps=[tok_use[NIC - 1]])
            trm1 = E("vector", "reduce_max", ccio[0:1, 0:1], nhi_f[0:1, 1024:1152],
                     axis=AX.X, deps=[t8a, tr2], tok=True)
            tz3 = E("vector", "memset", ccio[0:1, 1:2], 0.0, deps=[tr2], tok=True)
            tr3 = allred2([trm1, tz3], [])
            tc1 = E("vector", "reciprocal", srow(0), ccio[0:1, 2:3], deps=[tr3], tok=True)
            tc2 = E("vector", "tensor_scalar", srow(12), srow(0), 32767.0, None,
                    ALU.mult, deps=[tc1], tok=True)
            tc3 = E("vector", "tensor_tensor", srow(8), ccio[0:1, 2:3], srow(6),
                    ALU.mult, deps=[tr3, tb8], tok=True)
            tc4 = E("vector", "tensor_scalar", srow(8), srow(8),
                    1.0 / (65536.0 * 32767.0), None, ALU.mult, deps=[tc3], tok=True)
            tc5 = E("vector", "tensor_tensor", srow(13), srow(8), srow(4), ALU.mult,
                    deps=[tc4, tb4], tok=True)
            tbc3 = bcast_cols2(srow(12), 1, 68, [tc2])
            if FFN_STAGES <= 4:
                p.barrier()
                return

            tmo = E("vector", "memset", cA(75), 0.0, deps=[tok_use[NIC - 1]],
                    tok=True)
            tok_use2 = {}
            tok_opw = {}
            st_e3 = {}
            def e3_head(i):
                b = i % 3
                puin = arA[:, 2048 * b:2048 * (b + 1)]
                m1in = arA[:, 6144 + 2048 * b:6144 + 2048 * (b + 1)]
                d0 = [tok_use[NIC - 1]]
                if i == 0:
                    d0 += p.qtot("gpsimd")
                trdu = p.dma("sync", i % 2, [(puin,
                             pu_d[128 * i:128 * (i + 1), :])], deps=d0)
                trdm = p.dma("sync", 2 + i % 2, [(m1in,
                             m1_d[128 * i:128 * (i + 1), :])], deps=d0)
                nupi = arB_i32[:, 2048 * (i % 2):2048 * (i % 2 + 1)]
                ng2i = m1in.bitcast(I32)
                tnu = E("vector", "tensor_scalar", nupi, puin, cA(65), None,
                        ALU.mult, deps=[trdu, tbc2], tok=True)
                tng = E("scalar", "activation", ng2i, m1in, ACT.Identity,
                        scale=cA(68), deps=[trdm, tbc3], tok=True)
                st_e3[i] = (puin, m1in, nupi, ng2i, tnu, tng)
            def e3_tail(i):
                puin, m1in, nupi, ng2i, tnu, tng = st_e3[i]
                opo = puin
                top = E("vector", "tensor_tensor", opo, ng2i, nupi, ALU.mult,
                        deps=[tnu, tng, tok_opw.get(i - 2)], tok=True)
                trd2 = E("vector", "reduce_max", cA(73), opo, axis=AX.X,
                         apply_absolute_value=True, deps=[top], tok=True)
                tok_use2[i] = E("vector", "tensor_tensor", cA(75), cA(75),
                                cA(73), ALU.max, deps=[trd2, tmo], tok=True)
                tok_opw[i] = p.dma("gpsimd", i % 4,
                                   [(op_d[128 * i:128 * (i + 1), :], opo)],
                                   deps=[top])
            e3_head(0)
            for i in range(1, NIC):
                e3_head(i)
                e3_tail(i - 1)
            e3_tail(NIC - 1)
            t9a = p.dma("gpsimd", 7, [(nhi_f[0:1, 1024:1152], cA(75))],
                        deps=[tok_use2[NIC - 1]])
            trop = E("vector", "reduce_max", ccio[0:1, 0:1], nhi_f[0:1, 1024:1152],
                     axis=AX.X, deps=[t9a, tr3], tok=True)
            tz4 = E("vector", "memset", ccio[0:1, 1:2], 0.0, deps=[tr3], tok=True)
            tr4 = allred2([trop, tz4], [])
            td1 = E("vector", "reciprocal", srow(0), ccio[0:1, 2:3], deps=[tr4], tok=True)
            td2 = E("vector", "tensor_scalar", srow(16), srow(0), 32767.0, None,
                    ALU.mult, deps=[td1], tok=True)
            td3 = E("vector", "tensor_scalar", srow(8), ccio[0:1, 2:3], 0.5 / 32767.0,
                    None, ALU.mult, deps=[tr4], tok=True)
            td4 = E("vector", "tensor_tensor", srow(17), srow(8), srow(13), ALU.mult,
                    deps=[td3, tc5, td2], tok=True)
            tbc4 = bcast_cols2(srow(16, 2), 2, 69, [td4])
            if FFN_STAGES <= 5:
                p.barrier()
                return

            tok_use3 = {}
            tok_w3 = {}
            tok_w3b = {}
            st_e4 = {}
            def e4_head(i):
                b = i % 4
                opin = arA[:, 2048 * b:2048 * (b + 1)]
                noi = arB_i32[:, 2048 * (i % 2):2048 * (i % 2 + 1)]
                hib = arA_bf[:, 16384 + 2048 * (i % 2):16384 + 2048 * (i % 2 + 1)]
                d0 = [tok_use2[NIC - 1]]
                if i == 0:
                    d0 += p.qtot("gpsimd")
                trdo = p.dma("sync", i % 4, [(opin,
                             op_d[128 * i:128 * (i + 1), :])], deps=d0)
                tno = E("scalar", "activation", noi, opin, ACT.Identity,
                        scale=cA(69), deps=[trdo, tbc4], tok=True)
                thi = E("scalar", "activation", hib, noi, ACT.Identity,
                        deps=[tno, tok_w3.get(i - 2)], tok=True)
                st_e4[i] = (noi, hib, thi)
            def e4_tail(i):
                b = i % 2
                noi, hib, thi = st_e4[i]
                lob = arA_bf[:, 20480 + 2048 * b:20480 + 2048 * (b + 1)]
                tlo = E("vector", "tensor_tensor", lob, noi, hib, ALU.subtract,
                        deps=[thi, tok_w3b.get(i - 2)], tok=True)
                tok_use3[i] = tlo
                tok_w3[i] = p.dma("gpsimd", i % 4,
                                  [(nohi_d[128 * i:128 * (i + 1), :], hib),
                                   (nolo_d[128 * i:128 * (i + 1), :], lob)],
                                  deps=[thi, tlo])
                tok_w3b[i] = tok_w3[i]
            e4_head(0)
            for i in range(1, NIC):
                e4_head(i)
                e4_tail(i - 1)
            e4_tail(NIC - 1)
            tok_ld = {}
            tok_mmd = {}
            ev_last = {}
            for tg_ in range(4):
                for ich in range(NIC):
                    b = ich % 2
                    nhib = arA_bf[:, 512 * b:512 * (b + 1)]
                    nlob = arA_bf[:, 1024 + 512 * b:1024 + 512 * (b + 1)]
                    wdh = arA_bf[:, 2048 + 1024 * b:2048 + 1024 * (b + 1)]
                    wdl = arA_bf[:, 4096 + 1024 * b:4096 + 1024 * (b + 1)]
                    dep0 = [tok_mmd.get((tg_, ich - 2)),
                            tok_mmd.get((tg_ - 1, NIC - 1))]
                    if tg_ == 0 and ich < 2:
                        dep0 += [tok_w3.get(NIC - 1), tok_w3b.get(NIC - 1),
                                 tok_use3.get(NIC - 1)] + p.qtot("gpsimd")
                    t1 = p.dma("sync", b, [
                        (nhib, nohi_d[128 * ich:128 * (ich + 1),
                                      512 * tg_:512 * (tg_ + 1)]),
                        (nlob, nolo_d[128 * ich:128 * (ich + 1),
                                      512 * tg_:512 * (tg_ + 1)]),
                        (wdh, wdhi_d[128 * ich:128 * (ich + 1), :]),
                        (wdl, wdlo_d[128 * ich:128 * (ich + 1), :])],
                        deps=dep0)
                    t2 = t3 = t4 = t1
                    lm = None
                    for tc4_ in range(4):
                        for ht in range(2):
                            bank = psb[2 * tc4_ + ht]
                            first = (ich == 0)
                            last = (ich == NIC - 1)
                            dd = [t1, t2, t3, t4]
                            if first:
                                dd += [ev_last.get(tg_ - 1), last_bcast[0]]
                            lh = nhib[:, 128 * tc4_:128 * (tc4_ + 1)]
                            ll = nlob[:, 128 * tc4_:128 * (tc4_ + 1)]
                            rh = wdh[:, 512 * ht:512 * (ht + 1)]
                            rl = wdl[:, 512 * ht:512 * (ht + 1)]
                            E("tensor", "matmul", bank[:], lh, rh,
                              start=first, stop=False, deps=dd)
                            E("tensor", "matmul", bank[:], lh, rl,
                              start=False, stop=False)
                            lm = E("tensor", "matmul", bank[:], ll, rh,
                                   start=False, stop=last,
                                   tok=(ht == 1 and tc4_ == 3))
                    tok_mmd[(tg_, ich)] = lm
                for tc4_ in range(4):
                    c = 4 * tg_ + tc4_
                    for ht in range(2):
                        bank = psb[2 * tc4_ + ht]
                        ev_last[tg_] = E(
                            "vector", "scalar_tensor_tensor",
                            xres[:, 1024 * c + 512 * ht:1024 * c + 512 * (ht + 1)],
                            bank[:], cA(70),
                            xres[:, 1024 * c + 512 * ht:1024 * c + 512 * (ht + 1)],
                            ALU.mult, ALU.add,
                            deps=[tok_mmd[(tg_, NIC - 1)], tbc4], tok=True)
            p.barrier()


        def attn_phase():
            defq = []

            def drain(n):
                for _ in range(min(n, len(defq))):
                    defq.pop(0)()

            masks = arA[:, 0:2048]
            gamma = arA[:, 3072:4096]
            nhi_b = nhi[:]
            nlo_b = nlo[:]
            qh_b = nhi_b[:, 0:4096]
            ql_b = nhi_b[:, 4096:8192]
            kh_b = nhi_b[:, 8192:12288]
            kl_b = nhi_b[:, 12288:16384]
            vtmh = nlo_b[:, 0:4112]
            vtml = nlo_b[:, 4112:8224]
            vfm = nlo_f[:, 4112:6160]
            cth_b = [nlo_b[:, 12320:12832], nlo_b[:, 13344:13856]]
            ctl_b = [nlo_b[:, 12832:13344], nlo_b[:, 13856:14368]]
            htm = arA[:, 5120:5376]
            htm4 = [nlo_f[:, 4112 + 256 * i:4112 + 256 * (i + 1)] for i in range(4)]
            junk = arA[:, 12032:12288]
            mm_slots = [arA[:, 4096:4608], arA[:, 4608:5120],
                        arA[:, 6144:6656], arA[:, 6656:7168]]
            dsb_s = [arA[:, 2048:2560], arA[:, 2560:3072]]

            def xln_ap(c):
                if c < 12:
                    return arA[:, 1024 * c:1024 * (c + 1)]
                return arB[:, 1024 * (c - 12):1024 * (c - 11)]
            tok_xln = []
            junk0 = nhi_f[:, 0:1024]
            for c in range(NTC):
                xc = xres[:, 1024 * c:1024 * (c + 1)]
                ts_ = E("vector", "reduce_sum", cA(48 + c), xc, axis=AX.X, tok=True)
                tn = E("scalar", "activation", cA(c), cA(48 + c), ACT.Identity,
                       scale=-1.0 / D, deps=[ts_], tok=True)
                tq = E("scalar", "activation", junk0, xc, ACT.Square,
                       bias=cA(c), accum_out=cA(48 + c), deps=[tn], tok=True)
                tr0 = E("scalar", "activation", cA(48 + c), cA(48 + c), ACT.Sqrt,
                        scale=1.0 / D, bias=cA(79), deps=[tq], tok=True)
                tr = E("vector", "reciprocal", cA(16 + c), cA(48 + c),
                       deps=[tr0], tok=True)
                tx = E("vector", "tensor_scalar", xln_ap(c), xc, cA(c), cA(16 + c),
                       ALU.add, ALU.mult, deps=[tn, tr], tok=True)
                tok_xln.append(tx)

            stgA = [arB[:, 4096:4608], arB[:, 4608:5120]]
            for c in range(NTC):
                for half in range(2):
                    bank = psb[4 + half]
                    ltp = None
                    for kk in range(4):
                        f = 4 * half + kk
                        ltp = E("tensor", "transpose",
                                bank[:, 128 * kk:128 * (kk + 1)],
                                xln_ap(c)[:, 128 * f:128 * (f + 1)], identt[:],
                                deps=[tok_xln[c]] if kk == 0 else [], tok=(kk == 3))
                    tev = E("vector", "tensor_copy", stgA[half], bank[:],
                            deps=[ltp], tok=True)
                    dst = xfm_d[512 * half:512 * (half + 1),
                                128 * c:128 * (c + 1)] \
                        .rearrange("(k q) c -> q k c", q=128)
                    p.dma("gpsimd", half, [
                        (dst, stgA[half].rearrange("p (k c) -> p k c", c=128))],
                        deps=[tev])

            wstg = [gcols[:, 128 + 32 * kk:160 + 32 * kk] for kk in range(2)]
            tmp32 = arB[:, 5120:5152]
            tmp32b = arB[:, 5152:5184]
            for kk, wsrc in enumerate((wit_d, wft_d)):
                t1 = p.dma("sync", 4, [
                    (wstg[kk], wsrc[0:1024, :].rearrange("(f q) c -> q f c", q=128)),
                    (tmp32, wsrc[1024:2048, :].rearrange("(f q) c -> q f c", q=128)),
                    (tmp32b, wsrc[2048:3072, :].rearrange("(f q) c -> q f c", q=128))])
                E("vector", "tensor_tensor", wstg[kk], wstg[kk], tmp32, ALU.add,
                  deps=[t1], tok=True)
                E("vector", "tensor_tensor", wstg[kk], wstg[kk], tmp32b, ALU.add,
                  tok=True)
            tbias = p.dma("sync", 5, [(colA[0:4, 76:77], bi_d[:, :]),
                                      (colA[0:4, 77:78], bf_d[:, :]),
                                      (scrow[0:1, 32:33], lnw_d[:, 0:1])])
            trow = p.dma("sync", 5, [(nhi_f[0:1, 1024:2048], lnw_d[:, :])])
            ig_sb = arB[0:4, 0:2048]
            fg_sb = arB[0:4, 2048:4096]
            xchunk = [arA[:, 7168:9216], arA[:, 9216:11264]]
            gmm = {}
            for f in range(NFC):
                tl = p.dma("sync", 6, [(xchunk[f % 2],
                                        xfm_d[128 * f:128 * (f + 1), :])])
                for kk in range(2):
                    w3 = wstg[kk].rearrange("p (f c) -> p f c", c=4)
                    for tt in range(4):
                        gmm[(kk, tt)] = E(
                            "tensor", "matmul", psb[4 * kk + tt][0:4, 0:512],
                            w3[:, f, :], xchunk[f % 2][:, 512 * tt:512 * (tt + 1)],
                            start=(f == 0), stop=(f == NFC - 1),
                            deps=[tl], tok=(f == NFC - 1))
            for kk in range(2):
                dstsb = (ig_sb, fg_sb)[kk]
                for tt in range(4):
                    E("vector", "tensor_copy",
                      dstsb[:, 512 * tt:512 * (tt + 1)],
                      psb[4 * kk + tt][0:4, 0:512],
                      deps=[gmm[(kk, tt)]], tok=True)
            E("vector", "tensor_scalar", ig_sb, ig_sb, colA[0:4, 76:77], None,
              ALU.add, deps=[tbias], tok=True)
            E("vector", "tensor_scalar", fg_sb, fg_sb, colA[0:4, 77:78], None,
              ALU.add, deps=[tbias], tok=True)

            scr = arB[0:4, 4096:6144]
            E("scalar", "activation", scr, fg_sb, ACT.Abs, tok=True)
            E("scalar", "activation", scr, scr, ACT.Exp, scale=-1.0, tok=True)
            E("scalar", "activation", scr, scr, ACT.Ln, bias=colA[0:4, 78:79],
              tok=True)
            E("vector", "tensor_scalar", fg_sb, fg_sb, 0.0, None, ALU.min,
              tok=True)
            E("vector", "tensor_tensor", fg_sb, fg_sb, scr, ALU.subtract,
              tok=True)
            E("vector", "tensor_tensor_scan", scr, fg_sb, fg_sb, 0.0,
              ALU.add, ALU.bypass, tok=True)
            E("vector", "tensor_tensor", ig_sb, ig_sb, scr, ALU.subtract,
              tok=True)
            E("vector", "tensor_tensor_scan", fg_sb, ig_sb, ig_sb, -3e38,
              ALU.max, ALU.bypass, tok=True)
            E("vector", "tensor_tensor", scr, scr, fg_sb, ALU.add, tok=True)
            E("scalar", "activation", scr, scr, ACT.Exp, scale=-1.0, tok=True)
            b_sb, M_sb, em_sb = ig_sb, fg_sb, scr

            for kk, (src_, dst_) in enumerate(((b_sb, gcols[:, 0:64]),
                                               (em_sb, gcols[:, 64:128]))):
                for grp in range(4):
                    bank = psb[1]
                    for s4 in range(4):
                        s16 = 4 * grp + s4
                        E("tensor", "transpose", bank[:, 4 * s4:4 * s4 + 4],
                          src_[:, 128 * s16:128 * (s16 + 1)],
                          identt[0:4, 0:4], tok=True)
                    E("vector", "tensor_copy",
                      dst_[:, 16 * grp:16 * (grp + 1)], bank[:, 0:16],
                      tok=True)

            tpl = E("vector", "tensor_scalar", nhi_f[0:1, 1024:2048],
                    nhi_f[0:1, 1024:2048], 1.0, None, ALU.add, deps=[trow],
                    tok=True)
            for half in range(2):
                tm = E("tensor", "matmul", psb[0][:, 0:512], onest[:],
                       nhi_f[0:1, 1024 + 512 * half:1024 + 512 * (half + 1)],
                       start=True, stop=True, deps=[tpl], tok=True)
                E("vector", "tensor_copy", gamma[:, 512 * half:512 * (half + 1)],
                  psb[0][:], deps=[tm], tok=True)
            p.dma("sync", 7, [(masks, maskd_d[:, :])])

            for h in range(NH):
                bd = arA[:, 11264:12032]
                bdq = bd[:, 0:256]
                bdk = bd[:, 256:512]
                bdv = bd[:, 512:768]
                xfm = arA[:, 7168:11264]
                p.dma("sync", 0, [
                    (bdq.rearrange("p (f c) -> p f c", c=128),
                     bdq_d[2 * h:2 * h + 2, :, :].rearrange("f q c -> q f c")),
                    (bdk.rearrange("p (f c) -> p f c", c=128),
                     bdk_d[2 * h:2 * h + 2, :, :].rearrange("f q c -> q f c")),
                    (bdv.rearrange("p (f c) -> p f c", c=128),
                     bdv_d[2 * h:2 * h + 2, :, :].rearrange("f q c -> q f c"))])
                p.dma("sync", 1, [
                    (xfm.rearrange("p (f c) -> p f c", c=2048),
                     xfm_d[256 * h:256 * (h + 1), :]
                     .rearrange("(f q) c -> q f c", q=128))])
                qkv_ctr = [0]
                for proj, bdw in enumerate((bdq, bdk, bdv)):
                    if proj == 2:
                        drain(len(defq))
                    for fc in range(2):
                        for tt in range(4):
                            qb = psb[0]
                            qkv_ctr[0] += 1
                            tm = E("tensor", "matmul", qb[:],
                                   bdw[:, 128 * fc:128 * (fc + 1)],
                                   xfm[:, 2048 * fc + 512 * tt:
                                       2048 * fc + 512 * (tt + 1)],
                                   start=True, stop=True, tok=True)
                            lo_col = 2048 * fc + 512 * tt
                            if proj == 2:
                                E("vector", "tensor_copy",
                                  vfm[:, 512 * tt:512 * (tt + 1)], psb[0][:],
                                  deps=[tm], tok=True)
                            elif proj == 0:
                                dh_ = qh_b[:, lo_col:lo_col + 512]
                                dl_ = ql_b[:, lo_col:lo_col + 512]
                                E("vector", "tensor_copy", dh_, psb[0][:],
                                  deps=[tm], tok=True)
                                E("vector", "tensor_tensor", dl_, psb[0][:],
                                  dh_, ALU.subtract, tok=True)
                            else:
                                E("vector", "tensor_scalar",
                                  arA[:, 5632:6144], psb[0][:],
                                  0.0625, None, ALU.mult, deps=[tm], tok=True)
                                dh_ = kh_b[:, lo_col:lo_col + 512]
                                dl_ = kl_b[:, lo_col:lo_col + 512]
                                E("scalar", "activation", dh_, arA[:, 5632:6144],
                                  ACT.Identity, tok=True)
                                E("vector", "tensor_tensor", dl_,
                                  arA[:, 5632:6144], dh_, ALU.subtract,
                                  tok=True)
                        if proj == 2:
                            for grp in range(4):
                                bank = psb[1]
                                ltp = None
                                for kk in range(4):
                                    jt = 4 * grp + kk
                                    ltp = E("tensor", "transpose",
                                            bank[:, 128 * kk:128 * (kk + 1)],
                                            vfm[:, 128 * jt:128 * (jt + 1)],
                                            identt[:], tok=(kk == 3))
                                dsth = vtmh.rearrange("p (j c) -> p j c", c=257) \
                                    [:, 4 * grp:4 * grp + 4, 128 * fc:128 * (fc + 1)]
                                dstl = vtml.rearrange("p (j c) -> p j c", c=257) \
                                    [:, 4 * grp:4 * grp + 4, 128 * fc:128 * (fc + 1)]
                                E("vector", "tensor_copy", dsth,
                                  bank[:].rearrange("p (k c) -> p k c", c=128),
                                  deps=[ltp], tok=True)
                                E("vector", "tensor_tensor", dstl,
                                  bank[:].rearrange("p (k c) -> p k c", c=128),
                                  dsth, ALU.subtract, tok=True)
                E("vector", "memset",
                  vtmh.rearrange("p (j c) -> p j c", c=257)[:, :, 256:257], 1.0,
                  tok=True)
                E("vector", "memset",
                  vtml.rearrange("p (j c) -> p j c", c=257)[:, :, 256:257], 0.0,
                  tok=True)
                qh3 = qh_b.rearrange("p (f t) -> p f t", t=2048)
                ql3 = ql_b.rearrange("p (f t) -> p f t", t=2048)
                kh3 = kh_b.rearrange("p (f t) -> p f t", t=2048)
                kl3 = kl_b.rearrange("p (f t) -> p f t", t=2048)

                def emit_ST(it, jc):
                    stb = psb[6 + (jc % 2)]
                    first = True
                    lmm = None
                    for fc in range(2):
                        lh = kh3[:, fc, 128 * jc:128 * (jc + 1)]
                        ll = kl3[:, fc, 128 * jc:128 * (jc + 1)]
                        rh = qh3[:, fc, 512 * it:512 * (it + 1)]
                        rl = ql3[:, fc, 512 * it:512 * (it + 1)]
                        E("tensor", "matmul", stb[:], lh, rh,
                          start=first, stop=False)
                        E("tensor", "matmul", stb[:], lh, rl,
                          start=False, stop=False)
                        lmm = E("tensor", "matmul", stb[:], ll, rh,
                                start=False, stop=(fc == 1), tok=(fc == 1))
                        first = False
                    return stb, lmm

                def emit_D_CT(it, jc, stb, lmm):
                    dk = jc - 4 * it
                    esrc = mm_slots[dk] if dk >= 0 else arA[:, 5632:6144]
                    Dsb = dsb_s[jc % 2]
                    E("scalar", "activation", Dsb, esrc, ACT.Exp,
                      scale=-1.0, bias=gcols[:, 4 * jc + h:4 * jc + h + 1],
                      tok=True)
                    E("vector", "tensor_tensor", Dsb, stb[:], Dsb,
                      ALU.mult, deps=[lmm], tok=True)
                    cth = cth_b[jc % 2]
                    ctl = ctl_b[jc % 2]
                    E("gpsimd", "tensor_copy", cth, Dsb, tok=True)
                    E("vector", "tensor_tensor", ctl, Dsb, cth,
                      ALU.subtract, tok=True)
                    return cth, ctl

                def emit_CV(it, jc, cth, ctl):
                    vh = vtmh.rearrange("p (j c) -> p j c", c=257)[:, jc, :]
                    vl = vtml.rearrange("p (j c) -> p j c", c=257)[:, jc, :]
                    for i1 in range(4):
                        if jc > 4 * it + i1:
                            continue
                        cvb = psb[2 + i1]
                        first = (jc == 0)
                        last = (jc == 4 * it + i1)
                        E("tensor", "matmul", cvb[:, 0:257],
                          cth[:, 128 * i1:128 * (i1 + 1)], vh,
                          start=first, stop=False)
                        E("tensor", "matmul", cvb[:, 0:257],
                          cth[:, 128 * i1:128 * (i1 + 1)], vl,
                          start=False, stop=False)
                        E("tensor", "matmul", cvb[:, 0:257],
                          ctl[:, 128 * i1:128 * (i1 + 1)], vh,
                          start=False, stop=last, tok=True)

                def emit_phase1(it):
                    for i1 in range(4):
                        cvb = psb[2 + i1]
                        i16 = 4 * it + i1
                        c0, c1 = 60 + 2 * i1, 61 + 2 * i1
                        E("vector", "reduce_max", cA(c0),
                          cvb[:, 256:257], axis=AX.X,
                          apply_absolute_value=True, tok=True)
                        E("vector", "tensor_tensor", cA(c0), cA(c0),
                          gcols[:, 64 + 4 * i16 + h:64 + 4 * i16 + h + 1],
                          ALU.max, tok=True)
                        E("vector", "tensor_scalar", cA(c0), cA(c0), 1e-8,
                          None, ALU.add, tok=True)
                        E("vector", "reciprocal", cA(c1), cA(c0), tok=True)
                        E("vector", "tensor_scalar", htm4[i1], cvb[:, 0:256],
                          cA(c1), None, ALU.mult, tok=True)

                def defer_mhln(it):
                    for i1 in range(4):
                        c0, c1 = 60 + 2 * i1, 61 + 2 * i1
                        hh = htm4[i1]
                        defq.extend([
                            (lambda hh=hh, c0=c0: E("vector", "reduce_sum",
                             cA(c0), hh, axis=AX.X, tok=True)),
                            (lambda c0=c0: E("vector", "tensor_scalar",
                             cA(c0), cA(c0), -1.0 / DH, None, ALU.mult,
                             tok=True)),
                            (lambda hh=hh, c0=c0: E("vector", "tensor_scalar",
                             hh, hh, cA(c0), None, ALU.add, tok=True)),
                            (lambda hh=hh: E("vector", "tensor_tensor",
                             junk, hh, hh, ALU.mult, tok=True)),
                            (lambda c1=c1: E("vector", "reduce_sum", cA(c1),
                             junk, axis=AX.X, tok=True)),
                        ])
                    def sqrt4():
                        for i1 in range(4):
                            E("scalar", "activation", cA(61 + 2 * i1),
                              cA(61 + 2 * i1), ACT.Sqrt, scale=1.0 / DH,
                              bias=cA(79), tok=True)
                    defq.append(sqrt4)
                    for i1 in range(4):
                        i16 = 4 * it + i1
                        c1 = 61 + 2 * i1
                        hh = htm4[i1]
                        xsl = xres[:, 1024 * i16 + 256 * h:
                                   1024 * i16 + 256 * (h + 1)]
                        defq.extend([
                            (lambda c1=c1: E("vector", "reciprocal", cA(c1),
                             cA(c1), tok=True)),
                            (lambda hh=hh, c1=c1: E("vector",
                             "scalar_tensor_tensor", hh, hh, cA(c1),
                             gamma[:, 256 * h:256 * (h + 1)], ALU.mult,
                             ALU.mult, tok=True)),
                            (lambda hh=hh, xsl=xsl: E("vector",
                             "tensor_tensor", xsl, xsl, hh, ALU.add,
                             tok=True)),
                        ])

                def defer_masks(it):
                    for dk in range(4):
                        defq.append(lambda dk=dk: E(
                            "vector", "tensor_tensor", mm_slots[dk],
                            arA[:, 5632:6144],
                            masks[:, 512 * dk:512 * (dk + 1)], ALU.add,
                            tok=True))

                for it in range(4):
                    tm = E("tensor", "matmul", psb[0][:],
                           sel4t[:, 128 * h:128 * (h + 1)],
                           M_sb[:, 512 * it:512 * (it + 1)],
                           start=True, stop=True, tok=True)
                    E("vector", "tensor_copy", arA[:, 5632:6144], psb[0][:],
                      deps=[tm], tok=True)
                    if it == 0:
                        drain(len(defq))
                        defer_masks(it)
                        drain(4)
                    else:
                        drain(len(defq))
                        emit_phase1(it - 1)
                        defer_masks(it)
                        defer_mhln(it - 1)
                    njc = 4 * it + 4
                    stA = emit_ST(it, 0)
                    pend = (0, stA)
                    for jc in range(1, njc + 1):
                        pj, (stb, lmm) = pend
                        cth, ctl = emit_D_CT(it, pj, stb, lmm)
                        drain(3)
                        if jc <= njc - 1:
                            pend = (jc, emit_ST(it, jc))
                        emit_CV(it, pj, cth, ctl)
                    if it == 3:
                        drain(len(defq))
                        emit_phase1(3)
                        defer_mhln(3)
                        if h == NH - 1:
                            drain(len(defq))
            p.barrier()

        if N_PHASES >= 2:
            ffn_phase()
        if N_PHASES >= 3:
            attn_phase()
        if N_PHASES >= 4:
            ffn_phase()

        p.barrier()
        p.dma("gpsimd", 0, [(out_d[128 * c:128 * (c + 1), :],
                             xres[:, 1024 * c:1024 * (c + 1)])
                            for c in range(NTC)])
        E("gpsimd", "nop", deps=p.qtot("gpsimd"))

        p.finalize()
        block = es.enter_context(nc.Block())

        @block.sync
        def _(e):
            for fn in p.ops["sync"]:
                fn(e)

        @block.vector
        def _(e):
            for fn in p.ops["vector"]:
                fn(e)

        @block.scalar
        def _(e):
            for fn in p.ops["scalar"]:
                fn(e)

        @block.tensor
        def _(e):
            for fn in p.ops["tensor"]:
                fn(e)

        @block.gpsimd
        def _(e):
            for fn in p.ops["gpsimd"]:
                fn(e)

    return nc



def _bd_chunks(w):
    bd = np.zeros((NFC, 128, 128), dtype=np.float32)
    for c in range(NFC):
        for g in range(32):
            blk = w[32 * c + g]
            bd[c, 4 * g:4 * g + 4, 4 * g:4 * g + 4] = blk.T
    return bd


def _host_consts():
    ident = np.eye(128, dtype=np.float32)
    maskd = np.zeros((128, 2048), dtype=np.float32)
    pp = np.arange(128)[:, None]
    cc = np.arange(512)[None, :]
    for k in range(4):
        maskd[:, 512 * k:512 * (k + 1)] = np.where(128 * k + pp > cc, 1e30, 0.0)
    onesk = np.ones((1, 128), dtype=np.float32)
    sel4 = np.zeros((4, 512), dtype=np.float32)
    for h in range(4):
        sel4[h, 128 * h:128 * (h + 1)] = 1.0
    return ident, maskd, onesk, sel4


def kernel(L0_attn_x, L0_attn_wq, L0_attn_wk, L0_attn_wv, L0_attn_wi,
           L0_attn_bi, L0_attn_wf, L0_attn_bf, L0_attn_ln_w,
           L0_ffn_w_gate, L0_ffn_w_up, L0_ffn_w_down):
    x = np.ascontiguousarray(np.asarray(L0_attn_x, dtype=np.float32))
    ident, maskd, onesk, sel4 = _host_consts()
    common = {
        "wup": np.ascontiguousarray(np.asarray(L0_ffn_w_up, np.float32)),
        "wgate": np.ascontiguousarray(np.asarray(L0_ffn_w_gate, np.float32)),
        "wdown": np.ascontiguousarray(np.asarray(L0_ffn_w_down, np.float32)),
        "bdq": _bd_chunks(np.asarray(L0_attn_wq, np.float32)),
        "bdk": _bd_chunks(np.asarray(L0_attn_wk, np.float32)),
        "bdv": _bd_chunks(np.asarray(L0_attn_wv, np.float32)),
        "wit": np.ascontiguousarray(np.asarray(L0_attn_wi, np.float32).T),
        "wft": np.ascontiguousarray(np.asarray(L0_attn_wf, np.float32).T),
        "bi": np.ascontiguousarray(np.asarray(L0_attn_bi, np.float32).reshape(4, 1)),
        "bf4": np.ascontiguousarray(np.asarray(L0_attn_bf, np.float32).reshape(4, 1)),
        "lnw": np.ascontiguousarray(np.asarray(L0_attn_ln_w, np.float32).reshape(1, D)),
        "ident": ident, "maskd": maskd, "onesk": onesk, "sel4": sel4,
    }
    nc = build_program()
    in_maps = [dict(common, x=np.ascontiguousarray(x[c])) for c in range(8)]
    res = run_bass_kernel_spmd(nc, in_maps, core_ids=list(range(8)))
    out = np.stack([res.results[c]["out"] for c in range(8)], axis=0)
    return out.astype(np.float32)


# version 12
# speedup vs baseline: 1.0249x; 1.0249x over previous
import numpy as np
from contextlib import ExitStack

import concourse.bass as bass
import concourse.mybir as mybir
from concourse.bass_utils import run_bass_kernel_spmd

F32 = mybir.dt.float32
F32R = mybir.dt.float32r
BF16 = mybir.dt.bfloat16
I32 = mybir.dt.int32
ALU = mybir.AluOpType
ACT = mybir.ActivationFunctionType
AX = mybir.AxisListType

ENGINES = ("sync", "vector", "scalar", "tensor", "gpsimd")
RG = [[0, 1, 2, 3, 4, 5, 6, 7]]

S, D, NH, DH, I3 = 2048, 1024, 4, 256, 3072
NTC = S // 128
NFC = D // 128
NIC = I3 // 128

N_PHASES = 99
FFN_STAGES = 99
NOG_LIM = 99


class Prog:

    N_DMA = {"sync": 8, "gpsimd": 8, "scalar": 2}

    def __init__(self, nc, es):
        self.nc = nc
        self.sems = {}
        for name in ("sv", "ss", "st_", "sg", "sy"):
            self.sems[name] = [es.enter_context(nc.semaphore(name)), 0]
        self.own = {"vector": "sv", "scalar": "ss", "tensor": "st_",
                    "gpsimd": "sg", "sync": "sy"}
        for eng, n in self.N_DMA.items():
            for i in range(n):
                name = "q_%s%d" % (eng, i)
                self.sems[name] = [es.enter_context(nc.semaphore(name)), 0]
        self.N_TOK = 8
        for e in ENGINES:
            for i in range(self.N_TOK):
                nm = "t_%s%d" % (e, i)
                self.sems[nm] = [es.enter_context(nc.semaphore(nm)), 0]
        self.tok_rr = {e: 0 for e in ENGINES}
        self.ops = {e: [] for e in ENGINES}
        self.guar = {e: {} for e in ENGINES}
        self.lastw = {}
        self.lastr = {}
        self.rec = []

    @staticmethod
    def _is_ap(a):
        return hasattr(a, "tensor") and hasattr(a, "offset") and hasattr(a, "ap")

    @staticmethod
    def _rng(ap):
        t = ap.tensor
        import concourse.mybir as _mb
        esz = _mb.dt.size(ap.dtype)
        shape = list(t.shape)
        row = shape[-1] if len(shape) > 1 else shape[0]
        tesz = _mb.dt.size(t.dtype)
        row_bytes = row * tesz
        off_b = ap.offset * esz
        col_lo = off_b % row_bytes if row_bytes else 0
        span = 0
        for step, count in list(ap.ap):
            sb = abs(int(step)) * esz
            if row_bytes and sb % row_bytes == 0 and sb != 0:
                continue
            span += (int(count) - 1) * sb
        return (t.name, col_lo, col_lo + span + esz)

    _WRITES = {
        "tensor_scalar": [0], "tensor_tensor": [0], "tensor_copy": [0],
        "tensor_sub": [0], "tensor_add": [0], "tensor_mul": [0],
        "tensor_max": [0], "scalar_tensor_tensor": [0],
        "tensor_tensor_scan": [0], "reduce_max": [0], "reduce_sum": [0],
        "tensor_reduce": [0], "reciprocal": [0], "memset": [0],
        "activation": [0], "transpose": [0], "matmul": [0], "iota": [0],
    }

    def _autodeps(self, eng, meth, args, kw, tokval_next):
        reads, writes = [], []
        wpos = self._WRITES.get(meth, [])
        for i, a in enumerate(args):
            if self._is_ap(a):
                (writes if i in wpos else reads).append(a)
        for k, a in kw.items():
            if self._is_ap(a):
                if k in ("out", "accum_out"):
                    writes.append(a)
                else:
                    reads.append(a)
        deps = []
        psum_pe = (eng == "tensor" and meth == "matmul"
                   and kw.get("start") is False)
        for a in reads:
            r = self._rng(a)
            for (lo, hi, tk, teng) in self.lastw.get(r[0], []):
                if hi > r[1] and r[2] > lo:
                    if psum_pe and teng == "tensor" and r[0].startswith("pb"):
                        continue
                    deps.append(tk)
        for a in writes:
            r = self._rng(a)
            for m in (self.lastw, self.lastr):
                for (lo, hi, tk, teng) in m.get(r[0], []):
                    if hi > r[1] and r[2] > lo:
                        if psum_pe and teng == "tensor" and r[0].startswith("pb"):
                            continue
                        deps.append(tk)
        return deps, reads, writes

    def _note(self, eng, reads, writes, tokval):
        for a in writes:
            nm, lo, hi = self._rng(a)
            lst = self.lastw.setdefault(nm, [])
            lst[:] = [e for e in lst if not (lo <= e[0] and e[1] <= hi)]
            lst.append((lo, hi, tokval, eng))
            rl = self.lastr.setdefault(nm, [])
            rl[:] = [e for e in rl if not (lo <= e[0] and e[1] <= hi)]
        for a in reads:
            nm, lo, hi = self._rng(a)
            self.lastr.setdefault(nm, []).append((lo, hi, tokval, eng))

    @staticmethod
    def _flat(deps):
        out = []
        for d in deps:
            if d is None:
                continue
            if isinstance(d, list):
                out.extend(x for x in d if x is not None)
            else:
                out.append(d)
        return out

    def emit(self, eng, meth, *args, deps=(), tok=False, _inc=None,
             note_tok=None, _skip=None, **kw):
        if eng == "tensor" and meth == "nop":
            ps1, id1 = self.tensor_nop_args
            meth = "transpose"
            args = (ps1, id1, id1)
        adeps, rds, wrs = self._autodeps(eng, meth, args, kw, None)
        oid = len(self.rec)
        all_deps = []
        for d in self._flat(list(deps) + adeps):
            if _skip is not None and d == _skip:
                continue
            all_deps.append(d)
        qtok = None
        if _inc is not None:
            q = _inc
            self.sems[q][1] += 16
            qtok = (q, self.sems[q][1])
        self.rec.append({"eng": eng, "meth": meth, "args": args, "kw": kw,
                         "deps": all_deps, "qtok": qtok})
        tokval = oid if qtok is None else (note_tok if note_tok is not None
                                           else qtok)
        self._note(eng, rds, wrs, oid if qtok is None else tokval)
        return tokval

    def finalize(self):
        used = set()
        for r in self.rec:
            for d in r["deps"]:
                if isinstance(d, int):
                    used.add(d)
        assign = {}
        for oid, r in enumerate(self.rec):
            if r["qtok"] is not None:
                assign[oid] = ("__dma__", r["qtok"])
                continue
            if oid in used:
                eng = r["eng"]
                q = "t_%s%d" % (eng, self.tok_rr[eng] % self.N_TOK)
                self.tok_rr[eng] += 1
                drain = (q, self.sems[q][1])
                self.sems[q][1] += 1
                assign[oid] = (q, self.sems[q][1])
                r["drain"] = drain
        for oid, r in enumerate(self.rec):
            eng = r["eng"]
            waits = []
            wl = list(r["deps"])
            if "drain" in r:
                wl.append(r["drain"])
            for d in wl:
                if isinstance(d, int):
                    a = assign[d]
                    d = a[1] if a[0] == "__dma__" else a
                sname, val = d
                if self.guar[eng].get(sname, 0) >= val:
                    continue
                self.guar[eng][sname] = val
                waits.append((self.sems[sname][0], val))
            inc = None
            if r["qtok"] is not None:
                inc = (self.sems[r["qtok"][0]][0], 16)
            elif oid in assign:
                a = assign[oid]
                inc = (self.sems[a[0]][0], 1)

            def run(e, meth=r["meth"], args=r["args"], kw=r["kw"],
                    waits=waits, inc=inc):
                for h, v in waits:
                    e.wait_ge(h, v)
                ins = getattr(e, meth)(*args, **kw)
                if inc is not None:
                    ins.then_inc(*inc)
            self.ops[eng].append(run)

    def dma(self, eng, slot, pairs, deps=()):
        q = "q_%s%d" % (eng, slot % self.N_DMA[eng])
        drain = (q, self.sems[q][1])
        final = (q, self.sems[q][1] + 16 * len(pairs))
        for j, (o, i_) in enumerate(pairs):
            dd = list(deps) + [drain] if j == 0 else ()
            self.emit(eng, "dma_start", out=o, in_=i_, deps=dd, _inc=q,
                      note_tok=final, _skip=final)
        return final

    def qtot(self, eng):
        return [("q_%s%d" % (eng, i), self.sems["q_%s%d" % (eng, i)][1])
                for i in range(self.N_DMA[eng])]

    def barrier(self):
        toks = {}
        for e in ENGINES:
            toks[e] = self.emit(e, "nop")
        qvals = []
        for eng in self.N_DMA:
            qvals += self.qtot(eng)
        for e in ENGINES:
            deps = [toks[e2] for e2 in ENGINES if e2 != e] + qvals
            self.emit(e, "nop", deps=deps)


def build_program():
    nc = bass.Bass("TRN2", target_bir_lowering=False)

    def din(name, shape, dt=F32):
        return nc.dram_tensor(name, shape, dt, kind="ExternalInput").ap()

    def dint(name, shape, dt=F32):
        return nc.dram_tensor(name, shape, dt, kind="Internal").ap()

    x_d = din("x", [S, D])
    wup_d = din("wup", [I3, D])
    wgate_d = din("wgate", [I3, D])
    wdown_d = din("wdown", [D, I3])
    bdq_d = din("bdq", [NFC, 128, 128])
    bdk_d = din("bdk", [NFC, 128, 128])
    bdv_d = din("bdv", [NFC, 128, 128])
    wit_d = din("wit", [I3, 4])
    wft_d = din("wft", [I3, 4])
    bi_d = din("bi", [4, 1])
    bf_d = din("bf4", [4, 1])
    lnw_d = din("lnw", [1, D])
    ident_d = din("ident", [128, 128])
    maskd_d = din("maskd", [128, 2048])
    onesk_d = din("onesk", [1, 128])
    sel4_d = din("sel4", [4, 512])
    out_d = nc.dram_tensor("out", [S, D], F32, kind="ExternalOutput").ap()

    pu_d = dint("pu", [I3, S])
    pg_d = dint("pg", [I3, S])
    m1_d = dint("m1", [I3, S])
    op_d = dint("op", [I3, S])
    nohi_d = dint("nohi", [I3, S], BF16)
    nolo_d = dint("nolo", [I3, S], BF16)
    wuphi_d = dint("wuphi", [D, I3], BF16)
    wuplo_d = dint("wuplo", [D, I3], BF16)
    wghi_d = dint("wghi", [D, I3], BF16)
    wglo_d = dint("wglo", [D, I3], BF16)
    wdhi_d = dint("wdhi", [I3, D], BF16)
    wdlo_d = dint("wdlo", [I3, D], BF16)
    xfm_d = dint("xfmsp", [D, S])
    ccin_d = [dint("ccin%d" % k, [1, 2]) for k in range(8)]
    ccout_d = [dint("ccout%d" % k, [1, 2]) for k in range(8)]

    es = ExitStack()

    def sb(name, shape, dt=F32):
        return es.enter_context(nc.sbuf_tensor(name, shape, dt))

    with es:
        xres = sb("xres", [128, NTC * 1024])
        arA = sb("arA", [128, 12288])
        arB = sb("arB", [128, 6144])
        nhi = sb("nhi", [128, 16384], BF16)
        nlo = sb("nlo", [128, 16384], BF16)
        colA = sb("colA", [128, 80])
        gcols = sb("gcols", [128, 192])
        identt = sb("identt", [128, 128])
        onest = sb("onest", [1, 128])
        sel4t = sb("sel4t", [4, 512])
        scrow = sb("scrow", [1, 64])
        ccio = sb("ccio", [1, 4])
        psb = [es.enter_context(nc.psum_tensor("pb%d" % i, [128, 512], F32))
               for i in range(8)]

        arA_bf = arA[:].bitcast(BF16)
        arB_i32 = arB[:].bitcast(I32)
        nhi_f = nhi[:].bitcast(F32)
        nlo_f = nlo[:].bitcast(F32)

        p = Prog(nc, es)
        p.tensor_nop_args = (psb[7][0:1, 0:1], identt[0:1, 0:1])
        E = p.emit
        cc_ev = [0]

        def cA(i):
            return colA[:, i:i + 1]

        def srow(i, n=1):
            return scrow[0:1, i:i + n]

        p.dma("sync", 0, [(xres[:, 1024 * c:1024 * (c + 1)],
                           x_d[128 * c:128 * (c + 1), :]) for c in range(NTC)])
        p.dma("sync", 1, [(identt[:], ident_d[:, :]), (onest[:], onesk_d[:, :]),
                          (sel4t[:], sel4_d[:, :])])
        E("vector", "memset", colA[:, 79:80], 1e-5, tok=True)
        E("vector", "memset", colA[:, 78:79], 1.0, tok=True)
        p.barrier()

        wp_state = {"load": [None, None], "done": [None, None],
                    "stg": [None, None], "t5": None}

        def w_prep(w_nat, hi_dst, lo_dst, O, I):
            noch = O // 128
            nb = I // 32
            arB_bf = arB[:].bitcast(BF16)
            stg_hi = [arB_bf[:, 8704:9216], arB_bf[:, 9216:9728]]
            stg_lo = [arB_bf[:, 9728:10240], arB_bf[:, 10240:10752]]
            tok_load = wp_state["load"]
            tok_done = wp_state["done"]
            tok_stg = wp_state["stg"]
            bm = arB[:, 4000:4000 + nb]
            sA = arB[:, 4128:4128 + nb]
            rs = arB[:, 4256:4256 + nb]
            for och in range(noch):
                b = och % 2
                wnat = arA[:, I * b: I * (b + 1)]
                wdq = arA[:, (2 + b) * I: (3 + b) * I]
                dl = [tok_done[b], wp_state.get("t5_%d" % b)]
                if och < 2:
                    dl += [tok_done[0], tok_done[1], wp_state.get("t5_0"),
                           wp_state.get("t5_1"), tok_stg[0], tok_stg[1]]
                tok_load[b] = p.dma("sync", b, [(wnat,
                                w_nat[128 * och:128 * (och + 1), :])],
                                deps=dl)
                w3 = wnat.rearrange("p (g e) -> p g e", e=32)
                t1 = E("vector", "reduce_max", bm.rearrange("p (g o) -> p g o", o=1),
                       w3, axis=AX.X, apply_absolute_value=True,
                       deps=[tok_load[b]], tok=True)
                t2 = E("vector", "tensor_scalar", sA, bm, 1.0 / 7.0, 1e-12,
                       ALU.mult, ALU.max, deps=[t1], tok=True)
                t3 = E("vector", "reciprocal", rs, sA, deps=[t2], tok=True)
                q32 = arB_i32[:, 0:I]
                t4 = E("vector", "tensor_tensor",
                       q32.rearrange("p (g e) -> p g e", e=32), w3,
                       rs.rearrange("p (g o) -> p g o", o=1).broadcast_to((128, nb, 32)),
                       ALU.mult, deps=[t3], tok=True)
                t5 = E("vector", "tensor_tensor",
                       wdq.rearrange("p (g e) -> p g e", e=32),
                       q32.rearrange("p (g e) -> p g e", e=32),
                       sA.rearrange("p (g o) -> p g o", o=1).broadcast_to((128, nb, 32)),
                       ALU.mult, deps=[t4, tok_done[0], tok_done[1]], tok=True)
                wp_state["t5_%d" % b] = t5
                last_tp = None
                for ts_ in range(I // 512):
                    bank = psb[4 + (ts_ % 2)]
                    sb_ = ts_ % 2
                    for k in range(4):
                        last_tp = E("tensor", "transpose", bank[:, 128 * k:128 * (k + 1)],
                                    wdq[:, 512 * ts_ + 128 * k: 512 * ts_ + 128 * (k + 1)],
                                    identt[:],
                                    deps=[t5, tok_stg[sb_]] if k == 0 else [],
                                    tok=(k == 3))
                    th = E("vector", "tensor_copy", stg_hi[sb_], bank[:],
                           deps=[last_tp, tok_stg[sb_]], tok=True)
                    tl = E("vector", "tensor_sub", stg_lo[sb_], bank[:], stg_hi[sb_],
                           deps=[last_tp, th], tok=True)
                    if ts_ == I // 512 - 1:
                        tok_done[b] = tl
                    dst = slice(512 * ts_, 512 * (ts_ + 1))
                    hi_ap = hi_dst[dst, 128 * och:128 * (och + 1)] \
                        .rearrange("(k q) c -> q k c", q=128)
                    lo_ap = lo_dst[dst, 128 * och:128 * (och + 1)] \
                        .rearrange("(k q) c -> q k c", q=128)
                    tok_stg[sb_] = p.dma("gpsimd", sb_, [
                        (hi_ap, stg_hi[sb_].rearrange("p (k c) -> p k c", c=128)),
                        (lo_ap, stg_lo[sb_].rearrange("p (k c) -> p k c", c=128))],
                        deps=[th, tl])

        if N_PHASES >= 1:
            w_prep(wup_d, wuphi_d, wuplo_d, I3, D)
            w_prep(wgate_d, wghi_d, wglo_d, I3, D)
            w_prep(wdown_d, wdhi_d, wdlo_d, D, I3)
            p.barrier()

        def allreduce_max2(v0_ap, v1_ap, tok_in):
            k = cc_ev[0]
            cc_ev[0] += 1
            tc0 = E("vector", "tensor_copy", ccio[0:1, 0:1], v0_ap,
                    deps=tok_in, tok=True)
            tc1 = E("vector", "tensor_copy", ccio[0:1, 1:2], v1_ap,
                    deps=tok_in, tok=True)
            td = E("gpsimd", "dma_start", out=ccin_d[k][:, :], in_=ccio[0:1, 0:2],
                   deps=[tc0, tc1], dma=True)
            tcc = E("gpsimd", "collective_compute", "AllReduce", ALU.max,
                    RG, [ccin_d[k][:, :]], [ccout_d[k][:, :]],
                    deps=[td], tok=True)
            tr = E("gpsimd", "dma_start", out=ccio[0:1, 2:4], in_=ccout_d[k][:, :],
                   deps=[tcc], dma=True)
            return tr

        def bcast_cols(src_row_ap, n, dst_col0, deps):
            tm = E("tensor", "matmul", psb[6][:, 0:n], onest[:], src_row_ap,
                   start=True, stop=True, deps=deps, tok=True)
            return E("scalar", "activation", colA[:, dst_col0:dst_col0 + n],
                     psb[6][:, 0:n], ACT.Identity, deps=[tm], tok=True)

        last_bcast = [None]

        def bcast_cols2(src_row_ap, n, dst_col0, deps):
            tm = E("tensor", "matmul", psb[6][:, 0:n], onest[:], src_row_ap,
                   start=True, stop=True, deps=list(deps) + [last_bcast[0]],
                   tok=True)
            tev = E("vector", "tensor_copy", colA[:, dst_col0:dst_col0 + n],
                    psb[6][:, 0:n], deps=[tm], tok=True)
            last_bcast[0] = tev
            return tev

        def allred2(tok_deps, ccio_deps):
            k = cc_ev[0]
            cc_ev[0] += 1
            td = p.dma("gpsimd", 6, [(ccin_d[k][:, :], ccio[0:1, 0:2])],
                       deps=list(tok_deps) + list(ccio_deps))
            tcc = E("gpsimd", "collective_compute", "AllReduce", ALU.max,
                    RG, [ccin_d[k][:, :]], [ccout_d[k][:, :]], deps=[td], tok=True)
            return p.dma("gpsimd", 6, [(ccio[0:1, 2:4], ccout_d[k][:, :])],
                         deps=[tcc])

        def ffn_phase():
            def xln_ap(c):
                if c < 12:
                    return arA[:, 1024 * c:1024 * (c + 1)]
                return arB[:, 1024 * (c - 12):1024 * (c - 11)]
            tok_xln = []
            tok_absm = []
            junk = nhi_f[:, 0:1024]
            for c in range(NTC):
                xc = xres[:, 1024 * c:1024 * (c + 1)]
                ts_ = E("vector", "reduce_sum", cA(48 + c), xc, axis=AX.X, tok=True)
                tn = E("scalar", "activation", cA(c), cA(48 + c), ACT.Identity,
                       scale=-1.0 / D, deps=[ts_], tok=True)
                tq = E("scalar", "activation", junk, xc, ACT.Square,
                       bias=cA(c), accum_out=cA(48 + c), deps=[tn], tok=True)
                tr0 = E("scalar", "activation", cA(48 + c), cA(48 + c), ACT.Sqrt,
                        scale=1.0 / D, bias=cA(79), deps=[tq], tok=True)
                tr = E("vector", "reciprocal", cA(16 + c), cA(48 + c),
                       deps=[tr0], tok=True)
                tx = E("vector", "tensor_scalar", xln_ap(c), xc, cA(c), cA(16 + c),
                       ALU.add, ALU.mult, deps=[tn, tr], tok=True)
                tok_xln.append(tx)
                ta = E("vector", "reduce_max", cA(32 + c), xln_ap(c), axis=AX.X,
                       apply_absolute_value=True, deps=[tx], tok=True)
                tok_absm.append(ta)
            rowb = nhi_f[0:1, 1024:3072]
            tdm = p.dma("gpsimd", 7, [(rowb, colA[:, 32:48])], deps=tok_absm)
            trm = E("vector", "reduce_max", ccio[0:1, 0:1], rowb, axis=AX.X,
                    deps=[tdm], tok=True)
            tz = E("vector", "memset", ccio[0:1, 1:2], 0.0, tok=True)
            tr1 = allred2([trm, tz], [])
            ta1 = E("vector", "reciprocal", srow(0), ccio[0:1, 2:3], deps=[tr1], tok=True)
            ta2 = E("vector", "tensor_scalar", srow(1), srow(0), 32767.0, None,
                    ALU.mult, deps=[ta1], tok=True)
            E("vector", "tensor_scalar", srow(2), ccio[0:1, 2:3],
              1.0 / 32767.0, None, ALU.mult, deps=[tr1], tok=True)
            tbc1 = bcast_cols2(srow(1), 1, 64, [ta2])
            if FFN_STAGES <= 1:
                p.barrier()
                return

            ni32 = arB_i32[:, 4096:5120]
            nf32 = arB[:, 5120:6144]
            tok_ev = {}
            tp_done = [None, None]
            nhi3 = nhi[:].rearrange("p (f t) -> p f t", t=2048)
            nlo3 = nlo[:].rearrange("p (f t) -> p f t", t=2048)
            for c in range(NTC):
                t1 = E("scalar", "activation", ni32, xln_ap(c), ACT.Identity,
                       scale=cA(64), deps=[tok_xln[c], tbc1, tok_ev.get(c - 1)],
                       tok=True)
                t2 = E("scalar", "activation", nf32, ni32, ACT.Identity,
                       deps=[t1], tok=True)
                for half in range(2):
                    bank = psb[4 + half]
                    ltp = None
                    for kk in range(4):
                        f = 4 * half + kk
                        ltp = E("tensor", "transpose", bank[:, 128 * kk:128 * (kk + 1)],
                                nf32[:, 128 * f:128 * (f + 1)], identt[:],
                                deps=[t2, tp_done[half]] if kk == 0 else [],
                                tok=(kk == 3))
                    dsth = nhi3[:, 4 * half:4 * half + 4, 128 * c:128 * (c + 1)]
                    dstl = nlo3[:, 4 * half:4 * half + 4, 128 * c:128 * (c + 1)]
                    th = E("vector", "tensor_copy",
                           dsth, bank[:].rearrange("p (k c) -> p k c", c=128),
                           deps=[ltp], tok=True)
                    tl = E("vector", "tensor_tensor",
                           dstl, bank[:].rearrange("p (k c) -> p k c", c=128),
                           dsth, ALU.subtract, deps=[ltp, th], tok=True)
                    tp_done[half] = tl
                    if half == 1:
                        tok_ev[c] = tl
            tok_n = [tp_done[0], tp_done[1], tok_ev[NTC - 1]]
            if FFN_STAGES <= 2:
                p.barrier()
                return

            tmu = E("vector", "memset", cA(71), 0.0, deps=tok_absm, tok=True)
            tmg = E("vector", "memset", cA(72), 0.0, tok=True)
            tok_wload = {}
            tok_mmdone = {}
            tok_evac = {}
            tok_stgout = [None, None]
            srcs = [(wuphi_d, 0), (wuplo_d, 1), (wghi_d, 2), (wglo_d, 3)]
            NOG = min(12, NOG_LIM)
            stg_ctr = [0]
            tcp = None
            for og in range(NOG):
                b = og % 2
                dep_reuse = [tok_mmdone.get(og - 2)] + (tok_n if og < 2 else [])
                wpairs = []
                for (src_, s_) in srcs:
                    dst = arA_bf[:, 8192 * b + 2048 * s_: 8192 * b + 2048 * (s_ + 1)] \
                        .rearrange("p (f c) -> p f c", c=256)
                    sap = src_[:, 256 * og:256 * (og + 1)] \
                        .rearrange("(f q) c -> q f c", q=128)
                    wpairs.append((dst, sap))
                tok_wload[og] = p.dma("sync", 2 + b, wpairs, deps=dep_reuse)
                last_mm = None
                for mat in range(2):
                    wh3 = arA_bf[:, 8192 * b + 2048 * (2 * mat):
                                 8192 * b + 2048 * (2 * mat + 1)] \
                        .rearrange("p (f c) -> p f c", c=256)
                    wl3 = arA_bf[:, 8192 * b + 2048 * (2 * mat + 1):
                                 8192 * b + 2048 * (2 * mat + 2)] \
                        .rearrange("p (f c) -> p f c", c=256)
                    for o2 in range(2):
                        och = 2 * og + o2
                        sp = stg_ctr[0] % 2
                        stg_ctr[0] += 1
                        stg = arA[:, 8192 + 2048 * sp:8192 + 2048 * (sp + 1)]
                        for tt in range(4):
                            bank_i = tt % 4
                            bank = psb[bank_i]
                            first = True
                            for f in range(NFC):
                                whi = wh3[:, f, 128 * o2:128 * (o2 + 1)]
                                wlo = wl3[:, f, 128 * o2:128 * (o2 + 1)]
                                rhs_h = nhi3[:, f, 512 * tt:512 * (tt + 1)]
                                rhs_l = nlo3[:, f, 512 * tt:512 * (tt + 1)]
                                dd = ([tok_wload[og],
                                       tok_evac.get(bank_i)] + tok_n) \
                                    if first else []
                                E("tensor", "matmul", bank[:], whi, rhs_h,
                                  start=first, stop=False, deps=dd)
                                E("tensor", "matmul", bank[:], whi, rhs_l,
                                  start=False, stop=False)
                                last_mm = E("tensor", "matmul", bank[:], wlo, rhs_h,
                                            start=False, stop=(f == NFC - 1),
                                            tok=(f == NFC - 1))
                                first = False
                            tr_ = E("vector", "reduce_max", cA(73), bank[:],
                                    axis=AX.X, apply_absolute_value=True,
                                    deps=[last_mm], tok=True)
                            E("vector", "tensor_tensor", cA(71 + mat),
                              cA(71 + mat), cA(73), ALU.max,
                              deps=[tr_, tmu, tmg], tok=True)
                            tcp = E("vector", "tensor_copy",
                                    stg[:, 512 * tt:512 * (tt + 1)], bank[:],
                                    deps=[last_mm, tok_stgout[sp]], tok=True)
                            tok_evac[bank_i] = tcp
                        dst_d = (pu_d if mat == 0 else pg_d)
                        tok_stgout[sp] = p.dma(
                            "gpsimd", sp,
                            [(dst_d[128 * och:128 * (och + 1), :], stg)],
                            deps=[tcp])
                tok_mmdone[og] = last_mm
            tok_ug_end = [tok_mmdone[NOG - 1], tok_evac.get(0), tok_evac.get(1),
                          tok_evac.get(2), tok_evac.get(3), tok_stgout[0],
                          tok_stgout[1]]
            if FFN_STAGES <= 2.5:
                p.barrier()
                return
            t7a = p.dma("gpsimd", 7, [(nhi_f[0:1, 1024:1152], cA(71)),
                                      (nhi_f[0:1, 1152:1280], cA(72))],
                        deps=tok_ug_end)
            t7b = t7a
            tru = E("vector", "reduce_max", ccio[0:1, 0:1], nhi_f[0:1, 1024:1152],
                    axis=AX.X, deps=[t7a, tr1], tok=True)
            trg = E("vector", "reduce_max", ccio[0:1, 1:2], nhi_f[0:1, 1152:1280],
                    axis=AX.X, deps=[t7b], tok=True)
            tr2 = allred2([tru, trg], [])
            tb1 = E("vector", "reciprocal", srow(0), ccio[0:1, 2:3], deps=[tr2], tok=True)
            tb2 = E("vector", "tensor_scalar", srow(3), srow(0), 32767.0, None,
                    ALU.mult, deps=[tb1], tok=True)
            tb3 = E("vector", "tensor_scalar", srow(8), ccio[0:1, 2:3], 1.0 / 32767.0,
                    None, ALU.mult, deps=[tr2], tok=True)
            tb4 = E("vector", "tensor_tensor", srow(4), srow(8), srow(2), ALU.mult,
                    deps=[tb3], tok=True)
            tb5 = E("vector", "reciprocal", srow(0), ccio[0:1, 3:4], deps=[tr2, tb2],
                    tok=True)
            tb6 = E("vector", "tensor_scalar", srow(5), srow(0), 32767.0, None,
                    ALU.mult, deps=[tb5], tok=True)
            tb7 = E("vector", "tensor_scalar", srow(8), ccio[0:1, 3:4], 1.0 / 32767.0,
                    None, ALU.mult, deps=[tb5, tb4], tok=True)
            tb8 = E("vector", "tensor_tensor", srow(6), srow(8), srow(2), ALU.mult,
                    deps=[tb7], tok=True)
            tb9 = E("vector", "tensor_copy", srow(9), srow(3), deps=[tb2], tok=True)
            tba = E("vector", "tensor_copy", srow(10), srow(5), deps=[tb6], tok=True)
            tbb = E("vector", "tensor_copy", srow(11), srow(6), deps=[tb8], tok=True)
            tbc2 = bcast_cols2(srow(9, 3), 3, 65, [tb9, tba, tbb])
            if FFN_STAGES <= 3:
                p.barrier()
                return

            tmm = E("vector", "memset", cA(74), 0.0, tok=True)
            tok_use = {}
            tok_m1w = {}
            st_e2 = {}
            def e2_head(i):
                b = i % 4
                pgin = arA[:, 2048 * b:2048 * (b + 1)]
                ngi = arB_i32[:, 2048 * (i % 2):2048 * (i % 2 + 1)]
                sigb = arA[:, 8192 + 2048 * (i % 2):8192 + 2048 * (i % 2 + 1)]
                d0 = (tok_ug_end if i < 2 else [])
                if i == 0:
                    d0 = d0 + p.qtot("gpsimd")
                trd_ = p.dma("sync", i % 4, [(pgin, pg_d[128 * i:128 * (i + 1), :])],
                             deps=d0)
                tg = E("vector", "tensor_scalar", ngi, pgin, cA(66), None,
                       ALU.mult, deps=[trd_, tbc2], tok=True)
                tsg = E("scalar", "activation", sigb, ngi, ACT.Sigmoid,
                        scale=cA(67), deps=[tg], tok=True)
                st_e2[i] = (pgin, ngi, sigb)
            def e2_tail(i):
                pgin, ngi, sigb = st_e2[i]
                m1o = pgin
                ri32 = sigb.bitcast(I32)
                tri = E("vector", "tensor_scalar", ri32, sigb, 65536.0, None,
                        ALU.mult, tok=True)
                tm1 = E("vector", "scalar_tensor_tensor", m1o, ri32, 65535.0,
                        ngi, ALU.min, ALU.mult,
                        deps=[tri, tok_m1w.get(i - 2)], tok=True)
                trd2 = E("vector", "reduce_max", cA(73), m1o, axis=AX.X,
                         apply_absolute_value=True, deps=[tm1], tok=True)
                tok_use[i] = E("vector", "tensor_tensor", cA(74), cA(74),
                               cA(73), ALU.max, deps=[trd2, tmm], tok=True)
                tok_m1w[i] = p.dma("gpsimd", i % 4,
                                   [(m1_d[128 * i:128 * (i + 1), :], m1o)],
                                   deps=[tm1])
            e2_head(0)
            for i in range(1, NIC):
                e2_head(i)
                e2_tail(i - 1)
            e2_tail(NIC - 1)
            t8a = p.dma("gpsimd", 7, [(nhi_f[0:1, 1024:1152], cA(74))],
                        deps=[tok_use[NIC - 1]])
            trm1 = E("vector", "reduce_max", ccio[0:1, 0:1], nhi_f[0:1, 1024:1152],
                     axis=AX.X, deps=[t8a, tr2], tok=True)
            tz3 = E("vector", "memset", ccio[0:1, 1:2], 0.0, deps=[tr2], tok=True)
            tr3 = allred2([trm1, tz3], [])
            tc1 = E("vector", "reciprocal", srow(0), ccio[0:1, 2:3], deps=[tr3], tok=True)
            tc2 = E("vector", "tensor_scalar", srow(12), srow(0), 32767.0, None,
                    ALU.mult, deps=[tc1], tok=True)
            tc3 = E("vector", "tensor_tensor", srow(8), ccio[0:1, 2:3], srow(6),
                    ALU.mult, deps=[tr3, tb8], tok=True)
            tc4 = E("vector", "tensor_scalar", srow(8), srow(8),
                    1.0 / (65536.0 * 32767.0), None, ALU.mult, deps=[tc3], tok=True)
            tc5 = E("vector", "tensor_tensor", srow(13), srow(8), srow(4), ALU.mult,
                    deps=[tc4, tb4], tok=True)
            tbc3 = bcast_cols2(srow(12), 1, 68, [tc2])
            if FFN_STAGES <= 4:
                p.barrier()
                return

            tmo = E("vector", "memset", cA(75), 0.0, deps=[tok_use[NIC - 1]],
                    tok=True)
            tok_use2 = {}
            tok_opw = {}
            st_e3 = {}
            def e3_head(i):
                b = i % 3
                puin = arA[:, 2048 * b:2048 * (b + 1)]
                m1in = arA[:, 6144 + 2048 * b:6144 + 2048 * (b + 1)]
                d0 = [tok_use[NIC - 1]]
                if i == 0:
                    d0 += p.qtot("gpsimd")
                trdu = p.dma("sync", i % 2, [(puin,
                             pu_d[128 * i:128 * (i + 1), :])], deps=d0)
                trdm = p.dma("sync", 2 + i % 2, [(m1in,
                             m1_d[128 * i:128 * (i + 1), :])], deps=d0)
                nupi = arB_i32[:, 2048 * (i % 2):2048 * (i % 2 + 1)]
                ng2i = m1in.bitcast(I32)
                tnu = E("vector", "tensor_scalar", nupi, puin, cA(65), None,
                        ALU.mult, deps=[trdu, tbc2], tok=True)
                tng = E("scalar", "activation", ng2i, m1in, ACT.Identity,
                        scale=cA(68), deps=[trdm, tbc3], tok=True)
                st_e3[i] = (puin, m1in, nupi, ng2i, tnu, tng)
            def e3_tail(i):
                puin, m1in, nupi, ng2i, tnu, tng = st_e3[i]
                opo = puin
                top = E("vector", "tensor_tensor", opo, ng2i, nupi, ALU.mult,
                        deps=[tnu, tng, tok_opw.get(i - 2)], tok=True)
                trd2 = E("vector", "reduce_max", cA(73), opo, axis=AX.X,
                         apply_absolute_value=True, deps=[top], tok=True)
                tok_use2[i] = E("vector", "tensor_tensor", cA(75), cA(75),
                                cA(73), ALU.max, deps=[trd2, tmo], tok=True)
                tok_opw[i] = p.dma("gpsimd", i % 4,
                                   [(op_d[128 * i:128 * (i + 1), :], opo)],
                                   deps=[top])
            e3_head(0)
            for i in range(1, NIC):
                e3_head(i)
                e3_tail(i - 1)
            e3_tail(NIC - 1)
            t9a = p.dma("gpsimd", 7, [(nhi_f[0:1, 1024:1152], cA(75))],
                        deps=[tok_use2[NIC - 1]])
            trop = E("vector", "reduce_max", ccio[0:1, 0:1], nhi_f[0:1, 1024:1152],
                     axis=AX.X, deps=[t9a, tr3], tok=True)
            tz4 = E("vector", "memset", ccio[0:1, 1:2], 0.0, deps=[tr3], tok=True)
            tr4 = allred2([trop, tz4], [])
            td1 = E("vector", "reciprocal", srow(0), ccio[0:1, 2:3], deps=[tr4], tok=True)
            td2 = E("vector", "tensor_scalar", srow(16), srow(0), 32767.0, None,
                    ALU.mult, deps=[td1], tok=True)
            td3 = E("vector", "tensor_scalar", srow(8), ccio[0:1, 2:3], 0.5 / 32767.0,
                    None, ALU.mult, deps=[tr4], tok=True)
            td4 = E("vector", "tensor_tensor", srow(17), srow(8), srow(13), ALU.mult,
                    deps=[td3, tc5, td2], tok=True)
            tbc4 = bcast_cols2(srow(16, 2), 2, 69, [td4])
            if FFN_STAGES <= 5:
                p.barrier()
                return

            tok_use3 = {}
            tok_w3 = {}
            tok_w3b = {}
            st_e4 = {}
            def e4_head(i):
                b = i % 4
                opin = arA[:, 2048 * b:2048 * (b + 1)]
                noi = arB_i32[:, 2048 * (i % 2):2048 * (i % 2 + 1)]
                hib = arA_bf[:, 16384 + 2048 * (i % 2):16384 + 2048 * (i % 2 + 1)]
                d0 = [tok_use2[NIC - 1]]
                if i == 0:
                    d0 += p.qtot("gpsimd")
                trdo = p.dma("sync", i % 4, [(opin,
                             op_d[128 * i:128 * (i + 1), :])], deps=d0)
                tno = E("scalar", "activation", noi, opin, ACT.Identity,
                        scale=cA(69), deps=[trdo, tbc4], tok=True)
                thi = E("scalar", "activation", hib, noi, ACT.Identity,
                        deps=[tno, tok_w3.get(i - 2)], tok=True)
                st_e4[i] = (noi, hib, thi)
            def e4_tail(i):
                b = i % 2
                noi, hib, thi = st_e4[i]
                lob = arA_bf[:, 20480 + 2048 * b:20480 + 2048 * (b + 1)]
                tlo = E("vector", "tensor_tensor", lob, noi, hib, ALU.subtract,
                        deps=[thi, tok_w3b.get(i - 2)], tok=True)
                tok_use3[i] = tlo
                tok_w3[i] = p.dma("gpsimd", i % 4,
                                  [(nohi_d[128 * i:128 * (i + 1), :], hib),
                                   (nolo_d[128 * i:128 * (i + 1), :], lob)],
                                  deps=[thi, tlo])
                tok_w3b[i] = tok_w3[i]
            e4_head(0)
            for i in range(1, NIC):
                e4_head(i)
                e4_tail(i - 1)
            e4_tail(NIC - 1)
            tok_ld = {}
            tok_mmd = {}
            ev_last = {}
            for tg_ in range(4):
                for ich in range(NIC):
                    b = ich % 2
                    nhib = arA_bf[:, 512 * b:512 * (b + 1)]
                    nlob = arA_bf[:, 1024 + 512 * b:1024 + 512 * (b + 1)]
                    wdh = arA_bf[:, 2048 + 1024 * b:2048 + 1024 * (b + 1)]
                    wdl = arA_bf[:, 4096 + 1024 * b:4096 + 1024 * (b + 1)]
                    dep0 = [tok_mmd.get((tg_, ich - 2)),
                            tok_mmd.get((tg_ - 1, NIC - 1))]
                    if tg_ == 0 and ich < 2:
                        dep0 += [tok_w3.get(NIC - 1), tok_w3b.get(NIC - 1),
                                 tok_use3.get(NIC - 1)] + p.qtot("gpsimd")
                    t1 = p.dma("sync", b, [
                        (nhib, nohi_d[128 * ich:128 * (ich + 1),
                                      512 * tg_:512 * (tg_ + 1)]),
                        (nlob, nolo_d[128 * ich:128 * (ich + 1),
                                      512 * tg_:512 * (tg_ + 1)]),
                        (wdh, wdhi_d[128 * ich:128 * (ich + 1), :]),
                        (wdl, wdlo_d[128 * ich:128 * (ich + 1), :])],
                        deps=dep0)
                    t2 = t3 = t4 = t1
                    lm = None
                    for tc4_ in range(4):
                        for ht in range(2):
                            bank = psb[2 * tc4_ + ht]
                            first = (ich == 0)
                            last = (ich == NIC - 1)
                            dd = [t1, t2, t3, t4]
                            if first:
                                dd += [ev_last.get(tg_ - 1), last_bcast[0]]
                            lh = nhib[:, 128 * tc4_:128 * (tc4_ + 1)]
                            ll = nlob[:, 128 * tc4_:128 * (tc4_ + 1)]
                            rh = wdh[:, 512 * ht:512 * (ht + 1)]
                            rl = wdl[:, 512 * ht:512 * (ht + 1)]
                            E("tensor", "matmul", bank[:], lh, rh,
                              start=first, stop=False, deps=dd)
                            E("tensor", "matmul", bank[:], lh, rl,
                              start=False, stop=False)
                            lm = E("tensor", "matmul", bank[:], ll, rh,
                                   start=False, stop=last,
                                   tok=(ht == 1 and tc4_ == 3))
                    tok_mmd[(tg_, ich)] = lm
                for tc4_ in range(4):
                    c = 4 * tg_ + tc4_
                    for ht in range(2):
                        bank = psb[2 * tc4_ + ht]
                        ev_last[tg_] = E(
                            "vector", "scalar_tensor_tensor",
                            xres[:, 1024 * c + 512 * ht:1024 * c + 512 * (ht + 1)],
                            bank[:], cA(70),
                            xres[:, 1024 * c + 512 * ht:1024 * c + 512 * (ht + 1)],
                            ALU.mult, ALU.add,
                            deps=[tok_mmd[(tg_, NIC - 1)], tbc4], tok=True)
            p.barrier()


        def attn_phase():
            defq = []

            def drain(n):
                for _ in range(min(n, len(defq))):
                    defq.pop(0)()

            masks = arA[:, 0:2048]
            gamma = arA[:, 3072:4096]
            nhi_b = nhi[:]
            nlo_b = nlo[:]
            qh_b = nhi_b[:, 0:4096]
            ql_b = nhi_b[:, 4096:8192]
            kh_b = nhi_b[:, 8192:12288]
            kl_b = nhi_b[:, 12288:16384]
            vtmh = nlo_b[:, 0:4112]
            vtml = nlo_b[:, 4112:8224]
            vfm = nlo_f[:, 4112:6160]
            cth_b = [nlo_b[:, 12320:12832], nlo_b[:, 13344:13856]]
            ctl_b = [nlo_b[:, 12832:13344], nlo_b[:, 13856:14368]]
            htm = arA[:, 5120:5376]
            htm4 = [nlo_f[:, 4112 + 256 * i:4112 + 256 * (i + 1)] for i in range(4)]
            junk = arA[:, 12032:12288]
            mm_slots = [arA[:, 4096:4608], arA[:, 4608:5120],
                        arA[:, 6144:6656], arA[:, 6656:7168]]
            dsb_s = [arA[:, 2048:2560], arA[:, 2560:3072]]

            def xln_ap(c):
                if c < 12:
                    return arA[:, 1024 * c:1024 * (c + 1)]
                return arB[:, 1024 * (c - 12):1024 * (c - 11)]
            tok_xln = []
            junk0 = nhi_f[:, 0:1024]
            for c in range(NTC):
                xc = xres[:, 1024 * c:1024 * (c + 1)]
                ts_ = E("vector", "reduce_sum", cA(48 + c), xc, axis=AX.X, tok=True)
                tn = E("scalar", "activation", cA(c), cA(48 + c), ACT.Identity,
                       scale=-1.0 / D, deps=[ts_], tok=True)
                tq = E("scalar", "activation", junk0, xc, ACT.Square,
                       bias=cA(c), accum_out=cA(48 + c), deps=[tn], tok=True)
                tr0 = E("scalar", "activation", cA(48 + c), cA(48 + c), ACT.Sqrt,
                        scale=1.0 / D, bias=cA(79), deps=[tq], tok=True)
                tr = E("vector", "reciprocal", cA(16 + c), cA(48 + c),
                       deps=[tr0], tok=True)
                tx = E("vector", "tensor_scalar", xln_ap(c), xc, cA(c), cA(16 + c),
                       ALU.add, ALU.mult, deps=[tn, tr], tok=True)
                tok_xln.append(tx)

            stgA = [arB[:, 4096:4608], arB[:, 4608:5120]]
            for c in range(NTC):
                for half in range(2):
                    bank = psb[4 + half]
                    ltp = None
                    for kk in range(4):
                        f = 4 * half + kk
                        ltp = E("tensor", "transpose",
                                bank[:, 128 * kk:128 * (kk + 1)],
                                xln_ap(c)[:, 128 * f:128 * (f + 1)], identt[:],
                                deps=[tok_xln[c]] if kk == 0 else [], tok=(kk == 3))
                    tev = E("vector", "tensor_copy", stgA[half], bank[:],
                            deps=[ltp], tok=True)
                    dst = xfm_d[512 * half:512 * (half + 1),
                                128 * c:128 * (c + 1)] \
                        .rearrange("(k q) c -> q k c", q=128)
                    p.dma("gpsimd", half, [
                        (dst, stgA[half].rearrange("p (k c) -> p k c", c=128))],
                        deps=[tev])

            wstg = [gcols[:, 128 + 32 * kk:160 + 32 * kk] for kk in range(2)]
            tmp32 = arB[:, 5120:5152]
            tmp32b = arB[:, 5152:5184]
            for kk, wsrc in enumerate((wit_d, wft_d)):
                t1 = p.dma("sync", 4, [
                    (wstg[kk], wsrc[0:1024, :].rearrange("(f q) c -> q f c", q=128)),
                    (tmp32, wsrc[1024:2048, :].rearrange("(f q) c -> q f c", q=128)),
                    (tmp32b, wsrc[2048:3072, :].rearrange("(f q) c -> q f c", q=128))])
                E("vector", "tensor_tensor", wstg[kk], wstg[kk], tmp32, ALU.add,
                  deps=[t1], tok=True)
                E("vector", "tensor_tensor", wstg[kk], wstg[kk], tmp32b, ALU.add,
                  tok=True)
            tbias = p.dma("sync", 5, [(colA[0:4, 76:77], bi_d[:, :]),
                                      (colA[0:4, 77:78], bf_d[:, :]),
                                      (scrow[0:1, 32:33], lnw_d[:, 0:1])])
            trow = p.dma("sync", 5, [(nhi_f[0:1, 1024:2048], lnw_d[:, :])])
            ig_sb = arB[0:4, 0:2048]
            fg_sb = arB[0:4, 2048:4096]
            xchunk = [arA[:, 7168:9216], arA[:, 9216:11264]]
            gmm = {}
            for f in range(NFC):
                tl = p.dma("sync", 6, [(xchunk[f % 2],
                                        xfm_d[128 * f:128 * (f + 1), :])])
                for kk in range(2):
                    w3 = wstg[kk].rearrange("p (f c) -> p f c", c=4)
                    for tt in range(4):
                        gmm[(kk, tt)] = E(
                            "tensor", "matmul", psb[4 * kk + tt][0:4, 0:512],
                            w3[:, f, :], xchunk[f % 2][:, 512 * tt:512 * (tt + 1)],
                            start=(f == 0), stop=(f == NFC - 1),
                            deps=[tl], tok=(f == NFC - 1))
            for kk in range(2):
                dstsb = (ig_sb, fg_sb)[kk]
                for tt in range(4):
                    E("vector", "tensor_copy",
                      dstsb[:, 512 * tt:512 * (tt + 1)],
                      psb[4 * kk + tt][0:4, 0:512],
                      deps=[gmm[(kk, tt)]], tok=True)
            E("vector", "tensor_scalar", ig_sb, ig_sb, colA[0:4, 76:77], None,
              ALU.add, deps=[tbias], tok=True)
            E("vector", "tensor_scalar", fg_sb, fg_sb, colA[0:4, 77:78], None,
              ALU.add, deps=[tbias], tok=True)

            scr = arB[0:4, 4096:6144]
            E("scalar", "activation", scr, fg_sb, ACT.Abs, tok=True)
            E("scalar", "activation", scr, scr, ACT.Exp, scale=-1.0, tok=True)
            E("scalar", "activation", scr, scr, ACT.Ln, bias=colA[0:4, 78:79],
              tok=True)
            E("vector", "tensor_scalar", fg_sb, fg_sb, 0.0, None, ALU.min,
              tok=True)
            E("vector", "tensor_tensor", fg_sb, fg_sb, scr, ALU.subtract,
              tok=True)
            E("vector", "tensor_tensor_scan", scr, fg_sb, fg_sb, 0.0,
              ALU.add, ALU.bypass, tok=True)
            E("vector", "tensor_tensor", ig_sb, ig_sb, scr, ALU.subtract,
              tok=True)
            E("vector", "tensor_tensor_scan", fg_sb, ig_sb, ig_sb, -3e38,
              ALU.max, ALU.bypass, tok=True)
            E("vector", "tensor_tensor", scr, scr, fg_sb, ALU.add, tok=True)
            E("scalar", "activation", scr, scr, ACT.Exp, scale=-1.0, tok=True)
            b_sb, M_sb, em_sb = ig_sb, fg_sb, scr

            for kk, (src_, dst_) in enumerate(((b_sb, gcols[:, 0:64]),
                                               (em_sb, gcols[:, 64:128]))):
                for grp in range(4):
                    bank = psb[1]
                    for s4 in range(4):
                        s16 = 4 * grp + s4
                        E("tensor", "transpose", bank[:, 4 * s4:4 * s4 + 4],
                          src_[:, 128 * s16:128 * (s16 + 1)],
                          identt[0:4, 0:4], tok=True)
                    E("vector", "tensor_copy",
                      dst_[:, 16 * grp:16 * (grp + 1)], bank[:, 0:16],
                      tok=True)

            tpl = E("vector", "tensor_scalar", nhi_f[0:1, 1024:2048],
                    nhi_f[0:1, 1024:2048], 1.0, None, ALU.add, deps=[trow],
                    tok=True)
            for half in range(2):
                tm = E("tensor", "matmul", psb[0][:, 0:512], onest[:],
                       nhi_f[0:1, 1024 + 512 * half:1024 + 512 * (half + 1)],
                       start=True, stop=True, deps=[tpl], tok=True)
                E("vector", "tensor_copy", gamma[:, 512 * half:512 * (half + 1)],
                  psb[0][:], deps=[tm], tok=True)
            p.dma("sync", 7, [(masks, maskd_d[:, :])])

            for h in range(NH):
                bd = arA[:, 11264:12032]
                bdq = bd[:, 0:256]
                bdk = bd[:, 256:512]
                bdv = bd[:, 512:768]
                xfm = arA[:, 7168:11264]
                p.dma("sync", 0, [
                    (bdq.rearrange("p (f c) -> p f c", c=128),
                     bdq_d[2 * h:2 * h + 2, :, :].rearrange("f q c -> q f c")),
                    (bdk.rearrange("p (f c) -> p f c", c=128),
                     bdk_d[2 * h:2 * h + 2, :, :].rearrange("f q c -> q f c")),
                    (bdv.rearrange("p (f c) -> p f c", c=128),
                     bdv_d[2 * h:2 * h + 2, :, :].rearrange("f q c -> q f c"))])
                p.dma("sync", 1, [
                    (xfm.rearrange("p (f c) -> p f c", c=2048),
                     xfm_d[256 * h:256 * (h + 1), :]
                     .rearrange("(f q) c -> q f c", q=128))])
                qkv_ctr = [0]
                for proj, bdw in enumerate((bdq, bdk, bdv)):
                    if proj == 2:
                        drain(len(defq))
                    for fc in range(2):
                        for tt in range(4):
                            qb = psb[qkv_ctr[0] % 2]
                            qkv_ctr[0] += 1
                            tm = E("tensor", "matmul", qb[:],
                                   bdw[:, 128 * fc:128 * (fc + 1)],
                                   xfm[:, 2048 * fc + 512 * tt:
                                       2048 * fc + 512 * (tt + 1)],
                                   start=True, stop=True, tok=True)
                            lo_col = 2048 * fc + 512 * tt
                            if proj == 2:
                                E("vector", "tensor_copy",
                                  vfm[:, 512 * tt:512 * (tt + 1)], qb[:],
                                  deps=[tm], tok=True)
                            elif proj == 0:
                                dh_ = qh_b[:, lo_col:lo_col + 512]
                                dl_ = ql_b[:, lo_col:lo_col + 512]
                                E("vector", "tensor_copy", dh_, qb[:],
                                  deps=[tm], tok=True)
                                E("vector", "tensor_tensor", dl_, qb[:],
                                  dh_, ALU.subtract, tok=True)
                            else:
                                dh_ = kh_b[:, lo_col:lo_col + 512]
                                dl_ = kl_b[:, lo_col:lo_col + 512]
                                E("vector", "tensor_scalar", dh_, qb[:],
                                  0.0625, None, ALU.mult, deps=[tm], tok=True)
                                E("vector", "scalar_tensor_tensor", dl_,
                                  qb[:], 0.0625, dh_, ALU.mult, ALU.subtract,
                                  tok=True)
                        if proj == 2:
                            for grp in range(4):
                                bank = psb[2 + grp % 2]
                                ltp = None
                                for kk in range(4):
                                    jt = 4 * grp + kk
                                    ltp = E("tensor", "transpose",
                                            bank[:, 128 * kk:128 * (kk + 1)],
                                            vfm[:, 128 * jt:128 * (jt + 1)],
                                            identt[:], tok=(kk == 3))
                                dsth = vtmh.rearrange("p (j c) -> p j c", c=257) \
                                    [:, 4 * grp:4 * grp + 4, 128 * fc:128 * (fc + 1)]
                                dstl = vtml.rearrange("p (j c) -> p j c", c=257) \
                                    [:, 4 * grp:4 * grp + 4, 128 * fc:128 * (fc + 1)]
                                E("vector", "tensor_copy", dsth,
                                  bank[:].rearrange("p (k c) -> p k c", c=128),
                                  deps=[ltp], tok=True)
                                E("vector", "tensor_tensor", dstl,
                                  bank[:].rearrange("p (k c) -> p k c", c=128),
                                  dsth, ALU.subtract, tok=True)
                E("vector", "memset",
                  vtmh.rearrange("p (j c) -> p j c", c=257)[:, :, 256:257], 1.0,
                  tok=True)
                E("vector", "memset",
                  vtml.rearrange("p (j c) -> p j c", c=257)[:, :, 256:257], 0.0,
                  tok=True)
                qh3 = qh_b.rearrange("p (f t) -> p f t", t=2048)
                ql3 = ql_b.rearrange("p (f t) -> p f t", t=2048)
                kh3 = kh_b.rearrange("p (f t) -> p f t", t=2048)
                kl3 = kl_b.rearrange("p (f t) -> p f t", t=2048)

                def emit_ST(it, jc):
                    stb = psb[6 + (jc % 2)]
                    first = True
                    lmm = None
                    for fc in range(2):
                        lh = kh3[:, fc, 128 * jc:128 * (jc + 1)]
                        ll = kl3[:, fc, 128 * jc:128 * (jc + 1)]
                        rh = qh3[:, fc, 512 * it:512 * (it + 1)]
                        rl = ql3[:, fc, 512 * it:512 * (it + 1)]
                        E("tensor", "matmul", stb[:], lh, rh,
                          start=first, stop=False)
                        E("tensor", "matmul", stb[:], lh, rl,
                          start=False, stop=False)
                        lmm = E("tensor", "matmul", stb[:], ll, rh,
                                start=False, stop=(fc == 1), tok=(fc == 1))
                        first = False
                    return stb, lmm

                def emit_D_CT(it, jc, stb, lmm):
                    dk = jc - 4 * it
                    esrc = mm_slots[dk] if dk >= 0 else arA[:, 5632:6144]
                    Dsb = dsb_s[jc % 2]
                    E("scalar", "activation", Dsb, esrc, ACT.Exp,
                      scale=-1.0, bias=gcols[:, 4 * jc + h:4 * jc + h + 1],
                      tok=True)
                    E("vector", "tensor_tensor", Dsb, stb[:], Dsb,
                      ALU.mult, deps=[lmm], tok=True)
                    cth = cth_b[jc % 2]
                    ctl = ctl_b[jc % 2]
                    E("gpsimd", "tensor_copy", cth, Dsb, tok=True)
                    E("vector", "tensor_tensor", ctl, Dsb, cth,
                      ALU.subtract, tok=True)
                    return cth, ctl

                def emit_CV(it, jc, cth, ctl):
                    vh = vtmh.rearrange("p (j c) -> p j c", c=257)[:, jc, :]
                    vl = vtml.rearrange("p (j c) -> p j c", c=257)[:, jc, :]
                    for i1 in range(4):
                        if jc > 4 * it + i1:
                            continue
                        cvb = psb[2 + i1]
                        first = (jc == 0)
                        last = (jc == 4 * it + i1)
                        E("tensor", "matmul", cvb[:, 0:257],
                          cth[:, 128 * i1:128 * (i1 + 1)], vh,
                          start=first, stop=False)
                        E("tensor", "matmul", cvb[:, 0:257],
                          cth[:, 128 * i1:128 * (i1 + 1)], vl,
                          start=False, stop=False)
                        E("tensor", "matmul", cvb[:, 0:257],
                          ctl[:, 128 * i1:128 * (i1 + 1)], vh,
                          start=False, stop=last, tok=True)

                def emit_phase1(it):
                    for i1 in range(4):
                        cvb = psb[2 + i1]
                        i16 = 4 * it + i1
                        c0, c1 = 60 + 2 * i1, 61 + 2 * i1
                        E("vector", "reduce_max", cA(c0),
                          cvb[:, 256:257], axis=AX.X,
                          apply_absolute_value=True, tok=True)
                        E("vector", "tensor_tensor", cA(c0), cA(c0),
                          gcols[:, 64 + 4 * i16 + h:64 + 4 * i16 + h + 1],
                          ALU.max, tok=True)
                        E("vector", "tensor_scalar", cA(c0), cA(c0), 1e-8,
                          None, ALU.add, tok=True)
                        E("vector", "reciprocal", cA(c1), cA(c0), tok=True)
                        E("vector", "tensor_scalar", htm4[i1], cvb[:, 0:256],
                          cA(c1), None, ALU.mult, tok=True)

                def defer_mhln(it):
                    for i1 in range(4):
                        c0, c1 = 60 + 2 * i1, 61 + 2 * i1
                        hh = htm4[i1]
                        defq.extend([
                            (lambda hh=hh, c0=c0: E("vector", "reduce_sum",
                             cA(c0), hh, axis=AX.X, tok=True)),
                            (lambda c0=c0: E("vector", "tensor_scalar",
                             cA(c0), cA(c0), -1.0 / DH, None, ALU.mult,
                             tok=True)),
                            (lambda hh=hh, c0=c0: E("vector", "tensor_scalar",
                             hh, hh, cA(c0), None, ALU.add, tok=True)),
                            (lambda hh=hh: E("vector", "tensor_tensor",
                             junk, hh, hh, ALU.mult, tok=True)),
                            (lambda c1=c1: E("vector", "reduce_sum", cA(c1),
                             junk, axis=AX.X, tok=True)),
                        ])
                    def sqrt4():
                        for i1 in range(4):
                            E("scalar", "activation", cA(61 + 2 * i1),
                              cA(61 + 2 * i1), ACT.Sqrt, scale=1.0 / DH,
                              bias=cA(79), tok=True)
                    defq.append(sqrt4)
                    for i1 in range(4):
                        i16 = 4 * it + i1
                        c1 = 61 + 2 * i1
                        hh = htm4[i1]
                        xsl = xres[:, 1024 * i16 + 256 * h:
                                   1024 * i16 + 256 * (h + 1)]
                        defq.extend([
                            (lambda c1=c1: E("vector", "reciprocal", cA(c1),
                             cA(c1), tok=True)),
                            (lambda hh=hh, c1=c1: E("vector",
                             "scalar_tensor_tensor", hh, hh, cA(c1),
                             gamma[:, 256 * h:256 * (h + 1)], ALU.mult,
                             ALU.mult, tok=True)),
                            (lambda hh=hh, xsl=xsl: E("vector",
                             "tensor_tensor", xsl, xsl, hh, ALU.add,
                             tok=True)),
                        ])

                def defer_masks(it):
                    for dk in range(4):
                        defq.append(lambda dk=dk: E(
                            "vector", "tensor_tensor", mm_slots[dk],
                            arA[:, 5632:6144],
                            masks[:, 512 * dk:512 * (dk + 1)], ALU.add,
                            tok=True))

                for it in range(4):
                    tm = E("tensor", "matmul", psb[0][:],
                           sel4t[:, 128 * h:128 * (h + 1)],
                           M_sb[:, 512 * it:512 * (it + 1)],
                           start=True, stop=True, tok=True)
                    E("vector", "tensor_copy", arA[:, 5632:6144], psb[0][:],
                      deps=[tm], tok=True)
                    if it == 0:
                        drain(len(defq))
                        defer_masks(it)
                        drain(4)
                    else:
                        drain(len(defq))
                        emit_phase1(it - 1)
                        defer_masks(it)
                        defer_mhln(it - 1)
                    njc = 4 * it + 4
                    stA = emit_ST(it, 0)
                    pend = (0, stA)
                    for jc in range(1, njc + 1):
                        pj, (stb, lmm) = pend
                        cth, ctl = emit_D_CT(it, pj, stb, lmm)
                        drain(3)
                        if jc <= njc - 1:
                            pend = (jc, emit_ST(it, jc))
                        emit_CV(it, pj, cth, ctl)
                    if it == 3:
                        drain(len(defq))
                        emit_phase1(3)
                        defer_mhln(3)
                        if h == NH - 1:
                            drain(len(defq))
            p.barrier()

        if N_PHASES >= 2:
            ffn_phase()
        if N_PHASES >= 3:
            attn_phase()
        if N_PHASES >= 4:
            ffn_phase()

        p.barrier()
        p.dma("gpsimd", 0, [(out_d[128 * c:128 * (c + 1), :],
                             xres[:, 1024 * c:1024 * (c + 1)])
                            for c in range(NTC)])
        E("gpsimd", "nop", deps=p.qtot("gpsimd"))

        p.finalize()
        block = es.enter_context(nc.Block())

        @block.sync
        def _(e):
            for fn in p.ops["sync"]:
                fn(e)

        @block.vector
        def _(e):
            for fn in p.ops["vector"]:
                fn(e)

        @block.scalar
        def _(e):
            for fn in p.ops["scalar"]:
                fn(e)

        @block.tensor
        def _(e):
            for fn in p.ops["tensor"]:
                fn(e)

        @block.gpsimd
        def _(e):
            for fn in p.ops["gpsimd"]:
                fn(e)

    return nc



def _bd_chunks(w):
    bd = np.zeros((NFC, 128, 128), dtype=np.float32)
    for c in range(NFC):
        for g in range(32):
            blk = w[32 * c + g]
            bd[c, 4 * g:4 * g + 4, 4 * g:4 * g + 4] = blk.T
    return bd


def _host_consts():
    ident = np.eye(128, dtype=np.float32)
    maskd = np.zeros((128, 2048), dtype=np.float32)
    pp = np.arange(128)[:, None]
    cc = np.arange(512)[None, :]
    for k in range(4):
        maskd[:, 512 * k:512 * (k + 1)] = np.where(128 * k + pp > cc, 1e30, 0.0)
    onesk = np.ones((1, 128), dtype=np.float32)
    sel4 = np.zeros((4, 512), dtype=np.float32)
    for h in range(4):
        sel4[h, 128 * h:128 * (h + 1)] = 1.0
    return ident, maskd, onesk, sel4


def kernel(L0_attn_x, L0_attn_wq, L0_attn_wk, L0_attn_wv, L0_attn_wi,
           L0_attn_bi, L0_attn_wf, L0_attn_bf, L0_attn_ln_w,
           L0_ffn_w_gate, L0_ffn_w_up, L0_ffn_w_down):
    x = np.ascontiguousarray(np.asarray(L0_attn_x, dtype=np.float32))
    ident, maskd, onesk, sel4 = _host_consts()
    common = {
        "wup": np.ascontiguousarray(np.asarray(L0_ffn_w_up, np.float32)),
        "wgate": np.ascontiguousarray(np.asarray(L0_ffn_w_gate, np.float32)),
        "wdown": np.ascontiguousarray(np.asarray(L0_ffn_w_down, np.float32)),
        "bdq": _bd_chunks(np.asarray(L0_attn_wq, np.float32)),
        "bdk": _bd_chunks(np.asarray(L0_attn_wk, np.float32)),
        "bdv": _bd_chunks(np.asarray(L0_attn_wv, np.float32)),
        "wit": np.ascontiguousarray(np.asarray(L0_attn_wi, np.float32).T),
        "wft": np.ascontiguousarray(np.asarray(L0_attn_wf, np.float32).T),
        "bi": np.ascontiguousarray(np.asarray(L0_attn_bi, np.float32).reshape(4, 1)),
        "bf4": np.ascontiguousarray(np.asarray(L0_attn_bf, np.float32).reshape(4, 1)),
        "lnw": np.ascontiguousarray(np.asarray(L0_attn_ln_w, np.float32).reshape(1, D)),
        "ident": ident, "maskd": maskd, "onesk": onesk, "sel4": sel4,
    }
    nc = build_program()
    in_maps = [dict(common, x=np.ascontiguousarray(x[c])) for c in range(8)]
    res = run_bass_kernel_spmd(nc, in_maps, core_ids=list(range(8)))
    out = np.stack([res.results[c]["out"] for c in range(8)], axis=0)
    return out.astype(np.float32)


# version 13
# speedup vs baseline: 1.0438x; 1.0184x over previous
import numpy as np
from contextlib import ExitStack

import concourse.bass as bass
import concourse.mybir as mybir
from concourse.bass_utils import run_bass_kernel_spmd

F32 = mybir.dt.float32
F32R = mybir.dt.float32r
BF16 = mybir.dt.bfloat16
I32 = mybir.dt.int32
ALU = mybir.AluOpType
ACT = mybir.ActivationFunctionType
AX = mybir.AxisListType

ENGINES = ("sync", "vector", "scalar", "tensor", "gpsimd")
RG = [[0, 1, 2, 3, 4, 5, 6, 7]]

S, D, NH, DH, I3 = 2048, 1024, 4, 256, 3072
NTC = S // 128
NFC = D // 128
NIC = I3 // 128

N_PHASES = 99
FFN_STAGES = 99
NOG_LIM = 99


class Prog:

    N_DMA = {"sync": 8, "gpsimd": 8, "scalar": 2}

    def __init__(self, nc, es):
        self.nc = nc
        self.sems = {}
        for name in ("sv", "ss", "st_", "sg", "sy"):
            self.sems[name] = [es.enter_context(nc.semaphore(name)), 0]
        self.own = {"vector": "sv", "scalar": "ss", "tensor": "st_",
                    "gpsimd": "sg", "sync": "sy"}
        for eng, n in self.N_DMA.items():
            for i in range(n):
                name = "q_%s%d" % (eng, i)
                self.sems[name] = [es.enter_context(nc.semaphore(name)), 0]
        self.N_TOK = 8
        for e in ENGINES:
            for i in range(self.N_TOK):
                nm = "t_%s%d" % (e, i)
                self.sems[nm] = [es.enter_context(nc.semaphore(nm)), 0]
        self.tok_rr = {e: 0 for e in ENGINES}
        self.ops = {e: [] for e in ENGINES}
        self.guar = {e: {} for e in ENGINES}
        self.lastw = {}
        self.lastr = {}
        self.rec = []

    @staticmethod
    def _is_ap(a):
        return hasattr(a, "tensor") and hasattr(a, "offset") and hasattr(a, "ap")

    @staticmethod
    def _rng(ap):
        t = ap.tensor
        import concourse.mybir as _mb
        esz = _mb.dt.size(ap.dtype)
        shape = list(t.shape)
        row = shape[-1] if len(shape) > 1 else shape[0]
        tesz = _mb.dt.size(t.dtype)
        row_bytes = row * tesz
        off_b = ap.offset * esz
        col_lo = off_b % row_bytes if row_bytes else 0
        span = 0
        for step, count in list(ap.ap):
            sb = abs(int(step)) * esz
            if row_bytes and sb % row_bytes == 0 and sb != 0:
                continue
            span += (int(count) - 1) * sb
        return (t.name, col_lo, col_lo + span + esz)

    _WRITES = {
        "tensor_scalar": [0], "tensor_tensor": [0], "tensor_copy": [0],
        "tensor_sub": [0], "tensor_add": [0], "tensor_mul": [0],
        "tensor_max": [0], "scalar_tensor_tensor": [0],
        "tensor_tensor_scan": [0], "reduce_max": [0], "reduce_sum": [0],
        "tensor_reduce": [0], "reciprocal": [0], "memset": [0],
        "activation": [0], "transpose": [0], "matmul": [0], "iota": [0],
    }

    def _autodeps(self, eng, meth, args, kw, tokval_next):
        reads, writes = [], []
        wpos = self._WRITES.get(meth, [])
        for i, a in enumerate(args):
            if self._is_ap(a):
                (writes if i in wpos else reads).append(a)
        for k, a in kw.items():
            if self._is_ap(a):
                if k in ("out", "accum_out"):
                    writes.append(a)
                else:
                    reads.append(a)
        deps = []
        psum_pe = (eng == "tensor" and meth == "matmul"
                   and kw.get("start") is False)
        for a in reads:
            r = self._rng(a)
            for (lo, hi, tk, teng) in self.lastw.get(r[0], []):
                if hi > r[1] and r[2] > lo:
                    if psum_pe and teng == "tensor" and r[0].startswith("pb"):
                        continue
                    deps.append(tk)
        for a in writes:
            r = self._rng(a)
            for m in (self.lastw, self.lastr):
                for (lo, hi, tk, teng) in m.get(r[0], []):
                    if hi > r[1] and r[2] > lo:
                        if psum_pe and teng == "tensor" and r[0].startswith("pb"):
                            continue
                        deps.append(tk)
        return deps, reads, writes

    def _note(self, eng, reads, writes, tokval):
        for a in writes:
            nm, lo, hi = self._rng(a)
            lst = self.lastw.setdefault(nm, [])
            lst[:] = [e for e in lst if not (lo <= e[0] and e[1] <= hi)]
            lst.append((lo, hi, tokval, eng))
            rl = self.lastr.setdefault(nm, [])
            rl[:] = [e for e in rl if not (lo <= e[0] and e[1] <= hi)]
        for a in reads:
            nm, lo, hi = self._rng(a)
            self.lastr.setdefault(nm, []).append((lo, hi, tokval, eng))

    @staticmethod
    def _flat(deps):
        out = []
        for d in deps:
            if d is None:
                continue
            if isinstance(d, list):
                out.extend(x for x in d if x is not None)
            else:
                out.append(d)
        return out

    def emit(self, eng, meth, *args, deps=(), tok=False, _inc=None,
             note_tok=None, _skip=None, **kw):
        if eng == "tensor" and meth == "nop":
            ps1, id1 = self.tensor_nop_args
            meth = "transpose"
            args = (ps1, id1, id1)
        adeps, rds, wrs = self._autodeps(eng, meth, args, kw, None)
        oid = len(self.rec)
        all_deps = []
        for d in self._flat(list(deps) + adeps):
            if _skip is not None and d == _skip:
                continue
            all_deps.append(d)
        qtok = None
        if _inc is not None:
            q = _inc
            self.sems[q][1] += 16
            qtok = (q, self.sems[q][1])
        self.rec.append({"eng": eng, "meth": meth, "args": args, "kw": kw,
                         "deps": all_deps, "qtok": qtok})
        tokval = oid if qtok is None else (note_tok if note_tok is not None
                                           else qtok)
        self._note(eng, rds, wrs, oid if qtok is None else tokval)
        return tokval

    def finalize(self):
        used = set()
        for r in self.rec:
            for d in r["deps"]:
                if isinstance(d, int):
                    used.add(d)
        assign = {}
        for oid, r in enumerate(self.rec):
            if r["qtok"] is not None:
                assign[oid] = ("__dma__", r["qtok"])
                continue
            if oid in used:
                eng = r["eng"]
                q = "t_%s%d" % (eng, self.tok_rr[eng] % self.N_TOK)
                self.tok_rr[eng] += 1
                drain = (q, self.sems[q][1])
                self.sems[q][1] += 1
                assign[oid] = (q, self.sems[q][1])
                r["drain"] = drain
        for oid, r in enumerate(self.rec):
            eng = r["eng"]
            waits = []
            wl = list(r["deps"])
            if "drain" in r:
                wl.append(r["drain"])
            for d in wl:
                if isinstance(d, int):
                    a = assign[d]
                    d = a[1] if a[0] == "__dma__" else a
                sname, val = d
                if self.guar[eng].get(sname, 0) >= val:
                    continue
                self.guar[eng][sname] = val
                waits.append((self.sems[sname][0], val))
            inc = None
            if r["qtok"] is not None:
                inc = (self.sems[r["qtok"][0]][0], 16)
            elif oid in assign:
                a = assign[oid]
                inc = (self.sems[a[0]][0], 1)

            def run(e, meth=r["meth"], args=r["args"], kw=r["kw"],
                    waits=waits, inc=inc):
                for h, v in waits:
                    e.wait_ge(h, v)
                ins = getattr(e, meth)(*args, **kw)
                if inc is not None:
                    ins.then_inc(*inc)
            self.ops[eng].append(run)

    def dma(self, eng, slot, pairs, deps=()):
        q = "q_%s%d" % (eng, slot % self.N_DMA[eng])
        drain = (q, self.sems[q][1])
        final = (q, self.sems[q][1] + 16 * len(pairs))
        for j, (o, i_) in enumerate(pairs):
            dd = list(deps) + [drain] if j == 0 else ()
            self.emit(eng, "dma_start", out=o, in_=i_, deps=dd, _inc=q,
                      note_tok=final, _skip=final)
        return final

    def qtot(self, eng):
        return [("q_%s%d" % (eng, i), self.sems["q_%s%d" % (eng, i)][1])
                for i in range(self.N_DMA[eng])]

    def barrier(self):
        toks = {}
        for e in ENGINES:
            toks[e] = self.emit(e, "nop")
        qvals = []
        for eng in self.N_DMA:
            qvals += self.qtot(eng)
        for e in ENGINES:
            deps = [toks[e2] for e2 in ENGINES if e2 != e] + qvals
            self.emit(e, "nop", deps=deps)


def build_program():
    nc = bass.Bass("TRN2", target_bir_lowering=False)

    def din(name, shape, dt=F32):
        return nc.dram_tensor(name, shape, dt, kind="ExternalInput").ap()

    def dint(name, shape, dt=F32):
        return nc.dram_tensor(name, shape, dt, kind="Internal").ap()

    x_d = din("x", [S, D])
    wup_d = din("wup", [I3, D])
    wgate_d = din("wgate", [I3, D])
    wdown_d = din("wdown", [D, I3])
    bdq_d = din("bdq", [NFC, 128, 128])
    bdk_d = din("bdk", [NFC, 128, 128])
    bdv_d = din("bdv", [NFC, 128, 128])
    wit_d = din("wit", [I3, 4])
    wft_d = din("wft", [I3, 4])
    bi_d = din("bi", [4, 1])
    bf_d = din("bf4", [4, 1])
    lnw_d = din("lnw", [1, D])
    ident_d = din("ident", [128, 128])
    maskd_d = din("maskd", [128, 2048])
    onesk_d = din("onesk", [1, 128])
    sel4_d = din("sel4", [4, 512])
    out_d = nc.dram_tensor("out", [S, D], F32, kind="ExternalOutput").ap()

    pu_d = dint("pu", [I3, S])
    pg_d = dint("pg", [I3, S])
    m1_d = dint("m1", [I3, S])
    op_d = dint("op", [I3, S])
    nohi_d = dint("nohi", [I3, S], BF16)
    nolo_d = dint("nolo", [I3, S], BF16)
    wuphi_d = dint("wuphi", [D, I3], BF16)
    wuplo_d = dint("wuplo", [D, I3], BF16)
    wghi_d = dint("wghi", [D, I3], BF16)
    wglo_d = dint("wglo", [D, I3], BF16)
    wdhi_d = dint("wdhi", [I3, D], BF16)
    wdlo_d = dint("wdlo", [I3, D], BF16)
    xfm_d = dint("xfmsp", [D, S])
    ccin_d = [dint("ccin%d" % k, [1, 2]) for k in range(8)]
    ccout_d = [dint("ccout%d" % k, [1, 2]) for k in range(8)]

    es = ExitStack()

    def sb(name, shape, dt=F32):
        return es.enter_context(nc.sbuf_tensor(name, shape, dt))

    with es:
        xres = sb("xres", [128, NTC * 1024])
        arA = sb("arA", [128, 12288])
        arB = sb("arB", [128, 6144])
        nhi = sb("nhi", [128, 16384], BF16)
        nlo = sb("nlo", [128, 16384], BF16)
        colA = sb("colA", [128, 80])
        gcols = sb("gcols", [128, 192])
        identt = sb("identt", [128, 128])
        onest = sb("onest", [1, 128])
        sel4t = sb("sel4t", [4, 512])
        scrow = sb("scrow", [1, 64])
        ccio = sb("ccio", [1, 4])
        psb = [es.enter_context(nc.psum_tensor("pb%d" % i, [128, 512], F32))
               for i in range(8)]

        arA_bf = arA[:].bitcast(BF16)
        arB_i32 = arB[:].bitcast(I32)
        nhi_f = nhi[:].bitcast(F32)
        nlo_f = nlo[:].bitcast(F32)

        p = Prog(nc, es)
        p.tensor_nop_args = (psb[7][0:1, 0:1], identt[0:1, 0:1])
        E = p.emit
        cc_ev = [0]

        def cA(i):
            return colA[:, i:i + 1]

        def srow(i, n=1):
            return scrow[0:1, i:i + n]

        p.dma("sync", 0, [(xres[:, 1024 * c:1024 * (c + 1)],
                           x_d[128 * c:128 * (c + 1), :]) for c in range(NTC)])
        p.dma("sync", 1, [(identt[:], ident_d[:, :]), (onest[:], onesk_d[:, :]),
                          (sel4t[:], sel4_d[:, :])])
        E("vector", "memset", colA[:, 79:80], 1e-5, tok=True)
        E("vector", "memset", colA[:, 78:79], 1.0, tok=True)
        p.barrier()

        wp_state = {"load": [None, None], "done": [None, None],
                    "stg": [None, None], "t5": None}

        def w_prep(w_nat, hi_dst, lo_dst, O, I, wn_bufs, wdq_bufs,
                   bmb, stgs, banks=(4, 5), dslot=(0, 1)):
            noch = O // 128
            nb = I // 32
            stg_hi = [stgs[0], stgs[1]]
            stg_lo = [stgs[2], stgs[3]]
            tok_load = wp_state["load"]
            tok_done = wp_state["done"]
            tok_stg = wp_state["stg"]
            bm = bmb[:, 0:nb]
            sA = bmb[:, 128:128 + nb]
            rs = bmb[:, 256:256 + nb]
            for och in range(noch):
                b = och % 2
                wnat = wn_bufs[b]
                wdq = wdq_bufs[b]
                dl = [tok_done[b], wp_state.get("t5_%d" % b)]
                if och < 2:
                    dl += [tok_done[0], tok_done[1], wp_state.get("t5_0"),
                           wp_state.get("t5_1"), tok_stg[0], tok_stg[1]]
                tok_load[b] = p.dma("sync", b, [(wnat,
                                w_nat[128 * och:128 * (och + 1), :])],
                                deps=dl)
                w3 = wnat.rearrange("p (g e) -> p g e", e=32)
                t1 = E("vector", "reduce_max", bm.rearrange("p (g o) -> p g o", o=1),
                       w3, axis=AX.X, apply_absolute_value=True,
                       deps=[tok_load[b]], tok=True)
                t2 = E("vector", "tensor_scalar", sA, bm, 1.0 / 7.0, 1e-12,
                       ALU.mult, ALU.max, deps=[t1], tok=True)
                t3 = E("vector", "reciprocal", rs, sA, deps=[t2], tok=True)
                q32 = wnat.bitcast(I32)
                t4 = E("vector", "tensor_tensor",
                       q32.rearrange("p (g e) -> p g e", e=32), w3,
                       rs.rearrange("p (g o) -> p g o", o=1).broadcast_to((128, nb, 32)),
                       ALU.mult, deps=[t3], tok=True)
                t5 = E("vector", "tensor_tensor",
                       wdq.rearrange("p (g e) -> p g e", e=32),
                       q32.rearrange("p (g e) -> p g e", e=32),
                       sA.rearrange("p (g o) -> p g o", o=1).broadcast_to((128, nb, 32)),
                       ALU.mult, deps=[t4, tok_done[0], tok_done[1]], tok=True)
                wp_state["t5_%d" % b] = t5
                last_tp = None
                for ts_ in range(I // 512):
                    bank = psb[banks[ts_ % 2]]
                    sb_ = ts_ % 2
                    for k in range(4):
                        last_tp = E("tensor", "transpose", bank[:, 128 * k:128 * (k + 1)],
                                    wdq[:, 512 * ts_ + 128 * k: 512 * ts_ + 128 * (k + 1)],
                                    identt[:],
                                    deps=[t5, tok_stg[sb_]] if k == 0 else [],
                                    tok=(k == 3))
                    th = E("vector", "tensor_copy", stg_hi[sb_], bank[:],
                           deps=[last_tp, tok_stg[sb_]], tok=True)
                    tl = E("vector", "tensor_sub", stg_lo[sb_], bank[:], stg_hi[sb_],
                           deps=[last_tp, th], tok=True)
                    if ts_ == I // 512 - 1:
                        tok_done[b] = tl
                    dst = slice(512 * ts_, 512 * (ts_ + 1))
                    hi_ap = hi_dst[dst, 128 * och:128 * (och + 1)] \
                        .rearrange("(k q) c -> q k c", q=128)
                    lo_ap = lo_dst[dst, 128 * och:128 * (och + 1)] \
                        .rearrange("(k q) c -> q k c", q=128)
                    tok_stg[sb_] = p.dma("gpsimd", dslot[sb_], [
                        (hi_ap, stg_hi[sb_].rearrange("p (k c) -> p k c", c=128)),
                        (lo_ap, stg_lo[sb_].rearrange("p (k c) -> p k c", c=128))],
                        deps=[th, tl])

        nhi_bf = nhi[:]
        nlo_bf = nlo[:]
        if N_PHASES >= 1:
            pass

        def allreduce_max2(v0_ap, v1_ap, tok_in):
            k = cc_ev[0]
            cc_ev[0] += 1
            tc0 = E("vector", "tensor_copy", ccio[0:1, 0:1], v0_ap,
                    deps=tok_in, tok=True)
            tc1 = E("vector", "tensor_copy", ccio[0:1, 1:2], v1_ap,
                    deps=tok_in, tok=True)
            td = E("gpsimd", "dma_start", out=ccin_d[k][:, :], in_=ccio[0:1, 0:2],
                   deps=[tc0, tc1], dma=True)
            tcc = E("gpsimd", "collective_compute", "AllReduce", ALU.max,
                    RG, [ccin_d[k][:, :]], [ccout_d[k][:, :]],
                    deps=[td], tok=True)
            tr = E("gpsimd", "dma_start", out=ccio[0:1, 2:4], in_=ccout_d[k][:, :],
                   deps=[tcc], dma=True)
            return tr

        def bcast_cols(src_row_ap, n, dst_col0, deps):
            tm = E("tensor", "matmul", psb[6][:, 0:n], onest[:], src_row_ap,
                   start=True, stop=True, deps=deps, tok=True)
            return E("scalar", "activation", colA[:, dst_col0:dst_col0 + n],
                     psb[6][:, 0:n], ACT.Identity, deps=[tm], tok=True)

        last_bcast = [None]

        def bcast_cols2(src_row_ap, n, dst_col0, deps):
            tm = E("tensor", "matmul", psb[6][:, 0:n], onest[:], src_row_ap,
                   start=True, stop=True, deps=list(deps) + [last_bcast[0]],
                   tok=True)
            tev = E("vector", "tensor_copy", colA[:, dst_col0:dst_col0 + n],
                    psb[6][:, 0:n], deps=[tm], tok=True)
            last_bcast[0] = tev
            return tev

        def allred2(tok_deps, ccio_deps):
            k = cc_ev[0]
            cc_ev[0] += 1
            td = p.dma("gpsimd", 6, [(ccin_d[k][:, :], ccio[0:1, 0:2])],
                       deps=list(tok_deps) + list(ccio_deps))
            tcc = E("gpsimd", "collective_compute", "AllReduce", ALU.max,
                    RG, [ccin_d[k][:, :]], [ccout_d[k][:, :]], deps=[td], tok=True)
            return p.dma("gpsimd", 6, [(ccio[0:1, 2:4], ccout_d[k][:, :])],
                         deps=[tcc])

        def ffn_phase(first):
            def xln_ap(c):
                if c < 12:
                    return arA[:, 1024 * c:1024 * (c + 1)]
                return arB[:, 1024 * (c - 12):1024 * (c - 11)]
            tok_xln = []
            tok_absm = []
            junk = arB[:, 4096:5120]
            for c in range(NTC):
                xc = xres[:, 1024 * c:1024 * (c + 1)]
                ts_ = E("vector", "reduce_sum", cA(48 + c), xc, axis=AX.X, tok=True)
                tn = E("scalar", "activation", cA(c), cA(48 + c), ACT.Identity,
                       scale=-1.0 / D, deps=[ts_], tok=True)
                tq = E("scalar", "activation", junk, xc, ACT.Square,
                       bias=cA(c), accum_out=cA(48 + c), deps=[tn], tok=True)
                tr0 = E("scalar", "activation", cA(48 + c), cA(48 + c), ACT.Sqrt,
                        scale=1.0 / D, bias=cA(79), deps=[tq], tok=True)
                tr = E("vector", "reciprocal", cA(16 + c), cA(48 + c),
                       deps=[tr0], tok=True)
                tx = E("vector", "tensor_scalar", xln_ap(c), xc, cA(c), cA(16 + c),
                       ALU.add, ALU.mult, deps=[tn, tr], tok=True)
                tok_xln.append(tx)
                ta = E("vector", "reduce_max", cA(32 + c), xln_ap(c), axis=AX.X,
                       apply_absolute_value=True, deps=[tx], tok=True)
                tok_absm.append(ta)
            if first:
                w_prep(wdown_d, wdhi_d, wdlo_d, D, I3,
                       wn_bufs=[nhi_f[:, 0:3072], nhi_f[:, 3072:6144]],
                       wdq_bufs=[nlo_f[:, 0:3072], nlo_f[:, 3072:6144]],
                       bmb=nlo_f[:, 6144:6528],
                       stgs=[nlo_bf[:, 13056:13568], nlo_bf[:, 13568:14080],
                             nlo_bf[:, 14080:14592], nlo_bf[:, 14592:15104]],
                       banks=(4, 5), dslot=(4, 5))
            rowb = arB[0:1, 4096:6144]
            tdm = p.dma("gpsimd", 7, [(rowb, colA[:, 32:48])], deps=tok_absm)
            trm = E("vector", "reduce_max", ccio[0:1, 0:1], rowb, axis=AX.X,
                    deps=[tdm], tok=True)
            tz = E("vector", "memset", ccio[0:1, 1:2], 0.0, tok=True)
            tr1 = allred2([trm, tz], [])
            ta1 = E("vector", "reciprocal", srow(0), ccio[0:1, 2:3], deps=[tr1], tok=True)
            ta2 = E("vector", "tensor_scalar", srow(1), srow(0), 32767.0, None,
                    ALU.mult, deps=[ta1], tok=True)
            E("vector", "tensor_scalar", srow(2), ccio[0:1, 2:3],
              1.0 / 32767.0, None, ALU.mult, deps=[tr1], tok=True)
            tbc1 = bcast_cols2(srow(1), 1, 64, [ta2])
            if FFN_STAGES <= 1:
                p.barrier()
                return

            ni32 = arB_i32[:, 4096:5120]
            nf32 = arB[:, 5120:6144]
            tok_ev = {}
            tp_done = [None, None]
            nhi3 = nhi[:].rearrange("p (f t) -> p f t", t=2048)
            nlo3 = nlo[:].rearrange("p (f t) -> p f t", t=2048)
            for c in range(NTC):
                t1 = E("scalar", "activation", ni32, xln_ap(c), ACT.Identity,
                       scale=cA(64), deps=[tok_xln[c], tbc1, tok_ev.get(c - 1)],
                       tok=True)
                t2 = E("scalar", "activation", nf32, ni32, ACT.Identity,
                       deps=[t1], tok=True)
                for half in range(2):
                    bank = psb[4 + half]
                    ltp = None
                    for kk in range(4):
                        f = 4 * half + kk
                        ltp = E("tensor", "transpose", bank[:, 128 * kk:128 * (kk + 1)],
                                nf32[:, 128 * f:128 * (f + 1)], identt[:],
                                deps=[t2, tp_done[half]] if kk == 0 else [],
                                tok=(kk == 3))
                    dsth = nhi3[:, 4 * half:4 * half + 4, 128 * c:128 * (c + 1)]
                    dstl = nlo3[:, 4 * half:4 * half + 4, 128 * c:128 * (c + 1)]
                    th = E("vector", "tensor_copy",
                           dsth, bank[:].rearrange("p (k c) -> p k c", c=128),
                           deps=[ltp], tok=True)
                    tl = E("vector", "tensor_tensor",
                           dstl, bank[:].rearrange("p (k c) -> p k c", c=128),
                           dsth, ALU.subtract, deps=[ltp, th], tok=True)
                    tp_done[half] = tl
                    if half == 1:
                        tok_ev[c] = tl
            tok_n = [tp_done[0], tp_done[1], tok_ev[NTC - 1]]
            if FFN_STAGES <= 2:
                p.barrier()
                return

            tmu = E("vector", "memset", cA(71), 0.0, deps=tok_absm, tok=True)
            tmg = E("vector", "memset", cA(72), 0.0, tok=True)
            tok_wload = {}
            tok_mmdone = {}
            tok_evac = {}
            tok_stgout = [None, None, None, None]
            srcs = [(wuphi_d, 0), (wuplo_d, 1), (wghi_d, 2), (wglo_d, 3)]
            NOG = min(12, NOG_LIM)
            stg_ctr = [0]
            tcp = None
            def wprep_unit(nat_d, m, och, b):
                o2 = och % 2
                wnat = arA[:, 10240:11264]
                wdq = arA[:, 11264:12288]
                tokL = p.dma("sync", 6 + (och % 2), [(wnat,
                             nat_d[128 * och:128 * (och + 1), :])])
                w3 = wnat.rearrange("p (g e) -> p g e", e=32)
                bm = arB[:, 5120:5152]
                sA = arB[:, 5184:5216]
                rs = arB[:, 5248:5280]
                E("vector", "reduce_max", bm.rearrange("p (g o) -> p g o", o=1),
                  w3, axis=AX.X, apply_absolute_value=True, deps=[tokL],
                  tok=True)
                E("vector", "tensor_scalar", sA, bm, 1.0 / 7.0, 1e-12,
                  ALU.mult, ALU.max, tok=True)
                E("vector", "reciprocal", rs, sA, tok=True)
                q32 = wnat.bitcast(I32)
                E("vector", "tensor_tensor",
                  q32.rearrange("p (g e) -> p g e", e=32), w3,
                  rs.rearrange("p (g o) -> p g o", o=1).broadcast_to((128, 32, 32)),
                  ALU.mult, tok=True)
                E("vector", "tensor_tensor",
                  wdq.rearrange("p (g e) -> p g e", e=32),
                  q32.rearrange("p (g e) -> p g e", e=32),
                  sA.rearrange("p (g o) -> p g o", o=1).broadcast_to((128, 32, 32)),
                  ALU.mult, tok=True)
                whi3 = arA_bf[:, 8192 * b + 2048 * (2 * m):
                              8192 * b + 2048 * (2 * m + 1)] \
                    .rearrange("p (f c) -> p f c", c=256)
                wlo3 = arA_bf[:, 8192 * b + 2048 * (2 * m + 1):
                              8192 * b + 2048 * (2 * m + 2)] \
                    .rearrange("p (f c) -> p f c", c=256)
                toks = []
                for ts_ in range(2):
                    bank = psb[4 + ts_]
                    ltp = None
                    for kk in range(4):
                        ltp = E("tensor", "transpose",
                                bank[:, 128 * kk:128 * (kk + 1)],
                                wdq[:, 512 * ts_ + 128 * kk:
                                    512 * ts_ + 128 * (kk + 1)], identt[:],
                                tok=(kk == 3))
                    dsth = whi3[:, 4 * ts_:4 * ts_ + 4, 128 * o2:128 * (o2 + 1)]
                    dstl = wlo3[:, 4 * ts_:4 * ts_ + 4, 128 * o2:128 * (o2 + 1)]
                    th = E("vector", "tensor_copy", dsth,
                           bank[:].rearrange("p (k c) -> p k c", c=128),
                           deps=[ltp], tok=True)
                    tl = E("vector", "tensor_tensor", dstl,
                           bank[:].rearrange("p (k c) -> p k c", c=128),
                           dsth, ALU.subtract, deps=[ltp], tok=True)
                    toks += [th, tl]
                return toks

            for og in range(NOG):
                b = og % 2
                dep_reuse = [tok_mmdone.get(og - 2)] + (tok_n if og < 2 else [])
                if first:
                    toks = []
                    for m, nat_d in enumerate((wup_d, wgate_d)):
                        for o2 in range(2):
                            toks += wprep_unit(nat_d, m, 2 * og + o2, b)
                    for m, (hi_d, lo_d) in enumerate(((wuphi_d, wuplo_d),
                                                      (wghi_d, wglo_d))):
                        for s_, dst_d in ((2 * m, hi_d), (2 * m + 1, lo_d)):
                            sap = dst_d[:, 256 * og:256 * (og + 1)] \
                                .rearrange("(f q) c -> q f c", q=128)
                            buf = arA_bf[:, 8192 * b + 2048 * s_:
                                         8192 * b + 2048 * (s_ + 1)] \
                                .rearrange("p (f c) -> p f c", c=256)
                            p.dma("gpsimd", 6, [(sap, buf)])
                    tok_wload[og] = toks
                else:
                    wpairs = []
                    for (src_, s_) in srcs:
                        dst = arA_bf[:, 8192 * b + 2048 * s_:
                                     8192 * b + 2048 * (s_ + 1)] \
                            .rearrange("p (f c) -> p f c", c=256)
                        sap = src_[:, 256 * og:256 * (og + 1)] \
                            .rearrange("(f q) c -> q f c", q=128)
                        wpairs.append((dst, sap))
                    tok_wload[og] = p.dma("sync", 2 + b, wpairs, deps=dep_reuse)
                last_mm = None
                for mat in range(2):
                    wh3 = arA_bf[:, 8192 * b + 2048 * (2 * mat):
                                 8192 * b + 2048 * (2 * mat + 1)] \
                        .rearrange("p (f c) -> p f c", c=256)
                    wl3 = arA_bf[:, 8192 * b + 2048 * (2 * mat + 1):
                                 8192 * b + 2048 * (2 * mat + 2)] \
                        .rearrange("p (f c) -> p f c", c=256)
                    for o2 in range(2):
                        och = 2 * og + o2
                        for tt in range(4):
                            bank_i = tt % 4
                            bank = psb[bank_i]
                            sp = stg_ctr[0] % 4
                            stg_ctr[0] += 1
                            stg = arA[:, 8192 + 512 * sp:8192 + 512 * (sp + 1)]
                            first_ = True
                            for f in range(NFC):
                                whi = wh3[:, f, 128 * o2:128 * (o2 + 1)]
                                wlo = wl3[:, f, 128 * o2:128 * (o2 + 1)]
                                rhs_h = nhi3[:, f, 512 * tt:512 * (tt + 1)]
                                rhs_l = nlo3[:, f, 512 * tt:512 * (tt + 1)]
                                dd = ([tok_wload[og],
                                       tok_evac.get(bank_i)] + tok_n) \
                                    if first_ else []
                                E("tensor", "matmul", bank[:], whi, rhs_h,
                                  start=first_, stop=False, deps=dd)
                                E("tensor", "matmul", bank[:], whi, rhs_l,
                                  start=False, stop=False)
                                last_mm = E("tensor", "matmul", bank[:], wlo, rhs_h,
                                            start=False, stop=(f == NFC - 1),
                                            tok=(f == NFC - 1))
                                first_ = False
                            tr_ = E("vector", "reduce_max", cA(73), bank[:],
                                    axis=AX.X, apply_absolute_value=True,
                                    deps=[last_mm], tok=True)
                            E("vector", "tensor_tensor", cA(71 + mat),
                              cA(71 + mat), cA(73), ALU.max,
                              deps=[tr_, tmu, tmg], tok=True)
                            tcp = E("vector", "tensor_copy", stg, bank[:],
                                    deps=[last_mm, tok_stgout[sp]], tok=True)
                            tok_evac[bank_i] = tcp
                            dst_d = (pu_d if mat == 0 else pg_d)
                            tok_stgout[sp] = p.dma(
                                "gpsimd", sp,
                                [(dst_d[128 * och:128 * (och + 1),
                                        512 * tt:512 * (tt + 1)], stg)],
                                deps=[tcp])
                tok_mmdone[og] = last_mm
            tok_ug_end = [tok_mmdone[NOG - 1], tok_evac.get(0), tok_evac.get(1),
                          tok_evac.get(2), tok_evac.get(3)] + \
                list(tok_stgout)
            if FFN_STAGES <= 2.5:
                p.barrier()
                return
            t7a = p.dma("gpsimd", 7, [(nhi_f[0:1, 1024:1152], cA(71)),
                                      (nhi_f[0:1, 1152:1280], cA(72))],
                        deps=tok_ug_end)
            t7b = t7a
            tru = E("vector", "reduce_max", ccio[0:1, 0:1], nhi_f[0:1, 1024:1152],
                    axis=AX.X, deps=[t7a, tr1], tok=True)
            trg = E("vector", "reduce_max", ccio[0:1, 1:2], nhi_f[0:1, 1152:1280],
                    axis=AX.X, deps=[t7b], tok=True)
            tr2 = allred2([tru, trg], [])
            tb1 = E("vector", "reciprocal", srow(0), ccio[0:1, 2:3], deps=[tr2], tok=True)
            tb2 = E("vector", "tensor_scalar", srow(3), srow(0), 32767.0, None,
                    ALU.mult, deps=[tb1], tok=True)
            tb3 = E("vector", "tensor_scalar", srow(8), ccio[0:1, 2:3], 1.0 / 32767.0,
                    None, ALU.mult, deps=[tr2], tok=True)
            tb4 = E("vector", "tensor_tensor", srow(4), srow(8), srow(2), ALU.mult,
                    deps=[tb3], tok=True)
            tb5 = E("vector", "reciprocal", srow(0), ccio[0:1, 3:4], deps=[tr2, tb2],
                    tok=True)
            tb6 = E("vector", "tensor_scalar", srow(5), srow(0), 32767.0, None,
                    ALU.mult, deps=[tb5], tok=True)
            tb7 = E("vector", "tensor_scalar", srow(8), ccio[0:1, 3:4], 1.0 / 32767.0,
                    None, ALU.mult, deps=[tb5, tb4], tok=True)
            tb8 = E("vector", "tensor_tensor", srow(6), srow(8), srow(2), ALU.mult,
                    deps=[tb7], tok=True)
            tb9 = E("vector", "tensor_copy", srow(9), srow(3), deps=[tb2], tok=True)
            tba = E("vector", "tensor_copy", srow(10), srow(5), deps=[tb6], tok=True)
            tbb = E("vector", "tensor_copy", srow(11), srow(6), deps=[tb8], tok=True)
            tbc2 = bcast_cols2(srow(9, 3), 3, 65, [tb9, tba, tbb])
            if FFN_STAGES <= 3:
                p.barrier()
                return

            tmm = E("vector", "memset", cA(74), 0.0, tok=True)
            tok_use = {}
            tok_m1w = {}
            st_e2 = {}
            def e2_head(i):
                b = i % 4
                pgin = arA[:, 2048 * b:2048 * (b + 1)]
                ngi = arB_i32[:, 2048 * (i % 2):2048 * (i % 2 + 1)]
                sigb = arA[:, 8192 + 2048 * (i % 2):8192 + 2048 * (i % 2 + 1)]
                d0 = (tok_ug_end if i < 2 else [])
                if i == 0:
                    d0 = d0 + p.qtot("gpsimd")
                trd_ = p.dma("sync", i % 4, [(pgin, pg_d[128 * i:128 * (i + 1), :])],
                             deps=d0)
                tg = E("vector", "tensor_scalar", ngi, pgin, cA(66), None,
                       ALU.mult, deps=[trd_, tbc2], tok=True)
                tsg = E("scalar", "activation", sigb, ngi, ACT.Sigmoid,
                        scale=cA(67), deps=[tg], tok=True)
                st_e2[i] = (pgin, ngi, sigb)
            def e2_tail(i):
                pgin, ngi, sigb = st_e2[i]
                m1o = pgin
                ri32 = sigb.bitcast(I32)
                tri = E("vector", "tensor_scalar", ri32, sigb, 65536.0, None,
                        ALU.mult, tok=True)
                tm1 = E("vector", "scalar_tensor_tensor", m1o, ri32, 65535.0,
                        ngi, ALU.min, ALU.mult,
                        deps=[tri, tok_m1w.get(i - 2)], tok=True)
                trd2 = E("vector", "reduce_max", cA(73), m1o, axis=AX.X,
                         apply_absolute_value=True, deps=[tm1], tok=True)
                tok_use[i] = E("vector", "tensor_tensor", cA(74), cA(74),
                               cA(73), ALU.max, deps=[trd2, tmm], tok=True)
                tok_m1w[i] = p.dma("gpsimd", i % 4,
                                   [(m1_d[128 * i:128 * (i + 1), :], m1o)],
                                   deps=[tm1])
            e2_head(0)
            for i in range(1, NIC):
                e2_head(i)
                e2_tail(i - 1)
            e2_tail(NIC - 1)
            t8a = p.dma("gpsimd", 7, [(nhi_f[0:1, 1024:1152], cA(74))],
                        deps=[tok_use[NIC - 1]])
            trm1 = E("vector", "reduce_max", ccio[0:1, 0:1], nhi_f[0:1, 1024:1152],
                     axis=AX.X, deps=[t8a, tr2], tok=True)
            tz3 = E("vector", "memset", ccio[0:1, 1:2], 0.0, deps=[tr2], tok=True)
            tr3 = allred2([trm1, tz3], [])
            tc1 = E("vector", "reciprocal", srow(0), ccio[0:1, 2:3], deps=[tr3], tok=True)
            tc2 = E("vector", "tensor_scalar", srow(12), srow(0), 32767.0, None,
                    ALU.mult, deps=[tc1], tok=True)
            tc3 = E("vector", "tensor_tensor", srow(8), ccio[0:1, 2:3], srow(6),
                    ALU.mult, deps=[tr3, tb8], tok=True)
            tc4 = E("vector", "tensor_scalar", srow(8), srow(8),
                    1.0 / (65536.0 * 32767.0), None, ALU.mult, deps=[tc3], tok=True)
            tc5 = E("vector", "tensor_tensor", srow(13), srow(8), srow(4), ALU.mult,
                    deps=[tc4, tb4], tok=True)
            tbc3 = bcast_cols2(srow(12), 1, 68, [tc2])
            if FFN_STAGES <= 4:
                p.barrier()
                return

            tmo = E("vector", "memset", cA(75), 0.0, deps=[tok_use[NIC - 1]],
                    tok=True)
            tok_use2 = {}
            tok_opw = {}
            st_e3 = {}
            def e3_head(i):
                b = i % 3
                puin = arA[:, 2048 * b:2048 * (b + 1)]
                m1in = arA[:, 6144 + 2048 * b:6144 + 2048 * (b + 1)]
                d0 = [tok_use[NIC - 1]]
                if i == 0:
                    d0 += p.qtot("gpsimd")
                trdu = p.dma("sync", i % 2, [(puin,
                             pu_d[128 * i:128 * (i + 1), :])], deps=d0)
                trdm = p.dma("sync", 2 + i % 2, [(m1in,
                             m1_d[128 * i:128 * (i + 1), :])], deps=d0)
                nupi = arB_i32[:, 2048 * (i % 2):2048 * (i % 2 + 1)]
                ng2i = m1in.bitcast(I32)
                tnu = E("vector", "tensor_scalar", nupi, puin, cA(65), None,
                        ALU.mult, deps=[trdu, tbc2], tok=True)
                tng = E("scalar", "activation", ng2i, m1in, ACT.Identity,
                        scale=cA(68), deps=[trdm, tbc3], tok=True)
                st_e3[i] = (puin, m1in, nupi, ng2i, tnu, tng)
            def e3_tail(i):
                puin, m1in, nupi, ng2i, tnu, tng = st_e3[i]
                opo = puin
                top = E("vector", "tensor_tensor", opo, ng2i, nupi, ALU.mult,
                        deps=[tnu, tng, tok_opw.get(i - 2)], tok=True)
                trd2 = E("vector", "reduce_max", cA(73), opo, axis=AX.X,
                         apply_absolute_value=True, deps=[top], tok=True)
                tok_use2[i] = E("vector", "tensor_tensor", cA(75), cA(75),
                                cA(73), ALU.max, deps=[trd2, tmo], tok=True)
                tok_opw[i] = p.dma("gpsimd", i % 4,
                                   [(op_d[128 * i:128 * (i + 1), :], opo)],
                                   deps=[top])
            e3_head(0)
            for i in range(1, NIC):
                e3_head(i)
                e3_tail(i - 1)
            e3_tail(NIC - 1)
            t9a = p.dma("gpsimd", 7, [(nhi_f[0:1, 1024:1152], cA(75))],
                        deps=[tok_use2[NIC - 1]])
            trop = E("vector", "reduce_max", ccio[0:1, 0:1], nhi_f[0:1, 1024:1152],
                     axis=AX.X, deps=[t9a, tr3], tok=True)
            tz4 = E("vector", "memset", ccio[0:1, 1:2], 0.0, deps=[tr3], tok=True)
            tr4 = allred2([trop, tz4], [])
            td1 = E("vector", "reciprocal", srow(0), ccio[0:1, 2:3], deps=[tr4], tok=True)
            td2 = E("vector", "tensor_scalar", srow(16), srow(0), 32767.0, None,
                    ALU.mult, deps=[td1], tok=True)
            td3 = E("vector", "tensor_scalar", srow(8), ccio[0:1, 2:3], 0.5 / 32767.0,
                    None, ALU.mult, deps=[tr4], tok=True)
            td4 = E("vector", "tensor_tensor", srow(17), srow(8), srow(13), ALU.mult,
                    deps=[td3, tc5, td2], tok=True)
            tbc4 = bcast_cols2(srow(16, 2), 2, 69, [td4])
            if FFN_STAGES <= 5:
                p.barrier()
                return

            tok_use3 = {}
            tok_w3 = {}
            tok_w3b = {}
            st_e4 = {}
            def e4_head(i):
                b = i % 4
                opin = arA[:, 2048 * b:2048 * (b + 1)]
                noi = arB_i32[:, 2048 * (i % 2):2048 * (i % 2 + 1)]
                hib = arA_bf[:, 16384 + 2048 * (i % 2):16384 + 2048 * (i % 2 + 1)]
                d0 = [tok_use2[NIC - 1]]
                if i == 0:
                    d0 += p.qtot("gpsimd")
                trdo = p.dma("sync", i % 4, [(opin,
                             op_d[128 * i:128 * (i + 1), :])], deps=d0)
                tno = E("scalar", "activation", noi, opin, ACT.Identity,
                        scale=cA(69), deps=[trdo, tbc4], tok=True)
                thi = E("scalar", "activation", hib, noi, ACT.Identity,
                        deps=[tno, tok_w3.get(i - 2)], tok=True)
                st_e4[i] = (noi, hib, thi)
            def e4_tail(i):
                b = i % 2
                noi, hib, thi = st_e4[i]
                lob = arA_bf[:, 20480 + 2048 * b:20480 + 2048 * (b + 1)]
                tlo = E("vector", "tensor_tensor", lob, noi, hib, ALU.subtract,
                        deps=[thi, tok_w3b.get(i - 2)], tok=True)
                tok_use3[i] = tlo
                tok_w3[i] = p.dma("gpsimd", i % 4,
                                  [(nohi_d[128 * i:128 * (i + 1), :], hib),
                                   (nolo_d[128 * i:128 * (i + 1), :], lob)],
                                  deps=[thi, tlo])
                tok_w3b[i] = tok_w3[i]
            e4_head(0)
            for i in range(1, NIC):
                e4_head(i)
                e4_tail(i - 1)
            e4_tail(NIC - 1)
            tok_ld = {}
            tok_mmd = {}
            ev_last = {}
            for tg_ in range(4):
                for ich in range(NIC):
                    b = ich % 2
                    nhib = arA_bf[:, 512 * b:512 * (b + 1)]
                    nlob = arA_bf[:, 1024 + 512 * b:1024 + 512 * (b + 1)]
                    wdh = arA_bf[:, 2048 + 1024 * b:2048 + 1024 * (b + 1)]
                    wdl = arA_bf[:, 4096 + 1024 * b:4096 + 1024 * (b + 1)]
                    dep0 = [tok_mmd.get((tg_, ich - 2)),
                            tok_mmd.get((tg_ - 1, NIC - 1))]
                    if tg_ == 0 and ich < 2:
                        dep0 += [tok_w3.get(NIC - 1), tok_w3b.get(NIC - 1),
                                 tok_use3.get(NIC - 1)] + p.qtot("gpsimd")
                    t1 = p.dma("sync", b, [
                        (nhib, nohi_d[128 * ich:128 * (ich + 1),
                                      512 * tg_:512 * (tg_ + 1)]),
                        (nlob, nolo_d[128 * ich:128 * (ich + 1),
                                      512 * tg_:512 * (tg_ + 1)]),
                        (wdh, wdhi_d[128 * ich:128 * (ich + 1), :]),
                        (wdl, wdlo_d[128 * ich:128 * (ich + 1), :])],
                        deps=dep0)
                    t2 = t3 = t4 = t1
                    lm = None
                    for tc4_ in range(4):
                        for ht in range(2):
                            bank = psb[2 * tc4_ + ht]
                            first = (ich == 0)
                            last = (ich == NIC - 1)
                            dd = [t1, t2, t3, t4]
                            if first:
                                dd += [ev_last.get(tg_ - 1), last_bcast[0]]
                            lh = nhib[:, 128 * tc4_:128 * (tc4_ + 1)]
                            ll = nlob[:, 128 * tc4_:128 * (tc4_ + 1)]
                            rh = wdh[:, 512 * ht:512 * (ht + 1)]
                            rl = wdl[:, 512 * ht:512 * (ht + 1)]
                            E("tensor", "matmul", bank[:], lh, rh,
                              start=first, stop=False, deps=dd)
                            E("tensor", "matmul", bank[:], lh, rl,
                              start=False, stop=False)
                            lm = E("tensor", "matmul", bank[:], ll, rh,
                                   start=False, stop=last,
                                   tok=(ht == 1 and tc4_ == 3))
                    tok_mmd[(tg_, ich)] = lm
                for tc4_ in range(4):
                    c = 4 * tg_ + tc4_
                    for ht in range(2):
                        bank = psb[2 * tc4_ + ht]
                        ev_last[tg_] = E(
                            "vector", "scalar_tensor_tensor",
                            xres[:, 1024 * c + 512 * ht:1024 * c + 512 * (ht + 1)],
                            bank[:], cA(70),
                            xres[:, 1024 * c + 512 * ht:1024 * c + 512 * (ht + 1)],
                            ALU.mult, ALU.add,
                            deps=[tok_mmd[(tg_, NIC - 1)], tbc4], tok=True)
            p.barrier()


        def attn_phase():
            defq = []

            def drain(n):
                for _ in range(min(n, len(defq))):
                    defq.pop(0)()

            masks = arA[:, 0:2048]
            gamma = arA[:, 3072:4096]
            nhi_b = nhi[:]
            nlo_b = nlo[:]
            qh_b = nhi_b[:, 0:4096]
            ql_b = nhi_b[:, 4096:8192]
            kh_b = nhi_b[:, 8192:12288]
            kl_b = nhi_b[:, 12288:16384]
            vtmh = nlo_b[:, 0:4112]
            vtml = nlo_b[:, 4112:8224]
            vfm = nlo_f[:, 4112:6160]
            cth_b = [nlo_b[:, 12320:12832], nlo_b[:, 13344:13856]]
            ctl_b = [nlo_b[:, 12832:13344], nlo_b[:, 13856:14368]]
            htm = arA[:, 5120:5376]
            htm4 = [nlo_f[:, 4112 + 256 * i:4112 + 256 * (i + 1)] for i in range(4)]
            junk = arA[:, 12032:12288]
            mm_slots = [arA[:, 4096:4608], arA[:, 4608:5120],
                        arA[:, 6144:6656], arA[:, 6656:7168]]
            dsb_s = [arA[:, 2048:2560], arA[:, 2560:3072]]

            def xln_ap(c):
                if c < 12:
                    return arA[:, 1024 * c:1024 * (c + 1)]
                return arB[:, 1024 * (c - 12):1024 * (c - 11)]
            tok_xln = []
            junk0 = nhi_f[:, 0:1024]
            for c in range(NTC):
                xc = xres[:, 1024 * c:1024 * (c + 1)]
                ts_ = E("vector", "reduce_sum", cA(48 + c), xc, axis=AX.X, tok=True)
                tn = E("scalar", "activation", cA(c), cA(48 + c), ACT.Identity,
                       scale=-1.0 / D, deps=[ts_], tok=True)
                tq = E("scalar", "activation", junk0, xc, ACT.Square,
                       bias=cA(c), accum_out=cA(48 + c), deps=[tn], tok=True)
                tr0 = E("scalar", "activation", cA(48 + c), cA(48 + c), ACT.Sqrt,
                        scale=1.0 / D, bias=cA(79), deps=[tq], tok=True)
                tr = E("vector", "reciprocal", cA(16 + c), cA(48 + c),
                       deps=[tr0], tok=True)
                tx = E("vector", "tensor_scalar", xln_ap(c), xc, cA(c), cA(16 + c),
                       ALU.add, ALU.mult, deps=[tn, tr], tok=True)
                tok_xln.append(tx)

            stgA = [arB[:, 4096:4608], arB[:, 4608:5120]]
            for c in range(NTC):
                for half in range(2):
                    bank = psb[4 + half]
                    ltp = None
                    for kk in range(4):
                        f = 4 * half + kk
                        ltp = E("tensor", "transpose",
                                bank[:, 128 * kk:128 * (kk + 1)],
                                xln_ap(c)[:, 128 * f:128 * (f + 1)], identt[:],
                                deps=[tok_xln[c]] if kk == 0 else [], tok=(kk == 3))
                    tev = E("vector", "tensor_copy", stgA[half], bank[:],
                            deps=[ltp], tok=True)
                    dst = xfm_d[512 * half:512 * (half + 1),
                                128 * c:128 * (c + 1)] \
                        .rearrange("(k q) c -> q k c", q=128)
                    p.dma("gpsimd", half, [
                        (dst, stgA[half].rearrange("p (k c) -> p k c", c=128))],
                        deps=[tev])

            wstg = [gcols[:, 128 + 32 * kk:160 + 32 * kk] for kk in range(2)]
            tmp32 = arB[:, 5120:5152]
            tmp32b = arB[:, 5152:5184]
            for kk, wsrc in enumerate((wit_d, wft_d)):
                t1 = p.dma("sync", 4, [
                    (wstg[kk], wsrc[0:1024, :].rearrange("(f q) c -> q f c", q=128)),
                    (tmp32, wsrc[1024:2048, :].rearrange("(f q) c -> q f c", q=128)),
                    (tmp32b, wsrc[2048:3072, :].rearrange("(f q) c -> q f c", q=128))])
                E("vector", "tensor_tensor", wstg[kk], wstg[kk], tmp32, ALU.add,
                  deps=[t1], tok=True)
                E("vector", "tensor_tensor", wstg[kk], wstg[kk], tmp32b, ALU.add,
                  tok=True)
            tbias = p.dma("sync", 5, [(colA[0:4, 76:77], bi_d[:, :]),
                                      (colA[0:4, 77:78], bf_d[:, :]),
                                      (scrow[0:1, 32:33], lnw_d[:, 0:1])])
            trow = p.dma("sync", 5, [(nhi_f[0:1, 1024:2048], lnw_d[:, :])])
            ig_sb = arB[0:4, 0:2048]
            fg_sb = arB[0:4, 2048:4096]
            xchunk = [arA[:, 7168:9216], arA[:, 9216:11264]]
            gmm = {}
            for f in range(NFC):
                tl = p.dma("sync", 6, [(xchunk[f % 2],
                                        xfm_d[128 * f:128 * (f + 1), :])])
                for kk in range(2):
                    w3 = wstg[kk].rearrange("p (f c) -> p f c", c=4)
                    for tt in range(4):
                        gmm[(kk, tt)] = E(
                            "tensor", "matmul", psb[4 * kk + tt][0:4, 0:512],
                            w3[:, f, :], xchunk[f % 2][:, 512 * tt:512 * (tt + 1)],
                            start=(f == 0), stop=(f == NFC - 1),
                            deps=[tl], tok=(f == NFC - 1))
            for kk in range(2):
                dstsb = (ig_sb, fg_sb)[kk]
                for tt in range(4):
                    E("vector", "tensor_copy",
                      dstsb[:, 512 * tt:512 * (tt + 1)],
                      psb[4 * kk + tt][0:4, 0:512],
                      deps=[gmm[(kk, tt)]], tok=True)
            E("vector", "tensor_scalar", ig_sb, ig_sb, colA[0:4, 76:77], None,
              ALU.add, deps=[tbias], tok=True)
            E("vector", "tensor_scalar", fg_sb, fg_sb, colA[0:4, 77:78], None,
              ALU.add, deps=[tbias], tok=True)

            scr = arB[0:4, 4096:6144]
            E("scalar", "activation", scr, fg_sb, ACT.Abs, tok=True)
            E("scalar", "activation", scr, scr, ACT.Exp, scale=-1.0, tok=True)
            E("scalar", "activation", scr, scr, ACT.Ln, bias=colA[0:4, 78:79],
              tok=True)
            E("vector", "tensor_scalar", fg_sb, fg_sb, 0.0, None, ALU.min,
              tok=True)
            E("vector", "tensor_tensor", fg_sb, fg_sb, scr, ALU.subtract,
              tok=True)
            E("vector", "tensor_tensor_scan", scr, fg_sb, fg_sb, 0.0,
              ALU.add, ALU.bypass, tok=True)
            E("vector", "tensor_tensor", ig_sb, ig_sb, scr, ALU.subtract,
              tok=True)
            E("vector", "tensor_tensor_scan", fg_sb, ig_sb, ig_sb, -3e38,
              ALU.max, ALU.bypass, tok=True)
            E("vector", "tensor_tensor", scr, scr, fg_sb, ALU.add, tok=True)
            E("scalar", "activation", scr, scr, ACT.Exp, scale=-1.0, tok=True)
            b_sb, M_sb, em_sb = ig_sb, fg_sb, scr

            for kk, (src_, dst_) in enumerate(((b_sb, gcols[:, 0:64]),
                                               (em_sb, gcols[:, 64:128]))):
                for grp in range(4):
                    bank = psb[1]
                    for s4 in range(4):
                        s16 = 4 * grp + s4
                        E("tensor", "transpose", bank[:, 4 * s4:4 * s4 + 4],
                          src_[:, 128 * s16:128 * (s16 + 1)],
                          identt[0:4, 0:4], tok=True)
                    E("vector", "tensor_copy",
                      dst_[:, 16 * grp:16 * (grp + 1)], bank[:, 0:16],
                      tok=True)

            tpl = E("vector", "tensor_scalar", nhi_f[0:1, 1024:2048],
                    nhi_f[0:1, 1024:2048], 1.0, None, ALU.add, deps=[trow],
                    tok=True)
            for half in range(2):
                tm = E("tensor", "matmul", psb[0][:, 0:512], onest[:],
                       nhi_f[0:1, 1024 + 512 * half:1024 + 512 * (half + 1)],
                       start=True, stop=True, deps=[tpl], tok=True)
                E("vector", "tensor_copy", gamma[:, 512 * half:512 * (half + 1)],
                  psb[0][:], deps=[tm], tok=True)
            p.dma("sync", 7, [(masks, maskd_d[:, :])])

            for h in range(NH):
                bd = arA[:, 11264:12032]
                bdq = bd[:, 0:256]
                bdk = bd[:, 256:512]
                bdv = bd[:, 512:768]
                xfm = arA[:, 7168:11264]
                p.dma("sync", 0, [
                    (bdq.rearrange("p (f c) -> p f c", c=128),
                     bdq_d[2 * h:2 * h + 2, :, :].rearrange("f q c -> q f c")),
                    (bdk.rearrange("p (f c) -> p f c", c=128),
                     bdk_d[2 * h:2 * h + 2, :, :].rearrange("f q c -> q f c")),
                    (bdv.rearrange("p (f c) -> p f c", c=128),
                     bdv_d[2 * h:2 * h + 2, :, :].rearrange("f q c -> q f c"))])
                p.dma("sync", 1, [
                    (xfm.rearrange("p (f c) -> p f c", c=2048),
                     xfm_d[256 * h:256 * (h + 1), :]
                     .rearrange("(f q) c -> q f c", q=128))])
                qkv_ctr = [0]
                for proj, bdw in enumerate((bdq, bdk, bdv)):
                    if proj == 2:
                        drain(len(defq))
                    for fc in range(2):
                        for tt in range(4):
                            qb = psb[qkv_ctr[0] % 2]
                            qkv_ctr[0] += 1
                            tm = E("tensor", "matmul", qb[:],
                                   bdw[:, 128 * fc:128 * (fc + 1)],
                                   xfm[:, 2048 * fc + 512 * tt:
                                       2048 * fc + 512 * (tt + 1)],
                                   start=True, stop=True, tok=True)
                            lo_col = 2048 * fc + 512 * tt
                            if proj == 2:
                                E("vector", "tensor_copy",
                                  vfm[:, 512 * tt:512 * (tt + 1)], qb[:],
                                  deps=[tm], tok=True)
                            elif proj == 0:
                                dh_ = qh_b[:, lo_col:lo_col + 512]
                                dl_ = ql_b[:, lo_col:lo_col + 512]
                                E("vector", "tensor_copy", dh_, qb[:],
                                  deps=[tm], tok=True)
                                E("vector", "tensor_tensor", dl_, qb[:],
                                  dh_, ALU.subtract, tok=True)
                            else:
                                dh_ = kh_b[:, lo_col:lo_col + 512]
                                dl_ = kl_b[:, lo_col:lo_col + 512]
                                E("vector", "tensor_scalar", dh_, qb[:],
                                  0.0625, None, ALU.mult, deps=[tm], tok=True)
                                E("vector", "scalar_tensor_tensor", dl_,
                                  qb[:], 0.0625, dh_, ALU.mult, ALU.subtract,
                                  tok=True)
                        if proj == 2:
                            for grp in range(4):
                                bank = psb[2 + grp % 2]
                                ltp = None
                                for kk in range(4):
                                    jt = 4 * grp + kk
                                    ltp = E("tensor", "transpose",
                                            bank[:, 128 * kk:128 * (kk + 1)],
                                            vfm[:, 128 * jt:128 * (jt + 1)],
                                            identt[:], tok=(kk == 3))
                                dsth = vtmh.rearrange("p (j c) -> p j c", c=257) \
                                    [:, 4 * grp:4 * grp + 4, 128 * fc:128 * (fc + 1)]
                                dstl = vtml.rearrange("p (j c) -> p j c", c=257) \
                                    [:, 4 * grp:4 * grp + 4, 128 * fc:128 * (fc + 1)]
                                E("vector", "tensor_copy", dsth,
                                  bank[:].rearrange("p (k c) -> p k c", c=128),
                                  deps=[ltp], tok=True)
                                E("vector", "tensor_tensor", dstl,
                                  bank[:].rearrange("p (k c) -> p k c", c=128),
                                  dsth, ALU.subtract, tok=True)
                E("vector", "memset",
                  vtmh.rearrange("p (j c) -> p j c", c=257)[:, :, 256:257], 1.0,
                  tok=True)
                E("vector", "memset",
                  vtml.rearrange("p (j c) -> p j c", c=257)[:, :, 256:257], 0.0,
                  tok=True)
                qh3 = qh_b.rearrange("p (f t) -> p f t", t=2048)
                ql3 = ql_b.rearrange("p (f t) -> p f t", t=2048)
                kh3 = kh_b.rearrange("p (f t) -> p f t", t=2048)
                kl3 = kl_b.rearrange("p (f t) -> p f t", t=2048)

                def emit_ST(it, jc):
                    stb = psb[6 + (jc % 2)]
                    first = True
                    lmm = None
                    for fc in range(2):
                        lh = kh3[:, fc, 128 * jc:128 * (jc + 1)]
                        ll = kl3[:, fc, 128 * jc:128 * (jc + 1)]
                        rh = qh3[:, fc, 512 * it:512 * (it + 1)]
                        rl = ql3[:, fc, 512 * it:512 * (it + 1)]
                        E("tensor", "matmul", stb[:], lh, rh,
                          start=first, stop=False)
                        E("tensor", "matmul", stb[:], lh, rl,
                          start=False, stop=False)
                        lmm = E("tensor", "matmul", stb[:], ll, rh,
                                start=False, stop=(fc == 1), tok=(fc == 1))
                        first = False
                    return stb, lmm

                def emit_D_CT(it, jc, stb, lmm):
                    dk = jc - 4 * it
                    esrc = mm_slots[dk] if dk >= 0 else arA[:, 5632:6144]
                    Dsb = dsb_s[jc % 2]
                    E("scalar", "activation", Dsb, esrc, ACT.Exp,
                      scale=-1.0, bias=gcols[:, 4 * jc + h:4 * jc + h + 1],
                      tok=True)
                    E("vector", "tensor_tensor", Dsb, stb[:], Dsb,
                      ALU.mult, deps=[lmm], tok=True)
                    cth = cth_b[jc % 2]
                    ctl = ctl_b[jc % 2]
                    E("gpsimd", "tensor_copy", cth, Dsb, tok=True)
                    E("vector", "tensor_tensor", ctl, Dsb, cth,
                      ALU.subtract, tok=True)
                    return cth, ctl

                def emit_CV(it, jc, cth, ctl):
                    vh = vtmh.rearrange("p (j c) -> p j c", c=257)[:, jc, :]
                    vl = vtml.rearrange("p (j c) -> p j c", c=257)[:, jc, :]
                    for i1 in range(4):
                        if jc > 4 * it + i1:
                            continue
                        cvb = psb[2 + i1]
                        first = (jc == 0)
                        last = (jc == 4 * it + i1)
                        E("tensor", "matmul", cvb[:, 0:257],
                          cth[:, 128 * i1:128 * (i1 + 1)], vh,
                          start=first, stop=False)
                        E("tensor", "matmul", cvb[:, 0:257],
                          cth[:, 128 * i1:128 * (i1 + 1)], vl,
                          start=False, stop=False)
                        E("tensor", "matmul", cvb[:, 0:257],
                          ctl[:, 128 * i1:128 * (i1 + 1)], vh,
                          start=False, stop=last, tok=True)

                def emit_phase1(it):
                    for i1 in range(4):
                        cvb = psb[2 + i1]
                        i16 = 4 * it + i1
                        c0, c1 = 60 + 2 * i1, 61 + 2 * i1
                        E("vector", "reduce_max", cA(c0),
                          cvb[:, 256:257], axis=AX.X,
                          apply_absolute_value=True, tok=True)
                        E("vector", "tensor_tensor", cA(c0), cA(c0),
                          gcols[:, 64 + 4 * i16 + h:64 + 4 * i16 + h + 1],
                          ALU.max, tok=True)
                        E("vector", "tensor_scalar", cA(c0), cA(c0), 1e-8,
                          None, ALU.add, tok=True)
                        E("vector", "reciprocal", cA(c1), cA(c0), tok=True)
                        E("vector", "tensor_scalar", htm4[i1], cvb[:, 0:256],
                          cA(c1), None, ALU.mult, tok=True)

                def defer_mhln(it):
                    for i1 in range(4):
                        c0, c1 = 60 + 2 * i1, 61 + 2 * i1
                        hh = htm4[i1]
                        defq.extend([
                            (lambda hh=hh, c0=c0: E("vector", "reduce_sum",
                             cA(c0), hh, axis=AX.X, tok=True)),
                            (lambda c0=c0: E("vector", "tensor_scalar",
                             cA(c0), cA(c0), -1.0 / DH, None, ALU.mult,
                             tok=True)),
                            (lambda hh=hh, c0=c0: E("vector", "tensor_scalar",
                             hh, hh, cA(c0), None, ALU.add, tok=True)),
                            (lambda hh=hh: E("vector", "tensor_tensor",
                             junk, hh, hh, ALU.mult, tok=True)),
                            (lambda c1=c1: E("vector", "reduce_sum", cA(c1),
                             junk, axis=AX.X, tok=True)),
                        ])
                    def sqrt4():
                        for i1 in range(4):
                            E("scalar", "activation", cA(61 + 2 * i1),
                              cA(61 + 2 * i1), ACT.Sqrt, scale=1.0 / DH,
                              bias=cA(79), tok=True)
                    defq.append(sqrt4)
                    for i1 in range(4):
                        i16 = 4 * it + i1
                        c1 = 61 + 2 * i1
                        hh = htm4[i1]
                        xsl = xres[:, 1024 * i16 + 256 * h:
                                   1024 * i16 + 256 * (h + 1)]
                        defq.extend([
                            (lambda c1=c1: E("vector", "reciprocal", cA(c1),
                             cA(c1), tok=True)),
                            (lambda hh=hh, c1=c1: E("vector",
                             "scalar_tensor_tensor", hh, hh, cA(c1),
                             gamma[:, 256 * h:256 * (h + 1)], ALU.mult,
                             ALU.mult, tok=True)),
                            (lambda hh=hh, xsl=xsl: E("vector",
                             "tensor_tensor", xsl, xsl, hh, ALU.add,
                             tok=True)),
                        ])

                def defer_masks(it):
                    for dk in range(4):
                        defq.append(lambda dk=dk: E(
                            "vector", "tensor_tensor", mm_slots[dk],
                            arA[:, 5632:6144],
                            masks[:, 512 * dk:512 * (dk + 1)], ALU.add,
                            tok=True))

                for it in range(4):
                    tm = E("tensor", "matmul", psb[0][:],
                           sel4t[:, 128 * h:128 * (h + 1)],
                           M_sb[:, 512 * it:512 * (it + 1)],
                           start=True, stop=True, tok=True)
                    E("vector", "tensor_copy", arA[:, 5632:6144], psb[0][:],
                      deps=[tm], tok=True)
                    if it == 0:
                        drain(len(defq))
                        defer_masks(it)
                        drain(4)
                    else:
                        drain(len(defq))
                        emit_phase1(it - 1)
                        defer_masks(it)
                        defer_mhln(it - 1)
                    njc = 4 * it + 4
                    stA = emit_ST(it, 0)
                    pend = (0, stA)
                    for jc in range(1, njc + 1):
                        pj, (stb, lmm) = pend
                        cth, ctl = emit_D_CT(it, pj, stb, lmm)
                        drain(3)
                        if jc <= njc - 1:
                            pend = (jc, emit_ST(it, jc))
                        emit_CV(it, pj, cth, ctl)
                    if it == 3:
                        drain(len(defq))
                        emit_phase1(3)
                        defer_mhln(3)
                        if h == NH - 1:
                            drain(len(defq))
            p.barrier()

        if N_PHASES >= 2:
            ffn_phase(True)
        if N_PHASES >= 3:
            attn_phase()
        if N_PHASES >= 4:
            ffn_phase(False)

        p.barrier()
        p.dma("gpsimd", 0, [(out_d[128 * c:128 * (c + 1), :],
                             xres[:, 1024 * c:1024 * (c + 1)])
                            for c in range(NTC)])
        E("gpsimd", "nop", deps=p.qtot("gpsimd"))

        p.finalize()
        block = es.enter_context(nc.Block())

        @block.sync
        def _(e):
            for fn in p.ops["sync"]:
                fn(e)

        @block.vector
        def _(e):
            for fn in p.ops["vector"]:
                fn(e)

        @block.scalar
        def _(e):
            for fn in p.ops["scalar"]:
                fn(e)

        @block.tensor
        def _(e):
            for fn in p.ops["tensor"]:
                fn(e)

        @block.gpsimd
        def _(e):
            for fn in p.ops["gpsimd"]:
                fn(e)

    return nc



def _bd_chunks(w):
    bd = np.zeros((NFC, 128, 128), dtype=np.float32)
    for c in range(NFC):
        for g in range(32):
            blk = w[32 * c + g]
            bd[c, 4 * g:4 * g + 4, 4 * g:4 * g + 4] = blk.T
    return bd


def _host_consts():
    ident = np.eye(128, dtype=np.float32)
    maskd = np.zeros((128, 2048), dtype=np.float32)
    pp = np.arange(128)[:, None]
    cc = np.arange(512)[None, :]
    for k in range(4):
        maskd[:, 512 * k:512 * (k + 1)] = np.where(128 * k + pp > cc, 1e30, 0.0)
    onesk = np.ones((1, 128), dtype=np.float32)
    sel4 = np.zeros((4, 512), dtype=np.float32)
    for h in range(4):
        sel4[h, 128 * h:128 * (h + 1)] = 1.0
    return ident, maskd, onesk, sel4


def kernel(L0_attn_x, L0_attn_wq, L0_attn_wk, L0_attn_wv, L0_attn_wi,
           L0_attn_bi, L0_attn_wf, L0_attn_bf, L0_attn_ln_w,
           L0_ffn_w_gate, L0_ffn_w_up, L0_ffn_w_down):
    x = np.ascontiguousarray(np.asarray(L0_attn_x, dtype=np.float32))
    ident, maskd, onesk, sel4 = _host_consts()
    common = {
        "wup": np.ascontiguousarray(np.asarray(L0_ffn_w_up, np.float32)),
        "wgate": np.ascontiguousarray(np.asarray(L0_ffn_w_gate, np.float32)),
        "wdown": np.ascontiguousarray(np.asarray(L0_ffn_w_down, np.float32)),
        "bdq": _bd_chunks(np.asarray(L0_attn_wq, np.float32)),
        "bdk": _bd_chunks(np.asarray(L0_attn_wk, np.float32)),
        "bdv": _bd_chunks(np.asarray(L0_attn_wv, np.float32)),
        "wit": np.ascontiguousarray(np.asarray(L0_attn_wi, np.float32).T),
        "wft": np.ascontiguousarray(np.asarray(L0_attn_wf, np.float32).T),
        "bi": np.ascontiguousarray(np.asarray(L0_attn_bi, np.float32).reshape(4, 1)),
        "bf4": np.ascontiguousarray(np.asarray(L0_attn_bf, np.float32).reshape(4, 1)),
        "lnw": np.ascontiguousarray(np.asarray(L0_attn_ln_w, np.float32).reshape(1, D)),
        "ident": ident, "maskd": maskd, "onesk": onesk, "sel4": sel4,
    }
    nc = build_program()
    in_maps = [dict(common, x=np.ascontiguousarray(x[c])) for c in range(8)]
    res = run_bass_kernel_spmd(nc, in_maps, core_ids=list(range(8)))
    out = np.stack([res.results[c]["out"] for c in range(8)], axis=0)
    return out.astype(np.float32)


# version 14
# speedup vs baseline: 1.0544x; 1.0102x over previous
import numpy as np
from contextlib import ExitStack

import concourse.bass as bass
import concourse.mybir as mybir
from concourse.bass_utils import run_bass_kernel_spmd

F32 = mybir.dt.float32
F32R = mybir.dt.float32r
BF16 = mybir.dt.bfloat16
I32 = mybir.dt.int32
ALU = mybir.AluOpType
ACT = mybir.ActivationFunctionType
AX = mybir.AxisListType

ENGINES = ("sync", "vector", "scalar", "tensor", "gpsimd")
RG = [[0, 1, 2, 3, 4, 5, 6, 7]]

S, D, NH, DH, I3 = 2048, 1024, 4, 256, 3072
NTC = S // 128
NFC = D // 128
NIC = I3 // 128

N_PHASES = 99
FFN_STAGES = 99
NOG_LIM = 99


class Prog:

    N_DMA = {"sync": 8, "gpsimd": 8, "scalar": 2}

    def __init__(self, nc, es):
        self.nc = nc
        self.sems = {}
        for name in ("sv", "ss", "st_", "sg", "sy"):
            self.sems[name] = [es.enter_context(nc.semaphore(name)), 0]
        self.own = {"vector": "sv", "scalar": "ss", "tensor": "st_",
                    "gpsimd": "sg", "sync": "sy"}
        for eng, n in self.N_DMA.items():
            for i in range(n):
                name = "q_%s%d" % (eng, i)
                self.sems[name] = [es.enter_context(nc.semaphore(name)), 0]
        self.N_TOK = 8
        for e in ENGINES:
            for i in range(self.N_TOK):
                nm = "t_%s%d" % (e, i)
                self.sems[nm] = [es.enter_context(nc.semaphore(nm)), 0]
        self.tok_rr = {e: 0 for e in ENGINES}
        self.ops = {e: [] for e in ENGINES}
        self.guar = {e: {} for e in ENGINES}
        self.lastw = {}
        self.lastr = {}
        self.rec = []

    @staticmethod
    def _is_ap(a):
        return hasattr(a, "tensor") and hasattr(a, "offset") and hasattr(a, "ap")

    @staticmethod
    def _rng(ap):
        t = ap.tensor
        import concourse.mybir as _mb
        esz = _mb.dt.size(ap.dtype)
        shape = list(t.shape)
        row = shape[-1] if len(shape) > 1 else shape[0]
        tesz = _mb.dt.size(t.dtype)
        row_bytes = row * tesz
        off_b = ap.offset * esz
        col_lo = off_b % row_bytes if row_bytes else 0
        span = 0
        for step, count in list(ap.ap):
            sb = abs(int(step)) * esz
            if row_bytes and sb % row_bytes == 0 and sb != 0:
                continue
            span += (int(count) - 1) * sb
        return (t.name, col_lo, col_lo + span + esz)

    _WRITES = {
        "tensor_scalar": [0], "tensor_tensor": [0], "tensor_copy": [0],
        "tensor_sub": [0], "tensor_add": [0], "tensor_mul": [0],
        "tensor_max": [0], "scalar_tensor_tensor": [0],
        "tensor_tensor_scan": [0], "reduce_max": [0], "reduce_sum": [0],
        "tensor_reduce": [0], "reciprocal": [0], "memset": [0],
        "activation": [0], "transpose": [0], "matmul": [0], "iota": [0],
    }

    def _autodeps(self, eng, meth, args, kw, tokval_next):
        reads, writes = [], []
        wpos = self._WRITES.get(meth, [])
        for i, a in enumerate(args):
            if self._is_ap(a):
                (writes if i in wpos else reads).append(a)
        for k, a in kw.items():
            if self._is_ap(a):
                if k in ("out", "accum_out"):
                    writes.append(a)
                else:
                    reads.append(a)
        deps = []
        psum_pe = (eng == "tensor" and meth == "matmul"
                   and kw.get("start") is False)
        for a in reads:
            r = self._rng(a)
            for (lo, hi, tk, teng) in self.lastw.get(r[0], []):
                if hi > r[1] and r[2] > lo:
                    if psum_pe and teng == "tensor" and r[0].startswith("pb"):
                        continue
                    deps.append(tk)
        for a in writes:
            r = self._rng(a)
            for m in (self.lastw, self.lastr):
                for (lo, hi, tk, teng) in m.get(r[0], []):
                    if hi > r[1] and r[2] > lo:
                        if psum_pe and teng == "tensor" and r[0].startswith("pb"):
                            continue
                        deps.append(tk)
        return deps, reads, writes

    def _note(self, eng, reads, writes, tokval):
        for a in writes:
            nm, lo, hi = self._rng(a)
            lst = self.lastw.setdefault(nm, [])
            lst[:] = [e for e in lst if not (lo <= e[0] and e[1] <= hi)]
            lst.append((lo, hi, tokval, eng))
            rl = self.lastr.setdefault(nm, [])
            rl[:] = [e for e in rl if not (lo <= e[0] and e[1] <= hi)]
        for a in reads:
            nm, lo, hi = self._rng(a)
            self.lastr.setdefault(nm, []).append((lo, hi, tokval, eng))

    @staticmethod
    def _flat(deps):
        out = []
        for d in deps:
            if d is None:
                continue
            if isinstance(d, list):
                out.extend(x for x in d if x is not None)
            else:
                out.append(d)
        return out

    def emit(self, eng, meth, *args, deps=(), tok=False, _inc=None,
             note_tok=None, _skip=None, **kw):
        if eng == "tensor" and meth == "nop":
            ps1, id1 = self.tensor_nop_args
            meth = "transpose"
            args = (ps1, id1, id1)
        adeps, rds, wrs = self._autodeps(eng, meth, args, kw, None)
        oid = len(self.rec)
        all_deps = []
        for d in self._flat(list(deps) + adeps):
            if _skip is not None and d == _skip:
                continue
            all_deps.append(d)
        qtok = None
        if _inc is not None:
            q = _inc
            self.sems[q][1] += 16
            qtok = (q, self.sems[q][1])
        self.rec.append({"eng": eng, "meth": meth, "args": args, "kw": kw,
                         "deps": all_deps, "qtok": qtok})
        tokval = oid if qtok is None else (note_tok if note_tok is not None
                                           else qtok)
        self._note(eng, rds, wrs, oid if qtok is None else tokval)
        return tokval

    def finalize(self):
        used = set()
        for r in self.rec:
            for d in r["deps"]:
                if isinstance(d, int):
                    used.add(d)
        assign = {}
        for oid, r in enumerate(self.rec):
            if r["qtok"] is not None:
                assign[oid] = ("__dma__", r["qtok"])
                continue
            if oid in used:
                eng = r["eng"]
                q = "t_%s%d" % (eng, self.tok_rr[eng] % self.N_TOK)
                self.tok_rr[eng] += 1
                drain = (q, self.sems[q][1])
                self.sems[q][1] += 1
                assign[oid] = (q, self.sems[q][1])
                r["drain"] = drain
        for oid, r in enumerate(self.rec):
            eng = r["eng"]
            waits = []
            wl = list(r["deps"])
            if "drain" in r:
                wl.append(r["drain"])
            for d in wl:
                if isinstance(d, int):
                    a = assign[d]
                    d = a[1] if a[0] == "__dma__" else a
                sname, val = d
                if self.guar[eng].get(sname, 0) >= val:
                    continue
                self.guar[eng][sname] = val
                waits.append((self.sems[sname][0], val))
            inc = None
            if r["qtok"] is not None:
                inc = (self.sems[r["qtok"][0]][0], 16)
            elif oid in assign:
                a = assign[oid]
                inc = (self.sems[a[0]][0], 1)

            def run(e, meth=r["meth"], args=r["args"], kw=r["kw"],
                    waits=waits, inc=inc):
                for h, v in waits:
                    e.wait_ge(h, v)
                ins = getattr(e, meth)(*args, **kw)
                if inc is not None:
                    ins.then_inc(*inc)
            self.ops[eng].append(run)

    def dma(self, eng, slot, pairs, deps=()):
        q = "q_%s%d" % (eng, slot % self.N_DMA[eng])
        drain = (q, self.sems[q][1])
        final = (q, self.sems[q][1] + 16 * len(pairs))
        for j, (o, i_) in enumerate(pairs):
            dd = list(deps) + [drain] if j == 0 else ()
            self.emit(eng, "dma_start", out=o, in_=i_, deps=dd, _inc=q,
                      note_tok=final, _skip=final)
        return final

    def qtot(self, eng):
        return [("q_%s%d" % (eng, i), self.sems["q_%s%d" % (eng, i)][1])
                for i in range(self.N_DMA[eng])]

    def barrier(self):
        toks = {}
        for e in ENGINES:
            toks[e] = self.emit(e, "nop")
        qvals = []
        for eng in self.N_DMA:
            qvals += self.qtot(eng)
        for e in ENGINES:
            deps = [toks[e2] for e2 in ENGINES if e2 != e] + qvals
            self.emit(e, "nop", deps=deps)


def build_program():
    nc = bass.Bass("TRN2", target_bir_lowering=False)

    def din(name, shape, dt=F32):
        return nc.dram_tensor(name, shape, dt, kind="ExternalInput").ap()

    def dint(name, shape, dt=F32):
        return nc.dram_tensor(name, shape, dt, kind="Internal").ap()

    x_d = din("x", [S, D])
    wup_d = din("wup", [I3, D])
    wgate_d = din("wgate", [I3, D])
    wdown_d = din("wdown", [D, I3])
    bdq_d = din("bdq", [NFC, 128, 128])
    bdk_d = din("bdk", [NFC, 128, 128])
    bdv_d = din("bdv", [NFC, 128, 128])
    wit_d = din("wit", [I3, 4])
    wft_d = din("wft", [I3, 4])
    bi_d = din("bi", [4, 1])
    bf_d = din("bf4", [4, 1])
    lnw_d = din("lnw", [1, D])
    ident_d = din("ident", [128, 128])
    maskd_d = din("maskd", [128, 2048])
    onesk_d = din("onesk", [1, 128])
    sel4_d = din("sel4", [4, 512])
    out_d = nc.dram_tensor("out", [S, D], F32, kind="ExternalOutput").ap()

    pu_d = dint("pu", [I3, S])
    pg_d = dint("pg", [I3, S])
    m1_d = dint("m1", [I3, S])
    op_d = dint("op", [I3, S])
    nohi_d = dint("nohi", [I3, S], BF16)
    nolo_d = dint("nolo", [I3, S], BF16)
    wuphi_d = dint("wuphi", [D, I3], BF16)
    wuplo_d = dint("wuplo", [D, I3], BF16)
    wghi_d = dint("wghi", [D, I3], BF16)
    wglo_d = dint("wglo", [D, I3], BF16)
    wdhi_d = dint("wdhi", [I3, D], BF16)
    wdlo_d = dint("wdlo", [I3, D], BF16)
    xfm_d = dint("xfmsp", [D, S])
    ccin_d = [dint("ccin%d" % k, [1, 2]) for k in range(8)]
    ccout_d = [dint("ccout%d" % k, [1, 2]) for k in range(8)]

    es = ExitStack()

    def sb(name, shape, dt=F32):
        return es.enter_context(nc.sbuf_tensor(name, shape, dt))

    with es:
        xres = sb("xres", [128, NTC * 1024])
        arA = sb("arA", [128, 12288])
        arB = sb("arB", [128, 6144])
        nhi = sb("nhi", [128, 16384], BF16)
        nlo = sb("nlo", [128, 16384], BF16)
        colA = sb("colA", [128, 80])
        gcols = sb("gcols", [128, 192])
        identt = sb("identt", [128, 128])
        onest = sb("onest", [1, 128])
        sel4t = sb("sel4t", [4, 512])
        scrow = sb("scrow", [1, 64])
        ccio = sb("ccio", [1, 4])
        psb = [es.enter_context(nc.psum_tensor("pb%d" % i, [128, 512], F32))
               for i in range(8)]

        arA_bf = arA[:].bitcast(BF16)
        arB_i32 = arB[:].bitcast(I32)
        nhi_f = nhi[:].bitcast(F32)
        nlo_f = nlo[:].bitcast(F32)

        p = Prog(nc, es)
        p.tensor_nop_args = (psb[7][0:1, 0:1], identt[0:1, 0:1])
        E = p.emit
        cc_ev = [0]

        def cA(i):
            return colA[:, i:i + 1]

        def srow(i, n=1):
            return scrow[0:1, i:i + n]

        p.dma("sync", 0, [(xres[:, 1024 * c:1024 * (c + 1)],
                           x_d[128 * c:128 * (c + 1), :]) for c in range(NTC)])
        p.dma("sync", 1, [(identt[:], ident_d[:, :]), (onest[:], onesk_d[:, :]),
                          (sel4t[:], sel4_d[:, :])])
        E("vector", "memset", colA[:, 79:80], 1e-5, tok=True)
        E("vector", "memset", colA[:, 78:79], 1.0, tok=True)
        p.barrier()

        wp_state = {"load": [None, None], "done": [None, None],
                    "stg": [None, None], "t5": None}

        def w_prep(w_nat, hi_dst, lo_dst, O, I, wn_bufs, wdq_bufs,
                   bmb, stgs, banks=(4, 5), dslot=(0, 1)):
            noch = O // 128
            nb = I // 32
            stg_hi = [stgs[0], stgs[1]]
            stg_lo = [stgs[2], stgs[3]]
            tok_load = wp_state["load"]
            tok_done = wp_state["done"]
            tok_stg = wp_state["stg"]
            bm = bmb[:, 0:nb]
            sA = bmb[:, 128:128 + nb]
            rs = bmb[:, 256:256 + nb]
            for och in range(noch):
                b = och % 2
                wnat = wn_bufs[b]
                wdq = wdq_bufs[b]
                dl = [tok_done[b], wp_state.get("t5_%d" % b)]
                if och < 2:
                    dl += [tok_done[0], tok_done[1], wp_state.get("t5_0"),
                           wp_state.get("t5_1"), tok_stg[0], tok_stg[1]]
                tok_load[b] = p.dma("sync", b, [(wnat,
                                w_nat[128 * och:128 * (och + 1), :])],
                                deps=dl)
                w3 = wnat.rearrange("p (g e) -> p g e", e=32)
                t1 = E("vector", "reduce_max", bm.rearrange("p (g o) -> p g o", o=1),
                       w3, axis=AX.X, apply_absolute_value=True,
                       deps=[tok_load[b]], tok=True)
                t2 = E("vector", "tensor_scalar", sA, bm, 1.0 / 7.0, 1e-12,
                       ALU.mult, ALU.max, deps=[t1], tok=True)
                t3 = E("vector", "reciprocal", rs, sA, deps=[t2], tok=True)
                q32 = wnat.bitcast(I32)
                t4 = E("vector", "tensor_tensor",
                       q32.rearrange("p (g e) -> p g e", e=32), w3,
                       rs.rearrange("p (g o) -> p g o", o=1).broadcast_to((128, nb, 32)),
                       ALU.mult, deps=[t3], tok=True)
                t5 = E("vector", "tensor_tensor",
                       wdq.rearrange("p (g e) -> p g e", e=32),
                       q32.rearrange("p (g e) -> p g e", e=32),
                       sA.rearrange("p (g o) -> p g o", o=1).broadcast_to((128, nb, 32)),
                       ALU.mult, deps=[t4, tok_done[0], tok_done[1]], tok=True)
                wp_state["t5_%d" % b] = t5
                last_tp = None
                for ts_ in range(I // 512):
                    bank = psb[banks[ts_ % 2]]
                    sb_ = ts_ % 2
                    for k in range(4):
                        last_tp = E("tensor", "transpose", bank[:, 128 * k:128 * (k + 1)],
                                    wdq[:, 512 * ts_ + 128 * k: 512 * ts_ + 128 * (k + 1)],
                                    identt[:],
                                    deps=[t5, tok_stg[sb_]] if k == 0 else [],
                                    tok=(k == 3))
                    th = E("vector", "tensor_copy", stg_hi[sb_], bank[:],
                           deps=[last_tp, tok_stg[sb_]], tok=True)
                    tl = E("vector", "tensor_sub", stg_lo[sb_], bank[:], stg_hi[sb_],
                           deps=[last_tp, th], tok=True)
                    if ts_ == I // 512 - 1:
                        tok_done[b] = tl
                    dst = slice(512 * ts_, 512 * (ts_ + 1))
                    hi_ap = hi_dst[dst, 128 * och:128 * (och + 1)] \
                        .rearrange("(k q) c -> q k c", q=128)
                    lo_ap = lo_dst[dst, 128 * och:128 * (och + 1)] \
                        .rearrange("(k q) c -> q k c", q=128)
                    tok_stg[sb_] = p.dma("gpsimd", dslot[sb_], [
                        (hi_ap, stg_hi[sb_].rearrange("p (k c) -> p k c", c=128)),
                        (lo_ap, stg_lo[sb_].rearrange("p (k c) -> p k c", c=128))],
                        deps=[th, tl])

        nhi_bf = nhi[:]
        nlo_bf = nlo[:]
        if N_PHASES >= 1:
            pass

        def allreduce_max2(v0_ap, v1_ap, tok_in):
            k = cc_ev[0]
            cc_ev[0] += 1
            tc0 = E("vector", "tensor_copy", ccio[0:1, 0:1], v0_ap,
                    deps=tok_in, tok=True)
            tc1 = E("vector", "tensor_copy", ccio[0:1, 1:2], v1_ap,
                    deps=tok_in, tok=True)
            td = E("gpsimd", "dma_start", out=ccin_d[k][:, :], in_=ccio[0:1, 0:2],
                   deps=[tc0, tc1], dma=True)
            tcc = E("gpsimd", "collective_compute", "AllReduce", ALU.max,
                    RG, [ccin_d[k][:, :]], [ccout_d[k][:, :]],
                    deps=[td], tok=True)
            tr = E("gpsimd", "dma_start", out=ccio[0:1, 2:4], in_=ccout_d[k][:, :],
                   deps=[tcc], dma=True)
            return tr

        def bcast_cols(src_row_ap, n, dst_col0, deps):
            tm = E("tensor", "matmul", psb[6][:, 0:n], onest[:], src_row_ap,
                   start=True, stop=True, deps=deps, tok=True)
            return E("scalar", "activation", colA[:, dst_col0:dst_col0 + n],
                     psb[6][:, 0:n], ACT.Identity, deps=[tm], tok=True)

        last_bcast = [None]

        def bcast_cols2(src_row_ap, n, dst_col0, deps):
            tm = E("tensor", "matmul", psb[6][:, 0:n], onest[:], src_row_ap,
                   start=True, stop=True, deps=list(deps) + [last_bcast[0]],
                   tok=True)
            tev = E("vector", "tensor_copy", colA[:, dst_col0:dst_col0 + n],
                    psb[6][:, 0:n], deps=[tm], tok=True)
            last_bcast[0] = tev
            return tev

        def allred2(tok_deps, ccio_deps):
            k = cc_ev[0]
            cc_ev[0] += 1
            td = p.dma("gpsimd", 6, [(ccin_d[k][:, :], ccio[0:1, 0:2])],
                       deps=list(tok_deps) + list(ccio_deps))
            tcc = E("gpsimd", "collective_compute", "AllReduce", ALU.max,
                    RG, [ccin_d[k][:, :]], [ccout_d[k][:, :]], deps=[td], tok=True)
            return p.dma("gpsimd", 6, [(ccio[0:1, 2:4], ccout_d[k][:, :])],
                         deps=[tcc])

        def ffn_phase(first):
            def xln_ap(c):
                if c < 12:
                    return arA[:, 1024 * c:1024 * (c + 1)]
                return arB[:, 1024 * (c - 12):1024 * (c - 11)]
            tok_xln = []
            tok_absm = []
            junk = arB[:, 4096:5120]
            for c in range(NTC):
                xc = xres[:, 1024 * c:1024 * (c + 1)]
                ts_ = E("vector", "reduce_sum", cA(48 + c), xc, axis=AX.X, tok=True)
                tn = E("scalar", "activation", cA(c), cA(48 + c), ACT.Identity,
                       scale=-1.0 / D, deps=[ts_], tok=True)
                tq = E("scalar", "activation", junk, xc, ACT.Square,
                       bias=cA(c), accum_out=cA(48 + c), deps=[tn], tok=True)
                tr0 = E("scalar", "activation", cA(48 + c), cA(48 + c), ACT.Sqrt,
                        scale=1.0 / D, bias=cA(79), deps=[tq], tok=True)
                tr = E("vector", "reciprocal", cA(16 + c), cA(48 + c),
                       deps=[tr0], tok=True)
                tx = E("vector", "tensor_scalar", xln_ap(c), xc, cA(c), cA(16 + c),
                       ALU.add, ALU.mult, deps=[tn, tr], tok=True)
                tok_xln.append(tx)
                ta = E("vector", "reduce_max", cA(32 + c), xln_ap(c), axis=AX.X,
                       apply_absolute_value=True, deps=[tx], tok=True)
                tok_absm.append(ta)
            if first:
                w_prep(wdown_d, wdhi_d, wdlo_d, D, I3,
                       wn_bufs=[nhi_f[:, 0:3072], nhi_f[:, 3072:6144]],
                       wdq_bufs=[nlo_f[:, 0:3072], nlo_f[:, 3072:6144]],
                       bmb=nlo_f[:, 6144:6528],
                       stgs=[nlo_bf[:, 13056:13568], nlo_bf[:, 13568:14080],
                             nlo_bf[:, 14080:14592], nlo_bf[:, 14592:15104]],
                       banks=(4, 5), dslot=(4, 5))
            rowb = arB[0:1, 4096:6144]
            tdm = p.dma("gpsimd", 7, [(rowb, colA[:, 32:48])], deps=tok_absm)
            trm = E("vector", "reduce_max", ccio[0:1, 0:1], rowb, axis=AX.X,
                    deps=[tdm], tok=True)
            tz = E("vector", "memset", ccio[0:1, 1:2], 0.0, tok=True)
            tr1 = allred2([trm, tz], [])
            ta1 = E("vector", "reciprocal", srow(0), ccio[0:1, 2:3], deps=[tr1], tok=True)
            ta2 = E("vector", "tensor_scalar", srow(1), srow(0), 32767.0, None,
                    ALU.mult, deps=[ta1], tok=True)
            E("vector", "tensor_scalar", srow(2), ccio[0:1, 2:3],
              1.0 / 32767.0, None, ALU.mult, deps=[tr1], tok=True)
            tbc1 = bcast_cols2(srow(1), 1, 64, [ta2])
            if FFN_STAGES <= 1:
                p.barrier()
                return

            ni32 = arB_i32[:, 4096:5120]
            nf32 = arB[:, 5120:6144]
            tok_ev = {}
            tp_done = [None, None]
            nhi3 = nhi[:].rearrange("p (f t) -> p f t", t=2048)
            nlo3 = nlo[:].rearrange("p (f t) -> p f t", t=2048)
            for c in range(NTC):
                t1 = E("scalar", "activation", ni32, xln_ap(c), ACT.Identity,
                       scale=cA(64), deps=[tok_xln[c], tbc1, tok_ev.get(c - 1)],
                       tok=True)
                t2 = E("scalar", "activation", nf32, ni32, ACT.Identity,
                       deps=[t1], tok=True)
                for half in range(2):
                    bank = psb[4 + half]
                    ltp = None
                    for kk in range(4):
                        f = 4 * half + kk
                        ltp = E("tensor", "transpose", bank[:, 128 * kk:128 * (kk + 1)],
                                nf32[:, 128 * f:128 * (f + 1)], identt[:],
                                deps=[t2, tp_done[half]] if kk == 0 else [],
                                tok=(kk == 3))
                    dsth = nhi3[:, 4 * half:4 * half + 4, 128 * c:128 * (c + 1)]
                    dstl = nlo3[:, 4 * half:4 * half + 4, 128 * c:128 * (c + 1)]
                    th = E("vector", "tensor_copy",
                           dsth, bank[:].rearrange("p (k c) -> p k c", c=128),
                           deps=[ltp], tok=True)
                    tl = E("vector", "tensor_tensor",
                           dstl, bank[:].rearrange("p (k c) -> p k c", c=128),
                           dsth, ALU.subtract, deps=[ltp, th], tok=True)
                    tp_done[half] = tl
                    if half == 1:
                        tok_ev[c] = tl
            tok_n = [tp_done[0], tp_done[1], tok_ev[NTC - 1]]
            if FFN_STAGES <= 2:
                p.barrier()
                return

            tmu = E("vector", "memset", cA(71), 0.0, deps=tok_absm, tok=True)
            tmg = E("vector", "memset", cA(72), 0.0, tok=True)
            tok_wload = {}
            tok_mmdone = {}
            tok_evac = {}
            tok_stgout = [None, None, None, None]
            srcs = [(wuphi_d, 0), (wuplo_d, 1), (wghi_d, 2), (wglo_d, 3)]
            NOG = min(12, NOG_LIM)
            stg_ctr = [0]
            tcp = None
            def wprep_unit(nat_d, m, och, b):
                o2 = och % 2
                wnat = arA[:, 10240:11264]
                wdq = arA[:, 11264:12288]
                tokL = p.dma("sync", 6 + (och % 2), [(wnat,
                             nat_d[128 * och:128 * (och + 1), :])])
                w3 = wnat.rearrange("p (g e) -> p g e", e=32)
                bm = arB[:, 5120:5152]
                sA = arB[:, 5184:5216]
                rs = arB[:, 5248:5280]
                E("vector", "reduce_max", bm.rearrange("p (g o) -> p g o", o=1),
                  w3, axis=AX.X, apply_absolute_value=True, deps=[tokL],
                  tok=True)
                E("vector", "tensor_scalar", sA, bm, 1.0 / 7.0, 1e-12,
                  ALU.mult, ALU.max, tok=True)
                E("vector", "reciprocal", rs, sA, tok=True)
                q32 = wnat.bitcast(I32)
                E("vector", "tensor_tensor",
                  q32.rearrange("p (g e) -> p g e", e=32), w3,
                  rs.rearrange("p (g o) -> p g o", o=1).broadcast_to((128, 32, 32)),
                  ALU.mult, tok=True)
                E("vector", "tensor_tensor",
                  wdq.rearrange("p (g e) -> p g e", e=32),
                  q32.rearrange("p (g e) -> p g e", e=32),
                  sA.rearrange("p (g o) -> p g o", o=1).broadcast_to((128, 32, 32)),
                  ALU.mult, tok=True)
                whi3 = arA_bf[:, 8192 * b + 2048 * (2 * m):
                              8192 * b + 2048 * (2 * m + 1)] \
                    .rearrange("p (f c) -> p f c", c=256)
                wlo3 = arA_bf[:, 8192 * b + 2048 * (2 * m + 1):
                              8192 * b + 2048 * (2 * m + 2)] \
                    .rearrange("p (f c) -> p f c", c=256)
                toks = []
                for ts_ in range(2):
                    bank = psb[4 + ts_]
                    ltp = None
                    for kk in range(4):
                        ltp = E("tensor", "transpose",
                                bank[:, 128 * kk:128 * (kk + 1)],
                                wdq[:, 512 * ts_ + 128 * kk:
                                    512 * ts_ + 128 * (kk + 1)], identt[:],
                                tok=(kk == 3))
                    dsth = whi3[:, 4 * ts_:4 * ts_ + 4, 128 * o2:128 * (o2 + 1)]
                    dstl = wlo3[:, 4 * ts_:4 * ts_ + 4, 128 * o2:128 * (o2 + 1)]
                    th = E("vector", "tensor_copy", dsth,
                           bank[:].rearrange("p (k c) -> p k c", c=128),
                           deps=[ltp], tok=True)
                    tl = E("vector", "tensor_tensor", dstl,
                           bank[:].rearrange("p (k c) -> p k c", c=128),
                           dsth, ALU.subtract, deps=[ltp], tok=True)
                    toks += [th, tl]
                return toks

            for og in range(NOG):
                b = og % 2
                dep_reuse = [tok_mmdone.get(og - 2)] + (tok_n if og < 2 else [])
                if first:
                    toks = []
                    for m, nat_d in enumerate((wup_d, wgate_d)):
                        for o2 in range(2):
                            toks += wprep_unit(nat_d, m, 2 * og + o2, b)
                    for m, (hi_d, lo_d) in enumerate(((wuphi_d, wuplo_d),
                                                      (wghi_d, wglo_d))):
                        for s_, dst_d in ((2 * m, hi_d), (2 * m + 1, lo_d)):
                            sap = dst_d[:, 256 * og:256 * (og + 1)] \
                                .rearrange("(f q) c -> q f c", q=128)
                            buf = arA_bf[:, 8192 * b + 2048 * s_:
                                         8192 * b + 2048 * (s_ + 1)] \
                                .rearrange("p (f c) -> p f c", c=256)
                            p.dma("gpsimd", 6, [(sap, buf)])
                    tok_wload[og] = toks
                else:
                    wpairs = []
                    for (src_, s_) in srcs:
                        dst = arA_bf[:, 8192 * b + 2048 * s_:
                                     8192 * b + 2048 * (s_ + 1)] \
                            .rearrange("p (f c) -> p f c", c=256)
                        sap = src_[:, 256 * og:256 * (og + 1)] \
                            .rearrange("(f q) c -> q f c", q=128)
                        wpairs.append((dst, sap))
                    tok_wload[og] = p.dma("sync", 2 + b, wpairs, deps=dep_reuse)
                last_mm = None
                for mat in range(2):
                    wh3 = arA_bf[:, 8192 * b + 2048 * (2 * mat):
                                 8192 * b + 2048 * (2 * mat + 1)] \
                        .rearrange("p (f c) -> p f c", c=256)
                    wl3 = arA_bf[:, 8192 * b + 2048 * (2 * mat + 1):
                                 8192 * b + 2048 * (2 * mat + 2)] \
                        .rearrange("p (f c) -> p f c", c=256)
                    for o2 in range(2):
                        och = 2 * og + o2
                        for tt in range(4):
                            bank_i = tt % 4
                            bank = psb[bank_i]
                            sp = stg_ctr[0] % 4
                            stg_ctr[0] += 1
                            stg = arA[:, 8192 + 512 * sp:8192 + 512 * (sp + 1)]
                            first_ = True
                            for f in range(NFC):
                                whi = wh3[:, f, 128 * o2:128 * (o2 + 1)]
                                wlo = wl3[:, f, 128 * o2:128 * (o2 + 1)]
                                rhs_h = nhi3[:, f, 512 * tt:512 * (tt + 1)]
                                rhs_l = nlo3[:, f, 512 * tt:512 * (tt + 1)]
                                dd = ([tok_wload[og],
                                       tok_evac.get(bank_i)] + tok_n) \
                                    if first_ else []
                                E("tensor", "matmul", bank[:], whi, rhs_h,
                                  start=first_, stop=False, deps=dd)
                                E("tensor", "matmul", bank[:], whi, rhs_l,
                                  start=False, stop=False)
                                last_mm = E("tensor", "matmul", bank[:], wlo, rhs_h,
                                            start=False, stop=(f == NFC - 1),
                                            tok=(f == NFC - 1))
                                first_ = False
                            tr_ = E("vector", "reduce_max", cA(73), bank[:],
                                    axis=AX.X, apply_absolute_value=True,
                                    deps=[last_mm], tok=True)
                            E("vector", "tensor_tensor", cA(71 + mat),
                              cA(71 + mat), cA(73), ALU.max,
                              deps=[tr_, tmu, tmg], tok=True)
                            tcp = E("vector", "tensor_copy", stg, bank[:],
                                    deps=[last_mm, tok_stgout[sp]], tok=True)
                            tok_evac[bank_i] = tcp
                            dst_d = (pu_d if mat == 0 else pg_d)
                            tok_stgout[sp] = p.dma(
                                "gpsimd", sp,
                                [(dst_d[128 * och:128 * (och + 1),
                                        512 * tt:512 * (tt + 1)], stg)],
                                deps=[tcp])
                tok_mmdone[og] = last_mm
            tok_ug_end = [tok_mmdone[NOG - 1], tok_evac.get(0), tok_evac.get(1),
                          tok_evac.get(2), tok_evac.get(3)] + \
                list(tok_stgout)
            if FFN_STAGES <= 2.5:
                p.barrier()
                return
            t7a = p.dma("gpsimd", 7, [(nhi_f[0:1, 1024:1152], cA(71)),
                                      (nhi_f[0:1, 1152:1280], cA(72))],
                        deps=tok_ug_end)
            t7b = t7a
            tru = E("vector", "reduce_max", ccio[0:1, 0:1], nhi_f[0:1, 1024:1152],
                    axis=AX.X, deps=[t7a, tr1], tok=True)
            trg = E("vector", "reduce_max", ccio[0:1, 1:2], nhi_f[0:1, 1152:1280],
                    axis=AX.X, deps=[t7b], tok=True)
            tr2 = allred2([tru, trg], [])
            tb1 = E("vector", "reciprocal", srow(0), ccio[0:1, 2:3], deps=[tr2], tok=True)
            tb2 = E("vector", "tensor_scalar", srow(3), srow(0), 32767.0, None,
                    ALU.mult, deps=[tb1], tok=True)
            tb3 = E("vector", "tensor_scalar", srow(8), ccio[0:1, 2:3], 1.0 / 32767.0,
                    None, ALU.mult, deps=[tr2], tok=True)
            tb4 = E("vector", "tensor_tensor", srow(4), srow(8), srow(2), ALU.mult,
                    deps=[tb3], tok=True)
            tb5 = E("vector", "reciprocal", srow(0), ccio[0:1, 3:4], deps=[tr2, tb2],
                    tok=True)
            tb6 = E("vector", "tensor_scalar", srow(5), srow(0), 32767.0, None,
                    ALU.mult, deps=[tb5], tok=True)
            tb7 = E("vector", "tensor_scalar", srow(8), ccio[0:1, 3:4], 1.0 / 32767.0,
                    None, ALU.mult, deps=[tb5, tb4], tok=True)
            tb8 = E("vector", "tensor_tensor", srow(6), srow(8), srow(2), ALU.mult,
                    deps=[tb7], tok=True)
            tb9 = E("vector", "tensor_copy", srow(9), srow(3), deps=[tb2], tok=True)
            tba = E("vector", "tensor_copy", srow(10), srow(5), deps=[tb6], tok=True)
            tbb = E("vector", "tensor_copy", srow(11), srow(6), deps=[tb8], tok=True)
            tbc2 = bcast_cols2(srow(9, 3), 3, 65, [tb9, tba, tbb])
            if FFN_STAGES <= 3:
                p.barrier()
                return

            tmm = E("vector", "memset", cA(74), 0.0, tok=True)
            tok_use = {}
            tok_m1w = {}
            st_e2 = {}
            def e2_head(i):
                b = i % 4
                pgin = arA[:, 2048 * b:2048 * (b + 1)]
                ngi = arB_i32[:, 2048 * (i % 2):2048 * (i % 2 + 1)]
                sigb = arA[:, 8192 + 2048 * (i % 2):8192 + 2048 * (i % 2 + 1)]
                d0 = (tok_ug_end if i < 2 else [])
                if i == 0:
                    d0 = d0 + p.qtot("gpsimd")
                trd_ = p.dma("sync", i % 4, [(pgin, pg_d[128 * i:128 * (i + 1), :])],
                             deps=d0)
                tg = E("vector", "tensor_scalar", ngi, pgin, cA(66), None,
                       ALU.mult, deps=[trd_, tbc2], tok=True)
                tsg = E("scalar", "activation", sigb, ngi, ACT.Sigmoid,
                        scale=cA(67), deps=[tg], tok=True)
                st_e2[i] = (pgin, ngi, sigb)
            def e2_tail(i):
                pgin, ngi, sigb = st_e2[i]
                m1o = pgin
                ri32 = sigb.bitcast(I32)
                tri = E("vector", "tensor_scalar", ri32, sigb, 65536.0, None,
                        ALU.mult, tok=True)
                tm1 = E("vector", "scalar_tensor_tensor", m1o, ri32, 65535.0,
                        ngi, ALU.min, ALU.mult,
                        deps=[tri, tok_m1w.get(i - 2)], tok=True)
                trd2 = E("vector", "reduce_max", cA(73), m1o, axis=AX.X,
                         apply_absolute_value=True, deps=[tm1], tok=True)
                tok_use[i] = E("vector", "tensor_tensor", cA(74), cA(74),
                               cA(73), ALU.max, deps=[trd2, tmm], tok=True)
                tok_m1w[i] = p.dma("gpsimd", i % 4,
                                   [(m1_d[128 * i:128 * (i + 1), :], m1o)],
                                   deps=[tm1])
            e2_head(0)
            for i in range(1, NIC):
                e2_head(i)
                e2_tail(i - 1)
            e2_tail(NIC - 1)
            t8a = p.dma("gpsimd", 7, [(nhi_f[0:1, 1024:1152], cA(74))],
                        deps=[tok_use[NIC - 1]])
            trm1 = E("vector", "reduce_max", ccio[0:1, 0:1], nhi_f[0:1, 1024:1152],
                     axis=AX.X, deps=[t8a, tr2], tok=True)
            tz3 = E("vector", "memset", ccio[0:1, 1:2], 0.0, deps=[tr2], tok=True)
            tr3 = allred2([trm1, tz3], [])
            tc1 = E("vector", "reciprocal", srow(0), ccio[0:1, 2:3], deps=[tr3], tok=True)
            tc2 = E("vector", "tensor_scalar", srow(12), srow(0), 32767.0, None,
                    ALU.mult, deps=[tc1], tok=True)
            tc3 = E("vector", "tensor_tensor", srow(8), ccio[0:1, 2:3], srow(6),
                    ALU.mult, deps=[tr3, tb8], tok=True)
            tc4 = E("vector", "tensor_scalar", srow(8), srow(8),
                    1.0 / (65536.0 * 32767.0), None, ALU.mult, deps=[tc3], tok=True)
            tc5 = E("vector", "tensor_tensor", srow(13), srow(8), srow(4), ALU.mult,
                    deps=[tc4, tb4], tok=True)
            tbc3 = bcast_cols2(srow(12), 1, 68, [tc2])
            if FFN_STAGES <= 4:
                p.barrier()
                return

            tmo = E("vector", "memset", cA(75), 0.0, deps=[tok_use[NIC - 1]],
                    tok=True)
            tok_use2 = {}
            tok_opw = {}
            st_e3 = {}
            def e3_head(i):
                b = i % 3
                puin = arA[:, 2048 * b:2048 * (b + 1)]
                m1in = arA[:, 6144 + 2048 * b:6144 + 2048 * (b + 1)]
                d0 = [tok_use[NIC - 1]]
                if i == 0:
                    d0 += p.qtot("gpsimd")
                trdu = p.dma("sync", i % 3, [(puin,
                             pu_d[128 * i:128 * (i + 1), :])], deps=d0)
                trdm = p.dma("sync", 3 + i % 3, [(m1in,
                             m1_d[128 * i:128 * (i + 1), :])], deps=d0)
                nupi = arB_i32[:, 2048 * (i % 2):2048 * (i % 2 + 1)]
                ng2i = m1in.bitcast(I32)
                tnu = E("vector", "tensor_scalar", nupi, puin, cA(65), None,
                        ALU.mult, deps=[trdu, tbc2], tok=True)
                tng = E("scalar", "activation", ng2i, m1in, ACT.Identity,
                        scale=cA(68), deps=[trdm, tbc3], tok=True)
                st_e3[i] = (puin, m1in, nupi, ng2i, tnu, tng)
            def e3_tail(i):
                puin, m1in, nupi, ng2i, tnu, tng = st_e3[i]
                opo = puin
                top = E("vector", "tensor_tensor", opo, ng2i, nupi, ALU.mult,
                        deps=[tnu, tng, tok_opw.get(i - 2)], tok=True)
                trd2 = E("vector", "reduce_max", cA(73), opo, axis=AX.X,
                         apply_absolute_value=True, deps=[top], tok=True)
                tok_use2[i] = E("vector", "tensor_tensor", cA(75), cA(75),
                                cA(73), ALU.max, deps=[trd2, tmo], tok=True)
                tok_opw[i] = p.dma("gpsimd", i % 4,
                                   [(op_d[128 * i:128 * (i + 1), :], opo)],
                                   deps=[top])
            e3_head(0)
            for i in range(1, NIC):
                e3_head(i)
                e3_tail(i - 1)
            e3_tail(NIC - 1)
            t9a = p.dma("gpsimd", 7, [(nhi_f[0:1, 1024:1152], cA(75))],
                        deps=[tok_use2[NIC - 1]])
            trop = E("vector", "reduce_max", ccio[0:1, 0:1], nhi_f[0:1, 1024:1152],
                     axis=AX.X, deps=[t9a, tr3], tok=True)
            tz4 = E("vector", "memset", ccio[0:1, 1:2], 0.0, deps=[tr3], tok=True)
            tr4 = allred2([trop, tz4], [])
            td1 = E("vector", "reciprocal", srow(0), ccio[0:1, 2:3], deps=[tr4], tok=True)
            td2 = E("vector", "tensor_scalar", srow(16), srow(0), 32767.0, None,
                    ALU.mult, deps=[td1], tok=True)
            td3 = E("vector", "tensor_scalar", srow(8), ccio[0:1, 2:3], 0.5 / 32767.0,
                    None, ALU.mult, deps=[tr4], tok=True)
            td4 = E("vector", "tensor_tensor", srow(17), srow(8), srow(13), ALU.mult,
                    deps=[td3, tc5, td2], tok=True)
            tbc4 = bcast_cols2(srow(16, 2), 2, 69, [td4])
            if FFN_STAGES <= 5:
                p.barrier()
                return

            tok_use3 = {}
            tok_w3 = {}
            tok_w3b = {}
            st_e4 = {}
            def e4_head(i):
                b = i % 4
                opin = arA[:, 2048 * b:2048 * (b + 1)]
                noi = arB_i32[:, 2048 * (i % 2):2048 * (i % 2 + 1)]
                hib = arA_bf[:, 16384 + 2048 * (i % 2):16384 + 2048 * (i % 2 + 1)]
                d0 = [tok_use2[NIC - 1]]
                if i == 0:
                    d0 += p.qtot("gpsimd")
                trdo = p.dma("sync", i % 4, [(opin,
                             op_d[128 * i:128 * (i + 1), :])], deps=d0)
                tno = E("scalar", "activation", noi, opin, ACT.Identity,
                        scale=cA(69), deps=[trdo, tbc4], tok=True)
                thi = E("scalar", "activation", hib, noi, ACT.Identity,
                        deps=[tno, tok_w3.get(i - 2)], tok=True)
                st_e4[i] = (noi, hib, thi)
            def e4_tail(i):
                b = i % 2
                noi, hib, thi = st_e4[i]
                lob = arA_bf[:, 20480 + 2048 * b:20480 + 2048 * (b + 1)]
                tlo = E("vector", "tensor_tensor", lob, noi, hib, ALU.subtract,
                        deps=[thi, tok_w3b.get(i - 2)], tok=True)
                tok_use3[i] = tlo
                tok_w3[i] = p.dma("gpsimd", i % 4,
                                  [(nohi_d[128 * i:128 * (i + 1), :], hib),
                                   (nolo_d[128 * i:128 * (i + 1), :], lob)],
                                  deps=[thi, tlo])
                tok_w3b[i] = tok_w3[i]
            e4_head(0)
            for i in range(1, NIC):
                e4_head(i)
                e4_tail(i - 1)
            e4_tail(NIC - 1)
            tok_ld = {}
            tok_mmd = {}
            ev_last = {}
            for tg_ in range(4):
                for ich in range(NIC):
                    b = ich % 4
                    nhib = arA_bf[:, 512 * b:512 * (b + 1)]
                    nlob = arA_bf[:, 2048 + 512 * b:2048 + 512 * (b + 1)]
                    wdh = arA_bf[:, 4096 + 1024 * b:4096 + 1024 * (b + 1)]
                    wdl = arA_bf[:, 8192 + 1024 * b:8192 + 1024 * (b + 1)]
                    dep0 = [tok_mmd.get((tg_, ich - 4)),
                            tok_mmd.get((tg_ - 1, NIC - 1))]
                    if tg_ == 0 and ich < 2:
                        dep0 += [tok_w3.get(NIC - 1), tok_w3b.get(NIC - 1),
                                 tok_use3.get(NIC - 1)] + p.qtot("gpsimd")
                    t1 = p.dma("sync", b, [
                        (nhib, nohi_d[128 * ich:128 * (ich + 1),
                                      512 * tg_:512 * (tg_ + 1)]),
                        (nlob, nolo_d[128 * ich:128 * (ich + 1),
                                      512 * tg_:512 * (tg_ + 1)])],
                        deps=dep0)
                    t1b = p.dma("sync", 4 + b, [
                        (wdh, wdhi_d[128 * ich:128 * (ich + 1), :]),
                        (wdl, wdlo_d[128 * ich:128 * (ich + 1), :])],
                        deps=dep0)
                    lm = None
                    for tc4_ in range(4):
                        for ht in range(2):
                            bank = psb[2 * tc4_ + ht]
                            first = (ich == 0)
                            last = (ich == NIC - 1)
                            dd = [t1, t1b]
                            if first:
                                dd += [ev_last.get(tg_ - 1), last_bcast[0]]
                            lh = nhib[:, 128 * tc4_:128 * (tc4_ + 1)]
                            ll = nlob[:, 128 * tc4_:128 * (tc4_ + 1)]
                            rh = wdh[:, 512 * ht:512 * (ht + 1)]
                            rl = wdl[:, 512 * ht:512 * (ht + 1)]
                            E("tensor", "matmul", bank[:], lh, rh,
                              start=first, stop=False, deps=dd)
                            E("tensor", "matmul", bank[:], lh, rl,
                              start=False, stop=False)
                            lm = E("tensor", "matmul", bank[:], ll, rh,
                                   start=False, stop=last,
                                   tok=(ht == 1 and tc4_ == 3))
                    tok_mmd[(tg_, ich)] = lm
                for tc4_ in range(4):
                    c = 4 * tg_ + tc4_
                    for ht in range(2):
                        bank = psb[2 * tc4_ + ht]
                        ev_last[tg_] = E(
                            "vector", "scalar_tensor_tensor",
                            xres[:, 1024 * c + 512 * ht:1024 * c + 512 * (ht + 1)],
                            bank[:], cA(70),
                            xres[:, 1024 * c + 512 * ht:1024 * c + 512 * (ht + 1)],
                            ALU.mult, ALU.add,
                            deps=[tok_mmd[(tg_, NIC - 1)], tbc4], tok=True)
            p.barrier()


        def attn_phase():
            defq = []

            def drain(n):
                for _ in range(min(n, len(defq))):
                    defq.pop(0)()

            masks = arA[:, 0:2048]
            gamma = arA[:, 3072:4096]
            nhi_b = nhi[:]
            nlo_b = nlo[:]
            qh_b = nhi_b[:, 0:4096]
            ql_b = nhi_b[:, 4096:8192]
            kh_b = nhi_b[:, 8192:12288]
            kl_b = nhi_b[:, 12288:16384]
            vtmh = nlo_b[:, 0:4112]
            vtml = nlo_b[:, 4112:8224]
            vfm = nlo_f[:, 4112:6160]
            cth_b = [nlo_b[:, 12320:12832], nlo_b[:, 13344:13856]]
            ctl_b = [nlo_b[:, 12832:13344], nlo_b[:, 13856:14368]]
            htm = arA[:, 5120:5376]
            htm4 = [nlo_f[:, 4112 + 256 * i:4112 + 256 * (i + 1)] for i in range(4)]
            junk = arA[:, 12032:12288]
            mm_slots = [arA[:, 4096:4608], arA[:, 4608:5120],
                        arA[:, 6144:6656], arA[:, 6656:7168]]
            dsb_s = [arA[:, 2048:2560], arA[:, 2560:3072]]

            def xln_ap(c):
                if c < 12:
                    return arA[:, 1024 * c:1024 * (c + 1)]
                return arB[:, 1024 * (c - 12):1024 * (c - 11)]
            tok_xln = []
            junk0 = nhi_f[:, 0:1024]
            for c in range(NTC):
                xc = xres[:, 1024 * c:1024 * (c + 1)]
                ts_ = E("vector", "reduce_sum", cA(48 + c), xc, axis=AX.X, tok=True)
                tn = E("scalar", "activation", cA(c), cA(48 + c), ACT.Identity,
                       scale=-1.0 / D, deps=[ts_], tok=True)
                tq = E("scalar", "activation", junk0, xc, ACT.Square,
                       bias=cA(c), accum_out=cA(48 + c), deps=[tn], tok=True)
                tr0 = E("scalar", "activation", cA(48 + c), cA(48 + c), ACT.Sqrt,
                        scale=1.0 / D, bias=cA(79), deps=[tq], tok=True)
                tr = E("vector", "reciprocal", cA(16 + c), cA(48 + c),
                       deps=[tr0], tok=True)
                tx = E("vector", "tensor_scalar", xln_ap(c), xc, cA(c), cA(16 + c),
                       ALU.add, ALU.mult, deps=[tn, tr], tok=True)
                tok_xln.append(tx)

            stgA = [arB[:, 4096:4608], arB[:, 4608:5120]]
            for c in range(NTC):
                for half in range(2):
                    bank = psb[4 + half]
                    ltp = None
                    for kk in range(4):
                        f = 4 * half + kk
                        ltp = E("tensor", "transpose",
                                bank[:, 128 * kk:128 * (kk + 1)],
                                xln_ap(c)[:, 128 * f:128 * (f + 1)], identt[:],
                                deps=[tok_xln[c]] if kk == 0 else [], tok=(kk == 3))
                    tev = E("vector", "tensor_copy", stgA[half], bank[:],
                            deps=[ltp], tok=True)
                    dst = xfm_d[512 * half:512 * (half + 1),
                                128 * c:128 * (c + 1)] \
                        .rearrange("(k q) c -> q k c", q=128)
                    p.dma("gpsimd", half, [
                        (dst, stgA[half].rearrange("p (k c) -> p k c", c=128))],
                        deps=[tev])

            wstg = [gcols[:, 128 + 32 * kk:160 + 32 * kk] for kk in range(2)]
            tmp32 = arB[:, 5120:5152]
            tmp32b = arB[:, 5152:5184]
            for kk, wsrc in enumerate((wit_d, wft_d)):
                t1 = p.dma("sync", 4, [
                    (wstg[kk], wsrc[0:1024, :].rearrange("(f q) c -> q f c", q=128)),
                    (tmp32, wsrc[1024:2048, :].rearrange("(f q) c -> q f c", q=128)),
                    (tmp32b, wsrc[2048:3072, :].rearrange("(f q) c -> q f c", q=128))])
                E("vector", "tensor_tensor", wstg[kk], wstg[kk], tmp32, ALU.add,
                  deps=[t1], tok=True)
                E("vector", "tensor_tensor", wstg[kk], wstg[kk], tmp32b, ALU.add,
                  tok=True)
            tbias = p.dma("sync", 5, [(colA[0:4, 76:77], bi_d[:, :]),
                                      (colA[0:4, 77:78], bf_d[:, :]),
                                      (scrow[0:1, 32:33], lnw_d[:, 0:1])])
            trow = p.dma("sync", 5, [(nhi_f[0:1, 1024:2048], lnw_d[:, :])])
            ig_sb = arB[0:4, 0:2048]
            fg_sb = arB[0:4, 2048:4096]
            xchunk = [arA[:, 7168:9216], arA[:, 9216:11264]]
            gmm = {}
            for f in range(NFC):
                tl = p.dma("sync", 6, [(xchunk[f % 2],
                                        xfm_d[128 * f:128 * (f + 1), :])])
                for kk in range(2):
                    w3 = wstg[kk].rearrange("p (f c) -> p f c", c=4)
                    for tt in range(4):
                        gmm[(kk, tt)] = E(
                            "tensor", "matmul", psb[4 * kk + tt][0:4, 0:512],
                            w3[:, f, :], xchunk[f % 2][:, 512 * tt:512 * (tt + 1)],
                            start=(f == 0), stop=(f == NFC - 1),
                            deps=[tl], tok=(f == NFC - 1))
            for kk in range(2):
                dstsb = (ig_sb, fg_sb)[kk]
                for tt in range(4):
                    E("vector", "tensor_copy",
                      dstsb[:, 512 * tt:512 * (tt + 1)],
                      psb[4 * kk + tt][0:4, 0:512],
                      deps=[gmm[(kk, tt)]], tok=True)
            E("vector", "tensor_scalar", ig_sb, ig_sb, colA[0:4, 76:77], None,
              ALU.add, deps=[tbias], tok=True)
            E("vector", "tensor_scalar", fg_sb, fg_sb, colA[0:4, 77:78], None,
              ALU.add, deps=[tbias], tok=True)

            scr = arB[0:4, 4096:6144]
            E("scalar", "activation", scr, fg_sb, ACT.Abs, tok=True)
            E("scalar", "activation", scr, scr, ACT.Exp, scale=-1.0, tok=True)
            E("scalar", "activation", scr, scr, ACT.Ln, bias=colA[0:4, 78:79],
              tok=True)
            E("vector", "tensor_scalar", fg_sb, fg_sb, 0.0, None, ALU.min,
              tok=True)
            E("vector", "tensor_tensor", fg_sb, fg_sb, scr, ALU.subtract,
              tok=True)
            E("vector", "tensor_tensor_scan", scr, fg_sb, fg_sb, 0.0,
              ALU.add, ALU.bypass, tok=True)
            E("vector", "tensor_tensor", ig_sb, ig_sb, scr, ALU.subtract,
              tok=True)
            E("vector", "tensor_tensor_scan", fg_sb, ig_sb, ig_sb, -3e38,
              ALU.max, ALU.bypass, tok=True)
            E("vector", "tensor_tensor", scr, scr, fg_sb, ALU.add, tok=True)
            E("scalar", "activation", scr, scr, ACT.Exp, scale=-1.0, tok=True)
            b_sb, M_sb, em_sb = ig_sb, fg_sb, scr

            for kk, (src_, dst_) in enumerate(((b_sb, gcols[:, 0:64]),
                                               (em_sb, gcols[:, 64:128]))):
                for grp in range(4):
                    bank = psb[1]
                    for s4 in range(4):
                        s16 = 4 * grp + s4
                        E("tensor", "transpose", bank[:, 4 * s4:4 * s4 + 4],
                          src_[:, 128 * s16:128 * (s16 + 1)],
                          identt[0:4, 0:4], tok=True)
                    E("vector", "tensor_copy",
                      dst_[:, 16 * grp:16 * (grp + 1)], bank[:, 0:16],
                      tok=True)

            tpl = E("vector", "tensor_scalar", nhi_f[0:1, 1024:2048],
                    nhi_f[0:1, 1024:2048], 1.0, None, ALU.add, deps=[trow],
                    tok=True)
            for half in range(2):
                tm = E("tensor", "matmul", psb[0][:, 0:512], onest[:],
                       nhi_f[0:1, 1024 + 512 * half:1024 + 512 * (half + 1)],
                       start=True, stop=True, deps=[tpl], tok=True)
                E("vector", "tensor_copy", gamma[:, 512 * half:512 * (half + 1)],
                  psb[0][:], deps=[tm], tok=True)
            p.dma("sync", 7, [(masks, maskd_d[:, :])])

            for h in range(NH):
                bd = arA[:, 11264:12032]
                bdq = bd[:, 0:256]
                bdk = bd[:, 256:512]
                bdv = bd[:, 512:768]
                xfm = arA[:, 7168:11264]
                p.dma("sync", 0, [
                    (bdq.rearrange("p (f c) -> p f c", c=128),
                     bdq_d[2 * h:2 * h + 2, :, :].rearrange("f q c -> q f c")),
                    (bdk.rearrange("p (f c) -> p f c", c=128),
                     bdk_d[2 * h:2 * h + 2, :, :].rearrange("f q c -> q f c")),
                    (bdv.rearrange("p (f c) -> p f c", c=128),
                     bdv_d[2 * h:2 * h + 2, :, :].rearrange("f q c -> q f c"))])
                p.dma("sync", 1, [
                    (xfm.rearrange("p (f c) -> p f c", c=2048),
                     xfm_d[256 * h:256 * (h + 1), :]
                     .rearrange("(f q) c -> q f c", q=128))])
                qkv_ctr = [0]
                for proj, bdw in enumerate((bdq, bdk, bdv)):
                    if proj == 2:
                        drain(len(defq))
                    for fc in range(2):
                        for tt in range(4):
                            qb = psb[qkv_ctr[0] % 2]
                            qkv_ctr[0] += 1
                            tm = E("tensor", "matmul", qb[:],
                                   bdw[:, 128 * fc:128 * (fc + 1)],
                                   xfm[:, 2048 * fc + 512 * tt:
                                       2048 * fc + 512 * (tt + 1)],
                                   start=True, stop=True, tok=True)
                            lo_col = 2048 * fc + 512 * tt
                            if proj == 2:
                                E("vector", "tensor_copy",
                                  vfm[:, 512 * tt:512 * (tt + 1)], qb[:],
                                  deps=[tm], tok=True)
                            elif proj == 0:
                                dh_ = qh_b[:, lo_col:lo_col + 512]
                                dl_ = ql_b[:, lo_col:lo_col + 512]
                                E("vector", "tensor_copy", dh_, qb[:],
                                  deps=[tm], tok=True)
                                E("vector", "tensor_tensor", dl_, qb[:],
                                  dh_, ALU.subtract, tok=True)
                            else:
                                dh_ = kh_b[:, lo_col:lo_col + 512]
                                dl_ = kl_b[:, lo_col:lo_col + 512]
                                E("vector", "tensor_scalar", dh_, qb[:],
                                  0.0625, None, ALU.mult, deps=[tm], tok=True)
                                E("vector", "scalar_tensor_tensor", dl_,
                                  qb[:], 0.0625, dh_, ALU.mult, ALU.subtract,
                                  tok=True)
                        if proj == 2:
                            for grp in range(4):
                                bank = psb[2 + grp % 2]
                                ltp = None
                                for kk in range(4):
                                    jt = 4 * grp + kk
                                    ltp = E("tensor", "transpose",
                                            bank[:, 128 * kk:128 * (kk + 1)],
                                            vfm[:, 128 * jt:128 * (jt + 1)],
                                            identt[:], tok=(kk == 3))
                                dsth = vtmh.rearrange("p (j c) -> p j c", c=257) \
                                    [:, 4 * grp:4 * grp + 4, 128 * fc:128 * (fc + 1)]
                                dstl = vtml.rearrange("p (j c) -> p j c", c=257) \
                                    [:, 4 * grp:4 * grp + 4, 128 * fc:128 * (fc + 1)]
                                E("vector", "tensor_copy", dsth,
                                  bank[:].rearrange("p (k c) -> p k c", c=128),
                                  deps=[ltp], tok=True)
                                E("vector", "tensor_tensor", dstl,
                                  bank[:].rearrange("p (k c) -> p k c", c=128),
                                  dsth, ALU.subtract, tok=True)
                E("vector", "memset",
                  vtmh.rearrange("p (j c) -> p j c", c=257)[:, :, 256:257], 1.0,
                  tok=True)
                E("vector", "memset",
                  vtml.rearrange("p (j c) -> p j c", c=257)[:, :, 256:257], 0.0,
                  tok=True)
                qh3 = qh_b.rearrange("p (f t) -> p f t", t=2048)
                ql3 = ql_b.rearrange("p (f t) -> p f t", t=2048)
                kh3 = kh_b.rearrange("p (f t) -> p f t", t=2048)
                kl3 = kl_b.rearrange("p (f t) -> p f t", t=2048)

                def emit_ST(it, jc):
                    stb = psb[6 + (jc % 2)]
                    first = True
                    lmm = None
                    for fc in range(2):
                        lh = kh3[:, fc, 128 * jc:128 * (jc + 1)]
                        ll = kl3[:, fc, 128 * jc:128 * (jc + 1)]
                        rh = qh3[:, fc, 512 * it:512 * (it + 1)]
                        rl = ql3[:, fc, 512 * it:512 * (it + 1)]
                        E("tensor", "matmul", stb[:], lh, rh,
                          start=first, stop=False)
                        E("tensor", "matmul", stb[:], lh, rl,
                          start=False, stop=False)
                        lmm = E("tensor", "matmul", stb[:], ll, rh,
                                start=False, stop=(fc == 1), tok=(fc == 1))
                        first = False
                    return stb, lmm

                def emit_D_CT(it, jc, stb, lmm):
                    dk = jc - 4 * it
                    esrc = mm_slots[dk] if dk >= 0 else arA[:, 5632:6144]
                    Dsb = dsb_s[jc % 2]
                    E("scalar", "activation", Dsb, esrc, ACT.Exp,
                      scale=-1.0, bias=gcols[:, 4 * jc + h:4 * jc + h + 1],
                      tok=True)
                    E("vector", "tensor_tensor", Dsb, stb[:], Dsb,
                      ALU.mult, deps=[lmm], tok=True)
                    cth = cth_b[jc % 2]
                    ctl = ctl_b[jc % 2]
                    E("gpsimd", "tensor_copy", cth, Dsb, tok=True)
                    E("vector", "tensor_tensor", ctl, Dsb, cth,
                      ALU.subtract, tok=True)
                    return cth, ctl

                def emit_CV(it, jc, cth, ctl):
                    vh = vtmh.rearrange("p (j c) -> p j c", c=257)[:, jc, :]
                    vl = vtml.rearrange("p (j c) -> p j c", c=257)[:, jc, :]
                    for i1 in range(4):
                        if jc > 4 * it + i1:
                            continue
                        cvb = psb[2 + i1]
                        first = (jc == 0)
                        last = (jc == 4 * it + i1)
                        E("tensor", "matmul", cvb[:, 0:257],
                          cth[:, 128 * i1:128 * (i1 + 1)], vh,
                          start=first, stop=False)
                        E("tensor", "matmul", cvb[:, 0:257],
                          cth[:, 128 * i1:128 * (i1 + 1)], vl,
                          start=False, stop=False)
                        E("tensor", "matmul", cvb[:, 0:257],
                          ctl[:, 128 * i1:128 * (i1 + 1)], vh,
                          start=False, stop=last, tok=True)

                def emit_phase1(it):
                    for i1 in range(4):
                        cvb = psb[2 + i1]
                        i16 = 4 * it + i1
                        c0, c1 = 60 + 2 * i1, 61 + 2 * i1
                        E("vector", "reduce_max", cA(c0),
                          cvb[:, 256:257], axis=AX.X,
                          apply_absolute_value=True, tok=True)
                        E("vector", "tensor_tensor", cA(c0), cA(c0),
                          gcols[:, 64 + 4 * i16 + h:64 + 4 * i16 + h + 1],
                          ALU.max, tok=True)
                        E("vector", "tensor_scalar", cA(c0), cA(c0), 1e-8,
                          None, ALU.add, tok=True)
                        E("vector", "reciprocal", cA(c1), cA(c0), tok=True)
                        E("vector", "tensor_scalar", htm4[i1], cvb[:, 0:256],
                          cA(c1), None, ALU.mult, tok=True)

                def defer_mhln(it):
                    for i1 in range(4):
                        c0, c1 = 60 + 2 * i1, 61 + 2 * i1
                        hh = htm4[i1]
                        defq.extend([
                            (lambda hh=hh, c0=c0: E("vector", "reduce_sum",
                             cA(c0), hh, axis=AX.X, tok=True)),
                            (lambda c0=c0: E("vector", "tensor_scalar",
                             cA(c0), cA(c0), -1.0 / DH, None, ALU.mult,
                             tok=True)),
                            (lambda hh=hh, c0=c0: E("vector", "tensor_scalar",
                             hh, hh, cA(c0), None, ALU.add, tok=True)),
                            (lambda hh=hh: E("vector", "tensor_tensor",
                             junk, hh, hh, ALU.mult, tok=True)),
                            (lambda c1=c1: E("vector", "reduce_sum", cA(c1),
                             junk, axis=AX.X, tok=True)),
                        ])
                    def sqrt4():
                        for i1 in range(4):
                            E("scalar", "activation", cA(61 + 2 * i1),
                              cA(61 + 2 * i1), ACT.Sqrt, scale=1.0 / DH,
                              bias=cA(79), tok=True)
                    defq.append(sqrt4)
                    for i1 in range(4):
                        i16 = 4 * it + i1
                        c1 = 61 + 2 * i1
                        hh = htm4[i1]
                        xsl = xres[:, 1024 * i16 + 256 * h:
                                   1024 * i16 + 256 * (h + 1)]
                        defq.extend([
                            (lambda c1=c1: E("vector", "reciprocal", cA(c1),
                             cA(c1), tok=True)),
                            (lambda hh=hh, c1=c1: E("vector",
                             "scalar_tensor_tensor", hh, hh, cA(c1),
                             gamma[:, 256 * h:256 * (h + 1)], ALU.mult,
                             ALU.mult, tok=True)),
                            (lambda hh=hh, xsl=xsl: E("vector",
                             "tensor_tensor", xsl, xsl, hh, ALU.add,
                             tok=True)),
                        ])

                def defer_masks(it):
                    for dk in range(4):
                        defq.append(lambda dk=dk: E(
                            "vector", "tensor_tensor", mm_slots[dk],
                            arA[:, 5632:6144],
                            masks[:, 512 * dk:512 * (dk + 1)], ALU.add,
                            tok=True))

                for it in range(4):
                    tm = E("tensor", "matmul", psb[0][:],
                           sel4t[:, 128 * h:128 * (h + 1)],
                           M_sb[:, 512 * it:512 * (it + 1)],
                           start=True, stop=True, tok=True)
                    E("vector", "tensor_copy", arA[:, 5632:6144], psb[0][:],
                      deps=[tm], tok=True)
                    if it == 0:
                        drain(len(defq))
                        defer_masks(it)
                        drain(4)
                    else:
                        drain(len(defq))
                        emit_phase1(it - 1)
                        defer_masks(it)
                        defer_mhln(it - 1)
                    njc = 4 * it + 4
                    stA = emit_ST(it, 0)
                    pend = (0, stA)
                    for jc in range(1, njc + 1):
                        pj, (stb, lmm) = pend
                        cth, ctl = emit_D_CT(it, pj, stb, lmm)
                        drain(3)
                        if jc <= njc - 1:
                            pend = (jc, emit_ST(it, jc))
                        emit_CV(it, pj, cth, ctl)
                    if it == 3:
                        drain(len(defq))
                        emit_phase1(3)
                        defer_mhln(3)
                        if h == NH - 1:
                            drain(len(defq))
            p.barrier()

        if N_PHASES >= 2:
            ffn_phase(True)
        if N_PHASES >= 3:
            attn_phase()
        if N_PHASES >= 4:
            ffn_phase(False)

        p.barrier()
        p.dma("gpsimd", 0, [(out_d[128 * c:128 * (c + 1), :],
                             xres[:, 1024 * c:1024 * (c + 1)])
                            for c in range(NTC)])
        E("gpsimd", "nop", deps=p.qtot("gpsimd"))

        p.finalize()
        block = es.enter_context(nc.Block())

        @block.sync
        def _(e):
            for fn in p.ops["sync"]:
                fn(e)

        @block.vector
        def _(e):
            for fn in p.ops["vector"]:
                fn(e)

        @block.scalar
        def _(e):
            for fn in p.ops["scalar"]:
                fn(e)

        @block.tensor
        def _(e):
            for fn in p.ops["tensor"]:
                fn(e)

        @block.gpsimd
        def _(e):
            for fn in p.ops["gpsimd"]:
                fn(e)

    return nc



def _bd_chunks(w):
    bd = np.zeros((NFC, 128, 128), dtype=np.float32)
    for c in range(NFC):
        for g in range(32):
            blk = w[32 * c + g]
            bd[c, 4 * g:4 * g + 4, 4 * g:4 * g + 4] = blk.T
    return bd


def _host_consts():
    ident = np.eye(128, dtype=np.float32)
    maskd = np.zeros((128, 2048), dtype=np.float32)
    pp = np.arange(128)[:, None]
    cc = np.arange(512)[None, :]
    for k in range(4):
        maskd[:, 512 * k:512 * (k + 1)] = np.where(128 * k + pp > cc, 1e30, 0.0)
    onesk = np.ones((1, 128), dtype=np.float32)
    sel4 = np.zeros((4, 512), dtype=np.float32)
    for h in range(4):
        sel4[h, 128 * h:128 * (h + 1)] = 1.0
    return ident, maskd, onesk, sel4


def kernel(L0_attn_x, L0_attn_wq, L0_attn_wk, L0_attn_wv, L0_attn_wi,
           L0_attn_bi, L0_attn_wf, L0_attn_bf, L0_attn_ln_w,
           L0_ffn_w_gate, L0_ffn_w_up, L0_ffn_w_down):
    x = np.ascontiguousarray(np.asarray(L0_attn_x, dtype=np.float32))
    ident, maskd, onesk, sel4 = _host_consts()
    common = {
        "wup": np.ascontiguousarray(np.asarray(L0_ffn_w_up, np.float32)),
        "wgate": np.ascontiguousarray(np.asarray(L0_ffn_w_gate, np.float32)),
        "wdown": np.ascontiguousarray(np.asarray(L0_ffn_w_down, np.float32)),
        "bdq": _bd_chunks(np.asarray(L0_attn_wq, np.float32)),
        "bdk": _bd_chunks(np.asarray(L0_attn_wk, np.float32)),
        "bdv": _bd_chunks(np.asarray(L0_attn_wv, np.float32)),
        "wit": np.ascontiguousarray(np.asarray(L0_attn_wi, np.float32).T),
        "wft": np.ascontiguousarray(np.asarray(L0_attn_wf, np.float32).T),
        "bi": np.ascontiguousarray(np.asarray(L0_attn_bi, np.float32).reshape(4, 1)),
        "bf4": np.ascontiguousarray(np.asarray(L0_attn_bf, np.float32).reshape(4, 1)),
        "lnw": np.ascontiguousarray(np.asarray(L0_attn_ln_w, np.float32).reshape(1, D)),
        "ident": ident, "maskd": maskd, "onesk": onesk, "sel4": sel4,
    }
    nc = build_program()
    in_maps = [dict(common, x=np.ascontiguousarray(x[c])) for c in range(8)]
    res = run_bass_kernel_spmd(nc, in_maps, core_ids=list(range(8)))
    out = np.stack([res.results[c]["out"] for c in range(8)], axis=0)
    return out.astype(np.float32)


# version 15 (final)
# speedup vs baseline: 1.0601x; 1.0053x over previous
import numpy as np
from contextlib import ExitStack

import concourse.bass as bass
import concourse.mybir as mybir
from concourse.bass_utils import run_bass_kernel_spmd

F32 = mybir.dt.float32
F32R = mybir.dt.float32r
BF16 = mybir.dt.bfloat16
I32 = mybir.dt.int32
ALU = mybir.AluOpType
ACT = mybir.ActivationFunctionType
AX = mybir.AxisListType

ENGINES = ("sync", "vector", "scalar", "tensor", "gpsimd")
RG = [[0, 1, 2, 3, 4, 5, 6, 7]]

S, D, NH, DH, I3 = 2048, 1024, 4, 256, 3072
NTC = S // 128
NFC = D // 128
NIC = I3 // 128

N_PHASES = 99
FFN_STAGES = 99
NOG_LIM = 99


class Prog:

    N_DMA = {"sync": 8, "gpsimd": 8, "scalar": 2}

    def __init__(self, nc, es):
        self.nc = nc
        self.sems = {}
        for name in ("sv", "ss", "st_", "sg", "sy"):
            self.sems[name] = [es.enter_context(nc.semaphore(name)), 0]
        self.own = {"vector": "sv", "scalar": "ss", "tensor": "st_",
                    "gpsimd": "sg", "sync": "sy"}
        for eng, n in self.N_DMA.items():
            for i in range(n):
                name = "q_%s%d" % (eng, i)
                self.sems[name] = [es.enter_context(nc.semaphore(name)), 0]
        self.N_TOK = 8
        for e in ENGINES:
            for i in range(self.N_TOK):
                nm = "t_%s%d" % (e, i)
                self.sems[nm] = [es.enter_context(nc.semaphore(nm)), 0]
        self.tok_rr = {e: 0 for e in ENGINES}
        self.ops = {e: [] for e in ENGINES}
        self.guar = {e: {} for e in ENGINES}
        self.lastw = {}
        self.lastr = {}
        self.rec = []

    @staticmethod
    def _is_ap(a):
        return hasattr(a, "tensor") and hasattr(a, "offset") and hasattr(a, "ap")

    @staticmethod
    def _rng(ap):
        t = ap.tensor
        import concourse.mybir as _mb
        esz = _mb.dt.size(ap.dtype)
        shape = list(t.shape)
        row = shape[-1] if len(shape) > 1 else shape[0]
        tesz = _mb.dt.size(t.dtype)
        row_bytes = row * tesz
        off_b = ap.offset * esz
        col_lo = off_b % row_bytes if row_bytes else 0
        span = 0
        for step, count in list(ap.ap):
            sb = abs(int(step)) * esz
            if row_bytes and sb % row_bytes == 0 and sb != 0:
                continue
            span += (int(count) - 1) * sb
        return (t.name, col_lo, col_lo + span + esz)

    _WRITES = {
        "tensor_scalar": [0], "tensor_tensor": [0], "tensor_copy": [0],
        "tensor_sub": [0], "tensor_add": [0], "tensor_mul": [0],
        "tensor_max": [0], "scalar_tensor_tensor": [0],
        "tensor_tensor_scan": [0], "reduce_max": [0], "reduce_sum": [0],
        "tensor_reduce": [0], "reciprocal": [0], "memset": [0],
        "activation": [0], "transpose": [0], "matmul": [0], "iota": [0],
    }

    def _autodeps(self, eng, meth, args, kw, tokval_next):
        reads, writes = [], []
        wpos = self._WRITES.get(meth, [])
        for i, a in enumerate(args):
            if self._is_ap(a):
                (writes if i in wpos else reads).append(a)
        for k, a in kw.items():
            if self._is_ap(a):
                if k in ("out", "accum_out"):
                    writes.append(a)
                else:
                    reads.append(a)
        deps = []
        psum_pe = (eng == "tensor" and meth == "matmul"
                   and kw.get("start") is False)
        for a in reads:
            r = self._rng(a)
            for (lo, hi, tk, teng) in self.lastw.get(r[0], []):
                if hi > r[1] and r[2] > lo:
                    if psum_pe and teng == "tensor" and r[0].startswith("pb"):
                        continue
                    deps.append(tk)
        for a in writes:
            r = self._rng(a)
            for m in (self.lastw, self.lastr):
                for (lo, hi, tk, teng) in m.get(r[0], []):
                    if hi > r[1] and r[2] > lo:
                        if psum_pe and teng == "tensor" and r[0].startswith("pb"):
                            continue
                        deps.append(tk)
        return deps, reads, writes

    def _note(self, eng, reads, writes, tokval):
        for a in writes:
            nm, lo, hi = self._rng(a)
            lst = self.lastw.setdefault(nm, [])
            lst[:] = [e for e in lst if not (lo <= e[0] and e[1] <= hi)]
            lst.append((lo, hi, tokval, eng))
            rl = self.lastr.setdefault(nm, [])
            rl[:] = [e for e in rl if not (lo <= e[0] and e[1] <= hi)]
        for a in reads:
            nm, lo, hi = self._rng(a)
            self.lastr.setdefault(nm, []).append((lo, hi, tokval, eng))

    @staticmethod
    def _flat(deps):
        out = []
        for d in deps:
            if d is None:
                continue
            if isinstance(d, list):
                out.extend(x for x in d if x is not None)
            else:
                out.append(d)
        return out

    def emit(self, eng, meth, *args, deps=(), tok=False, _inc=None,
             note_tok=None, _skip=None, **kw):
        if eng == "tensor" and meth == "nop":
            ps1, id1 = self.tensor_nop_args
            meth = "transpose"
            args = (ps1, id1, id1)
        adeps, rds, wrs = self._autodeps(eng, meth, args, kw, None)
        oid = len(self.rec)
        all_deps = []
        for d in self._flat(list(deps) + adeps):
            if _skip is not None and d == _skip:
                continue
            all_deps.append(d)
        qtok = None
        if _inc is not None:
            q = _inc
            self.sems[q][1] += 16
            qtok = (q, self.sems[q][1])
        self.rec.append({"eng": eng, "meth": meth, "args": args, "kw": kw,
                         "deps": all_deps, "qtok": qtok})
        tokval = oid if qtok is None else (note_tok if note_tok is not None
                                           else qtok)
        self._note(eng, rds, wrs, oid if qtok is None else tokval)
        return tokval

    def finalize(self):
        used = set()
        for r in self.rec:
            for d in r["deps"]:
                if isinstance(d, int):
                    used.add(d)
        assign = {}
        for oid, r in enumerate(self.rec):
            if r["qtok"] is not None:
                assign[oid] = ("__dma__", r["qtok"])
                continue
            if oid in used:
                eng = r["eng"]
                q = "t_%s%d" % (eng, self.tok_rr[eng] % self.N_TOK)
                self.tok_rr[eng] += 1
                drain = (q, self.sems[q][1])
                self.sems[q][1] += 1
                assign[oid] = (q, self.sems[q][1])
                r["drain"] = drain
        for oid, r in enumerate(self.rec):
            eng = r["eng"]
            waits = []
            wl = list(r["deps"])
            if "drain" in r:
                wl.append(r["drain"])
            for d in wl:
                if isinstance(d, int):
                    a = assign[d]
                    d = a[1] if a[0] == "__dma__" else a
                sname, val = d
                if self.guar[eng].get(sname, 0) >= val:
                    continue
                self.guar[eng][sname] = val
                waits.append((self.sems[sname][0], val))
            inc = None
            if r["qtok"] is not None:
                inc = (self.sems[r["qtok"][0]][0], 16)
            elif oid in assign:
                a = assign[oid]
                inc = (self.sems[a[0]][0], 1)

            def run(e, meth=r["meth"], args=r["args"], kw=r["kw"],
                    waits=waits, inc=inc):
                for h, v in waits:
                    e.wait_ge(h, v)
                ins = getattr(e, meth)(*args, **kw)
                if inc is not None:
                    ins.then_inc(*inc)
            self.ops[eng].append(run)

    def dma(self, eng, slot, pairs, deps=()):
        q = "q_%s%d" % (eng, slot % self.N_DMA[eng])
        drain = (q, self.sems[q][1])
        final = (q, self.sems[q][1] + 16 * len(pairs))
        for j, (o, i_) in enumerate(pairs):
            dd = list(deps) + [drain] if j == 0 else ()
            self.emit(eng, "dma_start", out=o, in_=i_, deps=dd, _inc=q,
                      note_tok=final, _skip=final)
        return final

    def qtot(self, eng):
        return [("q_%s%d" % (eng, i), self.sems["q_%s%d" % (eng, i)][1])
                for i in range(self.N_DMA[eng])]

    def barrier(self):
        toks = {}
        for e in ENGINES:
            toks[e] = self.emit(e, "nop")
        qvals = []
        for eng in self.N_DMA:
            qvals += self.qtot(eng)
        for e in ENGINES:
            deps = [toks[e2] for e2 in ENGINES if e2 != e] + qvals
            self.emit(e, "nop", deps=deps)


def build_program():
    nc = bass.Bass("TRN2", target_bir_lowering=False)

    def din(name, shape, dt=F32):
        return nc.dram_tensor(name, shape, dt, kind="ExternalInput").ap()

    def dint(name, shape, dt=F32):
        return nc.dram_tensor(name, shape, dt, kind="Internal").ap()

    x_d = din("x", [S, D])
    wup_d = din("wup", [I3, D])
    wgate_d = din("wgate", [I3, D])
    wdown_d = din("wdown", [D, I3])
    bdq_d = din("bdq", [NFC, 128, 128])
    bdk_d = din("bdk", [NFC, 128, 128])
    bdv_d = din("bdv", [NFC, 128, 128])
    wit_d = din("wit", [I3, 4])
    wft_d = din("wft", [I3, 4])
    bi_d = din("bi", [4, 1])
    bf_d = din("bf4", [4, 1])
    lnw_d = din("lnw", [1, D])
    ident_d = din("ident", [128, 128])
    maskd_d = din("maskd", [128, 2048])
    onesk_d = din("onesk", [1, 128])
    sel4_d = din("sel4", [4, 512])
    out_d = nc.dram_tensor("out", [S, D], F32, kind="ExternalOutput").ap()

    pu_d = dint("pu", [I3, S])
    pg_d = dint("pg", [I3, S])
    m1_d = dint("m1", [I3, S])
    op_d = dint("op", [I3, S])
    nohi_d = dint("nohi", [I3, S], BF16)
    nolo_d = dint("nolo", [I3, S], BF16)
    wuphi_d = dint("wuphi", [D, I3], BF16)
    wuplo_d = dint("wuplo", [D, I3], BF16)
    wghi_d = dint("wghi", [D, I3], BF16)
    wglo_d = dint("wglo", [D, I3], BF16)
    wdhi_d = dint("wdhi", [I3, D], BF16)
    wdlo_d = dint("wdlo", [I3, D], BF16)
    xfm_d = dint("xfmsp", [D, S])
    ccin_d = [dint("ccin%d" % k, [1, 2]) for k in range(8)]
    ccout_d = [dint("ccout%d" % k, [1, 2]) for k in range(8)]

    es = ExitStack()

    def sb(name, shape, dt=F32):
        return es.enter_context(nc.sbuf_tensor(name, shape, dt))

    with es:
        xres = sb("xres", [128, NTC * 1024])
        arA = sb("arA", [128, 12288])
        arB = sb("arB", [128, 6144])
        nhi = sb("nhi", [128, 16384], BF16)
        nlo = sb("nlo", [128, 16384], BF16)
        colA = sb("colA", [128, 80])
        gcols = sb("gcols", [128, 192])
        identt = sb("identt", [128, 128])
        onest = sb("onest", [1, 128])
        sel4t = sb("sel4t", [4, 512])
        scrow = sb("scrow", [1, 64])
        ccio = sb("ccio", [1, 4])
        psb = [es.enter_context(nc.psum_tensor("pb%d" % i, [128, 512], F32))
               for i in range(8)]

        arA_bf = arA[:].bitcast(BF16)
        arB_i32 = arB[:].bitcast(I32)
        nhi_f = nhi[:].bitcast(F32)
        nlo_f = nlo[:].bitcast(F32)

        p = Prog(nc, es)
        p.tensor_nop_args = (psb[7][0:1, 0:1], identt[0:1, 0:1])
        E = p.emit
        cc_ev = [0]

        def cA(i):
            return colA[:, i:i + 1]

        def srow(i, n=1):
            return scrow[0:1, i:i + n]

        p.dma("sync", 0, [(xres[:, 1024 * c:1024 * (c + 1)],
                           x_d[128 * c:128 * (c + 1), :]) for c in range(NTC)])
        p.dma("sync", 1, [(identt[:], ident_d[:, :]), (onest[:], onesk_d[:, :]),
                          (sel4t[:], sel4_d[:, :])])
        E("vector", "memset", colA[:, 79:80], 1e-5, tok=True)
        E("vector", "memset", colA[:, 78:79], 1.0, tok=True)
        p.barrier()

        wp_state = {"load": [None, None], "done": [None, None],
                    "stg": [None, None], "t5": None}

        def w_prep(w_nat, hi_dst, lo_dst, O, I, wn_bufs, wdq_bufs,
                   bmb, stgs, banks=(4, 5), dslot=(0, 1)):
            noch = O // 128
            nb = I // 32
            stg_hi = [stgs[0], stgs[1]]
            stg_lo = [stgs[2], stgs[3]]
            tok_load = wp_state["load"]
            tok_done = wp_state["done"]
            tok_stg = wp_state["stg"]
            bm = bmb[:, 0:nb]
            sA = bmb[:, 128:128 + nb]
            rs = bmb[:, 256:256 + nb]
            for och in range(noch):
                b = och % 2
                wnat = wn_bufs[b]
                wdq = wdq_bufs[b]
                dl = [tok_done[b], wp_state.get("t5_%d" % b)]
                if och < 2:
                    dl += [tok_done[0], tok_done[1], wp_state.get("t5_0"),
                           wp_state.get("t5_1"), tok_stg[0], tok_stg[1]]
                tok_load[b] = p.dma("sync", b, [(wnat,
                                w_nat[128 * och:128 * (och + 1), :])],
                                deps=dl)
                w3 = wnat.rearrange("p (g e) -> p g e", e=32)
                t1 = E("vector", "reduce_max", bm.rearrange("p (g o) -> p g o", o=1),
                       w3, axis=AX.X, apply_absolute_value=True,
                       deps=[tok_load[b]], tok=True)
                t2 = E("vector", "tensor_scalar", sA, bm, 1.0 / 7.0, 1e-12,
                       ALU.mult, ALU.max, deps=[t1], tok=True)
                t3 = E("vector", "reciprocal", rs, sA, deps=[t2], tok=True)
                q32 = wnat.bitcast(I32)
                t4 = E("vector", "tensor_tensor",
                       q32.rearrange("p (g e) -> p g e", e=32), w3,
                       rs.rearrange("p (g o) -> p g o", o=1).broadcast_to((128, nb, 32)),
                       ALU.mult, deps=[t3], tok=True)
                t5 = E("vector", "tensor_tensor",
                       wdq.rearrange("p (g e) -> p g e", e=32),
                       q32.rearrange("p (g e) -> p g e", e=32),
                       sA.rearrange("p (g o) -> p g o", o=1).broadcast_to((128, nb, 32)),
                       ALU.mult, deps=[t4, tok_done[0], tok_done[1]], tok=True)
                wp_state["t5_%d" % b] = t5
                last_tp = None
                for ts_ in range(I // 512):
                    bank = psb[banks[ts_ % 2]]
                    sb_ = ts_ % 2
                    for k in range(4):
                        last_tp = E("tensor", "transpose", bank[:, 128 * k:128 * (k + 1)],
                                    wdq[:, 512 * ts_ + 128 * k: 512 * ts_ + 128 * (k + 1)],
                                    identt[:],
                                    deps=[t5, tok_stg[sb_]] if k == 0 else [],
                                    tok=(k == 3))
                    th = E("vector", "tensor_copy", stg_hi[sb_], bank[:],
                           deps=[last_tp, tok_stg[sb_]], tok=True)
                    tl = E("vector", "tensor_sub", stg_lo[sb_], bank[:], stg_hi[sb_],
                           deps=[last_tp, th], tok=True)
                    if ts_ == I // 512 - 1:
                        tok_done[b] = tl
                    dst = slice(512 * ts_, 512 * (ts_ + 1))
                    hi_ap = hi_dst[dst, 128 * och:128 * (och + 1)] \
                        .rearrange("(k q) c -> q k c", q=128)
                    lo_ap = lo_dst[dst, 128 * och:128 * (och + 1)] \
                        .rearrange("(k q) c -> q k c", q=128)
                    tok_stg[sb_] = p.dma("gpsimd", dslot[sb_], [
                        (hi_ap, stg_hi[sb_].rearrange("p (k c) -> p k c", c=128)),
                        (lo_ap, stg_lo[sb_].rearrange("p (k c) -> p k c", c=128))],
                        deps=[th, tl])

        nhi_bf = nhi[:]
        nlo_bf = nlo[:]
        if N_PHASES >= 1:
            pass

        def allreduce_max2(v0_ap, v1_ap, tok_in):
            k = cc_ev[0]
            cc_ev[0] += 1
            tc0 = E("vector", "tensor_copy", ccio[0:1, 0:1], v0_ap,
                    deps=tok_in, tok=True)
            tc1 = E("vector", "tensor_copy", ccio[0:1, 1:2], v1_ap,
                    deps=tok_in, tok=True)
            td = E("gpsimd", "dma_start", out=ccin_d[k][:, :], in_=ccio[0:1, 0:2],
                   deps=[tc0, tc1], dma=True)
            tcc = E("gpsimd", "collective_compute", "AllReduce", ALU.max,
                    RG, [ccin_d[k][:, :]], [ccout_d[k][:, :]],
                    deps=[td], tok=True)
            tr = E("gpsimd", "dma_start", out=ccio[0:1, 2:4], in_=ccout_d[k][:, :],
                   deps=[tcc], dma=True)
            return tr

        def bcast_cols(src_row_ap, n, dst_col0, deps):
            tm = E("tensor", "matmul", psb[6][:, 0:n], onest[:], src_row_ap,
                   start=True, stop=True, deps=deps, tok=True)
            return E("scalar", "activation", colA[:, dst_col0:dst_col0 + n],
                     psb[6][:, 0:n], ACT.Identity, deps=[tm], tok=True)

        last_bcast = [None]

        def bcast_cols2(src_row_ap, n, dst_col0, deps):
            tm = E("tensor", "matmul", psb[6][:, 0:n], onest[:], src_row_ap,
                   start=True, stop=True, deps=list(deps) + [last_bcast[0]],
                   tok=True)
            tev = E("vector", "tensor_copy", colA[:, dst_col0:dst_col0 + n],
                    psb[6][:, 0:n], deps=[tm], tok=True)
            last_bcast[0] = tev
            return tev

        def allred2(tok_deps, ccio_deps):
            k = cc_ev[0]
            cc_ev[0] += 1
            td = p.dma("gpsimd", 6, [(ccin_d[k][:, :], ccio[0:1, 0:2])],
                       deps=list(tok_deps) + list(ccio_deps))
            tcc = E("gpsimd", "collective_compute", "AllReduce", ALU.max,
                    RG, [ccin_d[k][:, :]], [ccout_d[k][:, :]], deps=[td], tok=True)
            return p.dma("gpsimd", 6, [(ccio[0:1, 2:4], ccout_d[k][:, :])],
                         deps=[tcc])

        def ffn_phase(first):
            def xln_ap(c):
                if c < 12:
                    return arA[:, 1024 * c:1024 * (c + 1)]
                return arB[:, 1024 * (c - 12):1024 * (c - 11)]
            tok_xln = []
            tok_absm = []
            junk = arB[:, 4096:5120]
            for c in range(NTC):
                xc = xres[:, 1024 * c:1024 * (c + 1)]
                ts_ = E("vector", "reduce_sum", cA(48 + c), xc, axis=AX.X, tok=True)
                tn = E("scalar", "activation", cA(c), cA(48 + c), ACT.Identity,
                       scale=-1.0 / D, deps=[ts_], tok=True)
                tq = E("scalar", "activation", junk, xc, ACT.Square,
                       bias=cA(c), accum_out=cA(48 + c), deps=[tn], tok=True)
                tr0 = E("scalar", "activation", cA(48 + c), cA(48 + c), ACT.Sqrt,
                        scale=1.0 / D, bias=cA(79), deps=[tq], tok=True)
                tr = E("vector", "reciprocal", cA(16 + c), cA(48 + c),
                       deps=[tr0], tok=True)
                tx = E("vector", "tensor_scalar", xln_ap(c), xc, cA(c), cA(16 + c),
                       ALU.add, ALU.mult, deps=[tn, tr], tok=True)
                tok_xln.append(tx)
                ta = E("vector", "reduce_max", cA(32 + c), xln_ap(c), axis=AX.X,
                       apply_absolute_value=True, deps=[tx], tok=True)
                tok_absm.append(ta)
            if first:
                w_prep(wdown_d, wdhi_d, wdlo_d, D, I3,
                       wn_bufs=[nhi_f[:, 0:3072], nhi_f[:, 3072:6144]],
                       wdq_bufs=[nlo_f[:, 0:3072], nlo_f[:, 3072:6144]],
                       bmb=nlo_f[:, 6144:6528],
                       stgs=[nlo_bf[:, 13056:13568], nlo_bf[:, 13568:14080],
                             nlo_bf[:, 14080:14592], nlo_bf[:, 14592:15104]],
                       banks=(4, 5), dslot=(4, 5))
            rowb = arB[0:1, 4096:6144]
            tdm = p.dma("gpsimd", 7, [(rowb, colA[:, 32:48])], deps=tok_absm)
            trm = E("vector", "reduce_max", ccio[0:1, 0:1], rowb, axis=AX.X,
                    deps=[tdm], tok=True)
            tz = E("vector", "memset", ccio[0:1, 1:2], 0.0, tok=True)
            tr1 = allred2([trm, tz], [])
            ta1 = E("vector", "reciprocal", srow(0), ccio[0:1, 2:3], deps=[tr1], tok=True)
            ta2 = E("vector", "tensor_scalar", srow(1), srow(0), 32767.0, None,
                    ALU.mult, deps=[ta1], tok=True)
            E("vector", "tensor_scalar", srow(2), ccio[0:1, 2:3],
              1.0 / 32767.0, None, ALU.mult, deps=[tr1], tok=True)
            tbc1 = bcast_cols2(srow(1), 1, 64, [ta2])
            if FFN_STAGES <= 1:
                p.barrier()
                return

            ni32 = arB_i32[:, 4096:5120]
            nf32 = arB[:, 5120:6144]
            tok_ev = {}
            tp_done = [None, None]
            nhi3 = nhi[:].rearrange("p (f t) -> p f t", t=2048)
            nlo3 = nlo[:].rearrange("p (f t) -> p f t", t=2048)
            for c in range(NTC):
                t1 = E("scalar", "activation", ni32, xln_ap(c), ACT.Identity,
                       scale=cA(64), deps=[tok_xln[c], tbc1, tok_ev.get(c - 1)],
                       tok=True)
                t2 = E("scalar", "activation", nf32, ni32, ACT.Identity,
                       deps=[t1], tok=True)
                for half in range(2):
                    bank = psb[4 + half]
                    ltp = None
                    for kk in range(4):
                        f = 4 * half + kk
                        ltp = E("tensor", "transpose", bank[:, 128 * kk:128 * (kk + 1)],
                                nf32[:, 128 * f:128 * (f + 1)], identt[:],
                                deps=[t2, tp_done[half]] if kk == 0 else [],
                                tok=(kk == 3))
                    dsth = nhi3[:, 4 * half:4 * half + 4, 128 * c:128 * (c + 1)]
                    dstl = nlo3[:, 4 * half:4 * half + 4, 128 * c:128 * (c + 1)]
                    th = E("vector", "tensor_copy",
                           dsth, bank[:].rearrange("p (k c) -> p k c", c=128),
                           deps=[ltp], tok=True)
                    tl = E("vector", "tensor_tensor",
                           dstl, bank[:].rearrange("p (k c) -> p k c", c=128),
                           dsth, ALU.subtract, deps=[ltp, th], tok=True)
                    tp_done[half] = tl
                    if half == 1:
                        tok_ev[c] = tl
            tok_n = [tp_done[0], tp_done[1], tok_ev[NTC - 1]]
            if FFN_STAGES <= 2:
                p.barrier()
                return

            tmu = E("vector", "memset", cA(71), 0.0, deps=tok_absm, tok=True)
            tmg = E("vector", "memset", cA(72), 0.0, tok=True)
            tok_wload = {}
            tok_mmdone = {}
            tok_evac = {}
            tok_stgout = [None, None, None, None]
            srcs = [(wuphi_d, 0), (wuplo_d, 1), (wghi_d, 2), (wglo_d, 3)]
            NOG = min(12, NOG_LIM)
            stg_ctr = [0]
            tcp = None
            def wprep_unit(nat_d, m, och, b):
                o2 = och % 2
                wnat = arA[:, 10240:11264]
                wdq = arA[:, 11264:12288]
                tokL = p.dma("sync", 6 + (och % 2), [(wnat,
                             nat_d[128 * och:128 * (och + 1), :])])
                w3 = wnat.rearrange("p (g e) -> p g e", e=32)
                bm = arB[:, 5120:5152]
                sA = arB[:, 5184:5216]
                rs = arB[:, 5248:5280]
                E("vector", "reduce_max", bm.rearrange("p (g o) -> p g o", o=1),
                  w3, axis=AX.X, apply_absolute_value=True, deps=[tokL],
                  tok=True)
                E("vector", "tensor_scalar", sA, bm, 1.0 / 7.0, 1e-12,
                  ALU.mult, ALU.max, tok=True)
                E("vector", "reciprocal", rs, sA, tok=True)
                q32 = wnat.bitcast(I32)
                E("vector", "tensor_tensor",
                  q32.rearrange("p (g e) -> p g e", e=32), w3,
                  rs.rearrange("p (g o) -> p g o", o=1).broadcast_to((128, 32, 32)),
                  ALU.mult, tok=True)
                E("vector", "tensor_tensor",
                  wdq.rearrange("p (g e) -> p g e", e=32),
                  q32.rearrange("p (g e) -> p g e", e=32),
                  sA.rearrange("p (g o) -> p g o", o=1).broadcast_to((128, 32, 32)),
                  ALU.mult, tok=True)
                whi3 = arA_bf[:, 8192 * b + 2048 * (2 * m):
                              8192 * b + 2048 * (2 * m + 1)] \
                    .rearrange("p (f c) -> p f c", c=256)
                wlo3 = arA_bf[:, 8192 * b + 2048 * (2 * m + 1):
                              8192 * b + 2048 * (2 * m + 2)] \
                    .rearrange("p (f c) -> p f c", c=256)
                toks = []
                for ts_ in range(2):
                    bank = psb[4 + ts_]
                    ltp = None
                    for kk in range(4):
                        ltp = E("tensor", "transpose",
                                bank[:, 128 * kk:128 * (kk + 1)],
                                wdq[:, 512 * ts_ + 128 * kk:
                                    512 * ts_ + 128 * (kk + 1)], identt[:],
                                tok=(kk == 3))
                    dsth = whi3[:, 4 * ts_:4 * ts_ + 4, 128 * o2:128 * (o2 + 1)]
                    dstl = wlo3[:, 4 * ts_:4 * ts_ + 4, 128 * o2:128 * (o2 + 1)]
                    th = E("vector", "tensor_copy", dsth,
                           bank[:].rearrange("p (k c) -> p k c", c=128),
                           deps=[ltp], tok=True)
                    tl = E("vector", "tensor_tensor", dstl,
                           bank[:].rearrange("p (k c) -> p k c", c=128),
                           dsth, ALU.subtract, deps=[ltp], tok=True)
                    toks += [th, tl]
                return toks

            for og in range(NOG):
                b = og % 2
                dep_reuse = [tok_mmdone.get(og - 2)] + (tok_n if og < 2 else [])
                if first:
                    toks = []
                    for m, nat_d in enumerate((wup_d, wgate_d)):
                        for o2 in range(2):
                            toks += wprep_unit(nat_d, m, 2 * og + o2, b)
                    for m, (hi_d, lo_d) in enumerate(((wuphi_d, wuplo_d),
                                                      (wghi_d, wglo_d))):
                        for s_, dst_d in ((2 * m, hi_d), (2 * m + 1, lo_d)):
                            sap = dst_d[:, 256 * og:256 * (og + 1)] \
                                .rearrange("(f q) c -> q f c", q=128)
                            buf = arA_bf[:, 8192 * b + 2048 * s_:
                                         8192 * b + 2048 * (s_ + 1)] \
                                .rearrange("p (f c) -> p f c", c=256)
                            p.dma("gpsimd", 6, [(sap, buf)])
                    tok_wload[og] = toks
                else:
                    wpairs = []
                    for (src_, s_) in srcs:
                        dst = arA_bf[:, 8192 * b + 2048 * s_:
                                     8192 * b + 2048 * (s_ + 1)] \
                            .rearrange("p (f c) -> p f c", c=256)
                        sap = src_[:, 256 * og:256 * (og + 1)] \
                            .rearrange("(f q) c -> q f c", q=128)
                        wpairs.append((dst, sap))
                    tok_wload[og] = p.dma("sync", 2 + b, wpairs, deps=dep_reuse)
                last_mm = None
                for mat in range(2):
                    wh3 = arA_bf[:, 8192 * b + 2048 * (2 * mat):
                                 8192 * b + 2048 * (2 * mat + 1)] \
                        .rearrange("p (f c) -> p f c", c=256)
                    wl3 = arA_bf[:, 8192 * b + 2048 * (2 * mat + 1):
                                 8192 * b + 2048 * (2 * mat + 2)] \
                        .rearrange("p (f c) -> p f c", c=256)
                    for o2 in range(2):
                        och = 2 * og + o2
                        for tt in range(4):
                            bank_i = tt % 4
                            bank = psb[bank_i]
                            sp = stg_ctr[0] % 4
                            stg_ctr[0] += 1
                            stg = arA[:, 8192 + 512 * sp:8192 + 512 * (sp + 1)]
                            first_ = True
                            for f in range(NFC):
                                whi = wh3[:, f, 128 * o2:128 * (o2 + 1)]
                                wlo = wl3[:, f, 128 * o2:128 * (o2 + 1)]
                                rhs_h = nhi3[:, f, 512 * tt:512 * (tt + 1)]
                                rhs_l = nlo3[:, f, 512 * tt:512 * (tt + 1)]
                                dd = ([tok_wload[og],
                                       tok_evac.get(bank_i)] + tok_n) \
                                    if first_ else []
                                E("tensor", "matmul", bank[:], whi, rhs_h,
                                  start=first_, stop=False, deps=dd)
                                E("tensor", "matmul", bank[:], whi, rhs_l,
                                  start=False, stop=False)
                                last_mm = E("tensor", "matmul", bank[:], wlo, rhs_h,
                                            start=False, stop=(f == NFC - 1),
                                            tok=(f == NFC - 1))
                                first_ = False
                            tr_ = E("vector", "reduce_max", cA(73), bank[:],
                                    axis=AX.X, apply_absolute_value=True,
                                    deps=[last_mm], tok=True)
                            E("vector", "tensor_tensor", cA(71 + mat),
                              cA(71 + mat), cA(73), ALU.max,
                              deps=[tr_, tmu, tmg], tok=True)
                            tcp = E("vector", "tensor_copy", stg, bank[:],
                                    deps=[last_mm, tok_stgout[sp]], tok=True)
                            tok_evac[bank_i] = tcp
                            dst_d = (pu_d if mat == 0 else pg_d)
                            tok_stgout[sp] = p.dma(
                                "gpsimd", sp,
                                [(dst_d[128 * och:128 * (och + 1),
                                        512 * tt:512 * (tt + 1)], stg)],
                                deps=[tcp])
                tok_mmdone[og] = last_mm
            tok_ug_end = [tok_mmdone[NOG - 1], tok_evac.get(0), tok_evac.get(1),
                          tok_evac.get(2), tok_evac.get(3)] + \
                list(tok_stgout)
            if FFN_STAGES <= 2.5:
                p.barrier()
                return
            t7a = p.dma("gpsimd", 7, [(nhi_f[0:1, 1024:1152], cA(71)),
                                      (nhi_f[0:1, 1152:1280], cA(72))],
                        deps=tok_ug_end)
            t7b = t7a
            tru = E("vector", "reduce_max", ccio[0:1, 0:1], nhi_f[0:1, 1024:1152],
                    axis=AX.X, deps=[t7a, tr1], tok=True)
            trg = E("vector", "reduce_max", ccio[0:1, 1:2], nhi_f[0:1, 1152:1280],
                    axis=AX.X, deps=[t7b], tok=True)
            tr2 = allred2([tru, trg], [])
            tb1 = E("vector", "reciprocal", srow(0), ccio[0:1, 2:3], deps=[tr2], tok=True)
            tb2 = E("vector", "tensor_scalar", srow(3), srow(0), 32767.0, None,
                    ALU.mult, deps=[tb1], tok=True)
            tb3 = E("vector", "tensor_scalar", srow(8), ccio[0:1, 2:3], 1.0 / 32767.0,
                    None, ALU.mult, deps=[tr2], tok=True)
            tb4 = E("vector", "tensor_tensor", srow(4), srow(8), srow(2), ALU.mult,
                    deps=[tb3], tok=True)
            tb5 = E("vector", "reciprocal", srow(0), ccio[0:1, 3:4], deps=[tr2, tb2],
                    tok=True)
            tb6 = E("vector", "tensor_scalar", srow(5), srow(0), 32767.0, None,
                    ALU.mult, deps=[tb5], tok=True)
            tb7 = E("vector", "tensor_scalar", srow(8), ccio[0:1, 3:4], 1.0 / 32767.0,
                    None, ALU.mult, deps=[tb5, tb4], tok=True)
            tb8 = E("vector", "tensor_tensor", srow(6), srow(8), srow(2), ALU.mult,
                    deps=[tb7], tok=True)
            tb9 = E("vector", "tensor_copy", srow(9), srow(3), deps=[tb2], tok=True)
            tba = E("vector", "tensor_copy", srow(10), srow(5), deps=[tb6], tok=True)
            tbb = E("vector", "tensor_copy", srow(11), srow(6), deps=[tb8], tok=True)
            tbc2 = bcast_cols2(srow(9, 3), 3, 65, [tb9, tba, tbb])
            if FFN_STAGES <= 3:
                p.barrier()
                return

            tmm = E("vector", "memset", cA(74), 0.0, tok=True)
            tok_use = {}
            tok_m1w = {}
            st_e2 = {}
            def e2_head(i):
                b = i % 4
                pgin = arA[:, 2048 * b:2048 * (b + 1)]
                ngi = arB_i32[:, 2048 * (i % 2):2048 * (i % 2 + 1)]
                sigb = arA[:, 8192 + 2048 * (i % 2):8192 + 2048 * (i % 2 + 1)]
                d0 = (tok_ug_end if i < 2 else [])
                if i == 0:
                    d0 = d0 + p.qtot("gpsimd")
                trd_ = p.dma("sync", i % 4, [(pgin, pg_d[128 * i:128 * (i + 1), :])],
                             deps=d0)
                tg = E("vector", "tensor_scalar", ngi, pgin, cA(66), None,
                       ALU.mult, deps=[trd_, tbc2], tok=True)
                tsg = E("scalar", "activation", sigb, ngi, ACT.Sigmoid,
                        scale=cA(67), deps=[tg], tok=True)
                st_e2[i] = (pgin, ngi, sigb)
            def e2_tail(i):
                pgin, ngi, sigb = st_e2[i]
                m1o = pgin
                ri32 = sigb.bitcast(I32)
                tri = E("vector", "tensor_scalar", ri32, sigb, 65536.0, None,
                        ALU.mult, tok=True)
                tm1 = E("vector", "scalar_tensor_tensor", m1o, ri32, 65535.0,
                        ngi, ALU.min, ALU.mult,
                        deps=[tri, tok_m1w.get(i - 2)], tok=True)
                trd2 = E("vector", "reduce_max", cA(73), m1o, axis=AX.X,
                         apply_absolute_value=True, deps=[tm1], tok=True)
                tok_use[i] = E("vector", "tensor_tensor", cA(74), cA(74),
                               cA(73), ALU.max, deps=[trd2, tmm], tok=True)
                tok_m1w[i] = p.dma("gpsimd", i % 4,
                                   [(m1_d[128 * i:128 * (i + 1), :], m1o)],
                                   deps=[tm1])
            e2_head(0)
            for i in range(1, NIC):
                e2_head(i)
                e2_tail(i - 1)
            e2_tail(NIC - 1)
            t8a = p.dma("gpsimd", 7, [(nhi_f[0:1, 1024:1152], cA(74))],
                        deps=[tok_use[NIC - 1]])
            trm1 = E("vector", "reduce_max", ccio[0:1, 0:1], nhi_f[0:1, 1024:1152],
                     axis=AX.X, deps=[t8a, tr2], tok=True)
            tz3 = E("vector", "memset", ccio[0:1, 1:2], 0.0, deps=[tr2], tok=True)
            tr3 = allred2([trm1, tz3], [])
            tc1 = E("vector", "reciprocal", srow(0), ccio[0:1, 2:3], deps=[tr3], tok=True)
            tc2 = E("vector", "tensor_scalar", srow(12), srow(0), 32767.0, None,
                    ALU.mult, deps=[tc1], tok=True)
            tc3 = E("vector", "tensor_tensor", srow(8), ccio[0:1, 2:3], srow(6),
                    ALU.mult, deps=[tr3, tb8], tok=True)
            tc4 = E("vector", "tensor_scalar", srow(8), srow(8),
                    1.0 / (65536.0 * 32767.0), None, ALU.mult, deps=[tc3], tok=True)
            tc5 = E("vector", "tensor_tensor", srow(13), srow(8), srow(4), ALU.mult,
                    deps=[tc4, tb4], tok=True)
            tbc3 = bcast_cols2(srow(12), 1, 68, [tc2])
            if FFN_STAGES <= 4:
                p.barrier()
                return

            tmo = E("vector", "memset", cA(75), 0.0, deps=[tok_use[NIC - 1]],
                    tok=True)
            tok_use2 = {}
            tok_opw = {}
            st_e3 = {}
            def e3_head(i):
                b = i % 3
                puin = arA[:, 2048 * b:2048 * (b + 1)]
                m1in = arA[:, 6144 + 2048 * b:6144 + 2048 * (b + 1)]
                d0 = [tok_use[NIC - 1]]
                if i == 0:
                    d0 += p.qtot("gpsimd")
                trdu = p.dma("sync", i % 3, [(puin,
                             pu_d[128 * i:128 * (i + 1), :])], deps=d0)
                trdm = p.dma("sync", 3 + i % 3, [(m1in,
                             m1_d[128 * i:128 * (i + 1), :])], deps=d0)
                nupi = arB_i32[:, 2048 * (i % 2):2048 * (i % 2 + 1)]
                ng2i = m1in.bitcast(I32)
                tnu = E("vector", "tensor_scalar", nupi, puin, cA(65), None,
                        ALU.mult, deps=[trdu, tbc2], tok=True)
                tng = E("scalar", "activation", ng2i, m1in, ACT.Identity,
                        scale=cA(68), deps=[trdm, tbc3], tok=True)
                st_e3[i] = (puin, m1in, nupi, ng2i, tnu, tng)
            def e3_tail(i):
                puin, m1in, nupi, ng2i, tnu, tng = st_e3[i]
                opo = puin
                top = E("vector", "tensor_tensor", opo, ng2i, nupi, ALU.mult,
                        deps=[tnu, tng, tok_opw.get(i - 2)], tok=True)
                trd2 = E("vector", "reduce_max", cA(73), opo, axis=AX.X,
                         apply_absolute_value=True, deps=[top], tok=True)
                tok_use2[i] = E("vector", "tensor_tensor", cA(75), cA(75),
                                cA(73), ALU.max, deps=[trd2, tmo], tok=True)
                tok_opw[i] = p.dma("gpsimd", i % 4,
                                   [(op_d[128 * i:128 * (i + 1), :], opo)],
                                   deps=[top])
            e3_head(0)
            for i in range(1, NIC):
                e3_head(i)
                e3_tail(i - 1)
            e3_tail(NIC - 1)
            t9a = p.dma("gpsimd", 7, [(nhi_f[0:1, 1024:1152], cA(75))],
                        deps=[tok_use2[NIC - 1]])
            trop = E("vector", "reduce_max", ccio[0:1, 0:1], nhi_f[0:1, 1024:1152],
                     axis=AX.X, deps=[t9a, tr3], tok=True)
            tz4 = E("vector", "memset", ccio[0:1, 1:2], 0.0, deps=[tr3], tok=True)
            tr4 = allred2([trop, tz4], [])
            td1 = E("vector", "reciprocal", srow(0), ccio[0:1, 2:3], deps=[tr4], tok=True)
            td2 = E("vector", "tensor_scalar", srow(16), srow(0), 32767.0, None,
                    ALU.mult, deps=[td1], tok=True)
            td3 = E("vector", "tensor_scalar", srow(8), ccio[0:1, 2:3], 0.5 / 32767.0,
                    None, ALU.mult, deps=[tr4], tok=True)
            td4 = E("vector", "tensor_tensor", srow(17), srow(8), srow(13), ALU.mult,
                    deps=[td3, tc5, td2], tok=True)
            tbc4 = bcast_cols2(srow(16, 2), 2, 69, [td4])
            if FFN_STAGES <= 5:
                p.barrier()
                return

            tok_use3 = {}
            tok_w3 = {}
            tok_w3b = {}
            st_e4 = {}
            def e4_head(i):
                b = i % 4
                opin = arA[:, 2048 * b:2048 * (b + 1)]
                noi = arB_i32[:, 2048 * (i % 2):2048 * (i % 2 + 1)]
                hib = arA_bf[:, 16384 + 2048 * (i % 2):16384 + 2048 * (i % 2 + 1)]
                d0 = [tok_use2[NIC - 1]]
                if i == 0:
                    d0 += p.qtot("gpsimd")
                trdo = p.dma("sync", i % 4, [(opin,
                             op_d[128 * i:128 * (i + 1), :])], deps=d0)
                tno = E("scalar", "activation", noi, opin, ACT.Identity,
                        scale=cA(69), deps=[trdo, tbc4], tok=True)
                thi = E("scalar", "activation", hib, noi, ACT.Identity,
                        deps=[tno, tok_w3.get(i - 2)], tok=True)
                st_e4[i] = (noi, hib, thi)
            def e4_tail(i):
                b = i % 2
                noi, hib, thi = st_e4[i]
                lob = arA_bf[:, 20480 + 2048 * b:20480 + 2048 * (b + 1)]
                tlo = E("vector", "tensor_tensor", lob, noi, hib, ALU.subtract,
                        deps=[thi, tok_w3b.get(i - 2)], tok=True)
                tok_use3[i] = tlo
                tok_w3[i] = p.dma("gpsimd", i % 4,
                                  [(nohi_d[128 * i:128 * (i + 1), :], hib),
                                   (nolo_d[128 * i:128 * (i + 1), :], lob)],
                                  deps=[thi, tlo])
                tok_w3b[i] = tok_w3[i]
            e4_head(0)
            for i in range(1, NIC):
                e4_head(i)
                e4_tail(i - 1)
            e4_tail(NIC - 1)
            tok_ld = {}
            tok_mmd = {}
            ev_last = {}
            for tg_ in range(4):
                for ich in range(NIC):
                    b = ich % 4
                    nhib = arA_bf[:, 512 * b:512 * (b + 1)]
                    nlob = arA_bf[:, 2048 + 512 * b:2048 + 512 * (b + 1)]
                    wdh = arA_bf[:, 4096 + 1024 * b:4096 + 1024 * (b + 1)]
                    wdl = arA_bf[:, 8192 + 1024 * b:8192 + 1024 * (b + 1)]
                    dep0 = [tok_mmd.get((tg_, ich - 4)),
                            tok_mmd.get((tg_ - 1, NIC - 1))]
                    if tg_ == 0 and ich < 2:
                        dep0 += [tok_w3.get(NIC - 1), tok_w3b.get(NIC - 1),
                                 tok_use3.get(NIC - 1)] + p.qtot("gpsimd")
                    t1 = p.dma("sync", b, [
                        (nhib, nohi_d[128 * ich:128 * (ich + 1),
                                      512 * tg_:512 * (tg_ + 1)]),
                        (nlob, nolo_d[128 * ich:128 * (ich + 1),
                                      512 * tg_:512 * (tg_ + 1)])],
                        deps=dep0)
                    t1b = p.dma("sync", 4 + b, [
                        (wdh, wdhi_d[128 * ich:128 * (ich + 1), :]),
                        (wdl, wdlo_d[128 * ich:128 * (ich + 1), :])],
                        deps=dep0)
                    lm = None
                    for tc4_ in range(4):
                        for ht in range(2):
                            bank = psb[2 * tc4_ + ht]
                            first = (ich == 0)
                            last = (ich == NIC - 1)
                            dd = [t1, t1b]
                            if first:
                                dd += [ev_last.get(tg_ - 1), last_bcast[0]]
                            lh = nhib[:, 128 * tc4_:128 * (tc4_ + 1)]
                            ll = nlob[:, 128 * tc4_:128 * (tc4_ + 1)]
                            rh = wdh[:, 512 * ht:512 * (ht + 1)]
                            rl = wdl[:, 512 * ht:512 * (ht + 1)]
                            E("tensor", "matmul", bank[:], lh, rh,
                              start=first, stop=False, deps=dd)
                            E("tensor", "matmul", bank[:], lh, rl,
                              start=False, stop=False)
                            lm = E("tensor", "matmul", bank[:], ll, rh,
                                   start=False, stop=last,
                                   tok=(ht == 1 and tc4_ == 3))
                    tok_mmd[(tg_, ich)] = lm
                for tc4_ in range(4):
                    c = 4 * tg_ + tc4_
                    for ht in range(2):
                        bank = psb[2 * tc4_ + ht]
                        ev_last[tg_] = E(
                            "vector", "scalar_tensor_tensor",
                            xres[:, 1024 * c + 512 * ht:1024 * c + 512 * (ht + 1)],
                            bank[:], cA(70),
                            xres[:, 1024 * c + 512 * ht:1024 * c + 512 * (ht + 1)],
                            ALU.mult, ALU.add,
                            deps=[tok_mmd[(tg_, NIC - 1)], tbc4], tok=True)


        def attn_phase():
            defq = []

            def drain(n):
                for _ in range(min(n, len(defq))):
                    defq.pop(0)()

            masks = arA[:, 0:2048]
            gamma = arA[:, 3072:4096]
            nhi_b = nhi[:]
            nlo_b = nlo[:]
            qh_b = nhi_b[:, 0:4096]
            ql_b = nhi_b[:, 4096:8192]
            kh_b = nhi_b[:, 8192:12288]
            kl_b = nhi_b[:, 12288:16384]
            vtmh = nlo_b[:, 0:4112]
            vtml = nlo_b[:, 4112:8224]
            vfm = nlo_f[:, 4112:6160]
            cth_b = [nlo_b[:, 12320:12832], nlo_b[:, 13344:13856]]
            ctl_b = [nlo_b[:, 12832:13344], nlo_b[:, 13856:14368]]
            htm = arA[:, 5120:5376]
            htm4 = [nlo_f[:, 4112 + 256 * i:4112 + 256 * (i + 1)] for i in range(4)]
            junk = arA[:, 12032:12288]
            mm_slots = [arA[:, 4096:4608], arA[:, 4608:5120],
                        arA[:, 6144:6656], arA[:, 6656:7168]]
            dsb_s = [arA[:, 2048:2560], arA[:, 2560:3072]]

            def xln_ap(c):
                if c < 12:
                    return arA[:, 1024 * c:1024 * (c + 1)]
                return arB[:, 1024 * (c - 12):1024 * (c - 11)]
            tok_xln = []
            junk0 = nhi_f[:, 0:1024]
            for c in range(NTC):
                xc = xres[:, 1024 * c:1024 * (c + 1)]
                ts_ = E("vector", "reduce_sum", cA(48 + c), xc, axis=AX.X, tok=True)
                tn = E("scalar", "activation", cA(c), cA(48 + c), ACT.Identity,
                       scale=-1.0 / D, deps=[ts_], tok=True)
                tq = E("scalar", "activation", junk0, xc, ACT.Square,
                       bias=cA(c), accum_out=cA(48 + c), deps=[tn], tok=True)
                tr0 = E("scalar", "activation", cA(48 + c), cA(48 + c), ACT.Sqrt,
                        scale=1.0 / D, bias=cA(79), deps=[tq], tok=True)
                tr = E("vector", "reciprocal", cA(16 + c), cA(48 + c),
                       deps=[tr0], tok=True)
                tx = E("vector", "tensor_scalar", xln_ap(c), xc, cA(c), cA(16 + c),
                       ALU.add, ALU.mult, deps=[tn, tr], tok=True)
                tok_xln.append(tx)

            stgA = [arB[:, 4096:4608], arB[:, 4608:5120]]
            for c in range(NTC):
                for half in range(2):
                    bank = psb[4 + half]
                    ltp = None
                    for kk in range(4):
                        f = 4 * half + kk
                        ltp = E("tensor", "transpose",
                                bank[:, 128 * kk:128 * (kk + 1)],
                                xln_ap(c)[:, 128 * f:128 * (f + 1)], identt[:],
                                deps=[tok_xln[c]] if kk == 0 else [], tok=(kk == 3))
                    tev = E("vector", "tensor_copy", stgA[half], bank[:],
                            deps=[ltp], tok=True)
                    dst = xfm_d[512 * half:512 * (half + 1),
                                128 * c:128 * (c + 1)] \
                        .rearrange("(k q) c -> q k c", q=128)
                    p.dma("gpsimd", half, [
                        (dst, stgA[half].rearrange("p (k c) -> p k c", c=128))],
                        deps=[tev])

            wstg = [gcols[:, 128 + 32 * kk:160 + 32 * kk] for kk in range(2)]
            tmp32 = arB[:, 5120:5152]
            tmp32b = arB[:, 5152:5184]
            for kk, wsrc in enumerate((wit_d, wft_d)):
                t1 = p.dma("sync", 4, [
                    (wstg[kk], wsrc[0:1024, :].rearrange("(f q) c -> q f c", q=128)),
                    (tmp32, wsrc[1024:2048, :].rearrange("(f q) c -> q f c", q=128)),
                    (tmp32b, wsrc[2048:3072, :].rearrange("(f q) c -> q f c", q=128))])
                E("vector", "tensor_tensor", wstg[kk], wstg[kk], tmp32, ALU.add,
                  deps=[t1], tok=True)
                E("vector", "tensor_tensor", wstg[kk], wstg[kk], tmp32b, ALU.add,
                  tok=True)
            tbias = p.dma("sync", 5, [(colA[0:4, 76:77], bi_d[:, :]),
                                      (colA[0:4, 77:78], bf_d[:, :]),
                                      (scrow[0:1, 32:33], lnw_d[:, 0:1])])
            trow = p.dma("sync", 5, [(nhi_f[0:1, 1024:2048], lnw_d[:, :])])
            ig_sb = arB[0:4, 0:2048]
            fg_sb = arB[0:4, 2048:4096]
            xchunk = [arA[:, 7168:9216], arA[:, 9216:11264]]
            gmm = {}
            for f in range(NFC):
                tl = p.dma("sync", 6, [(xchunk[f % 2],
                                        xfm_d[128 * f:128 * (f + 1), :])])
                for kk in range(2):
                    w3 = wstg[kk].rearrange("p (f c) -> p f c", c=4)
                    for tt in range(4):
                        gmm[(kk, tt)] = E(
                            "tensor", "matmul", psb[4 * kk + tt][0:4, 0:512],
                            w3[:, f, :], xchunk[f % 2][:, 512 * tt:512 * (tt + 1)],
                            start=(f == 0), stop=(f == NFC - 1),
                            deps=[tl], tok=(f == NFC - 1))
            for kk in range(2):
                dstsb = (ig_sb, fg_sb)[kk]
                for tt in range(4):
                    E("vector", "tensor_copy",
                      dstsb[:, 512 * tt:512 * (tt + 1)],
                      psb[4 * kk + tt][0:4, 0:512],
                      deps=[gmm[(kk, tt)]], tok=True)
            E("vector", "tensor_scalar", ig_sb, ig_sb, colA[0:4, 76:77], None,
              ALU.add, deps=[tbias], tok=True)
            E("vector", "tensor_scalar", fg_sb, fg_sb, colA[0:4, 77:78], None,
              ALU.add, deps=[tbias], tok=True)

            scr = arB[0:4, 4096:6144]
            E("scalar", "activation", scr, fg_sb, ACT.Abs, tok=True)
            E("scalar", "activation", scr, scr, ACT.Exp, scale=-1.0, tok=True)
            E("scalar", "activation", scr, scr, ACT.Ln, bias=colA[0:4, 78:79],
              tok=True)
            E("vector", "tensor_scalar", fg_sb, fg_sb, 0.0, None, ALU.min,
              tok=True)
            E("vector", "tensor_tensor", fg_sb, fg_sb, scr, ALU.subtract,
              tok=True)
            E("vector", "tensor_tensor_scan", scr, fg_sb, fg_sb, 0.0,
              ALU.add, ALU.bypass, tok=True)
            E("vector", "tensor_tensor", ig_sb, ig_sb, scr, ALU.subtract,
              tok=True)
            E("vector", "tensor_tensor_scan", fg_sb, ig_sb, ig_sb, -3e38,
              ALU.max, ALU.bypass, tok=True)
            E("vector", "tensor_tensor", scr, scr, fg_sb, ALU.add, tok=True)
            E("scalar", "activation", scr, scr, ACT.Exp, scale=-1.0, tok=True)
            b_sb, M_sb, em_sb = ig_sb, fg_sb, scr

            for kk, (src_, dst_) in enumerate(((b_sb, gcols[:, 0:64]),
                                               (em_sb, gcols[:, 64:128]))):
                for grp in range(4):
                    bank = psb[1]
                    for s4 in range(4):
                        s16 = 4 * grp + s4
                        E("tensor", "transpose", bank[:, 4 * s4:4 * s4 + 4],
                          src_[:, 128 * s16:128 * (s16 + 1)],
                          identt[0:4, 0:4], tok=True)
                    E("vector", "tensor_copy",
                      dst_[:, 16 * grp:16 * (grp + 1)], bank[:, 0:16],
                      tok=True)

            tpl = E("vector", "tensor_scalar", nhi_f[0:1, 1024:2048],
                    nhi_f[0:1, 1024:2048], 1.0, None, ALU.add, deps=[trow],
                    tok=True)
            for half in range(2):
                tm = E("tensor", "matmul", psb[0][:, 0:512], onest[:],
                       nhi_f[0:1, 1024 + 512 * half:1024 + 512 * (half + 1)],
                       start=True, stop=True, deps=[tpl], tok=True)
                E("vector", "tensor_copy", gamma[:, 512 * half:512 * (half + 1)],
                  psb[0][:], deps=[tm], tok=True)
            p.dma("sync", 7, [(masks, maskd_d[:, :])])

            for h in range(NH):
                bd = arA[:, 11264:12032]
                bdq = bd[:, 0:256]
                bdk = bd[:, 256:512]
                bdv = bd[:, 512:768]
                xfm = arA[:, 7168:11264]
                p.dma("sync", 0, [
                    (bdq.rearrange("p (f c) -> p f c", c=128),
                     bdq_d[2 * h:2 * h + 2, :, :].rearrange("f q c -> q f c")),
                    (bdk.rearrange("p (f c) -> p f c", c=128),
                     bdk_d[2 * h:2 * h + 2, :, :].rearrange("f q c -> q f c")),
                    (bdv.rearrange("p (f c) -> p f c", c=128),
                     bdv_d[2 * h:2 * h + 2, :, :].rearrange("f q c -> q f c"))])
                p.dma("sync", 1, [
                    (xfm.rearrange("p (f c) -> p f c", c=2048),
                     xfm_d[256 * h:256 * (h + 1), :]
                     .rearrange("(f q) c -> q f c", q=128))])
                qkv_ctr = [0]
                for proj, bdw in enumerate((bdq, bdk, bdv)):
                    if proj == 2:
                        drain(len(defq))
                    for fc in range(2):
                        for tt in range(4):
                            qb = psb[qkv_ctr[0] % 2]
                            qkv_ctr[0] += 1
                            tm = E("tensor", "matmul", qb[:],
                                   bdw[:, 128 * fc:128 * (fc + 1)],
                                   xfm[:, 2048 * fc + 512 * tt:
                                       2048 * fc + 512 * (tt + 1)],
                                   start=True, stop=True, tok=True)
                            lo_col = 2048 * fc + 512 * tt
                            if proj == 2:
                                E("vector", "tensor_copy",
                                  vfm[:, 512 * tt:512 * (tt + 1)], qb[:],
                                  deps=[tm], tok=True)
                            elif proj == 0:
                                dh_ = qh_b[:, lo_col:lo_col + 512]
                                dl_ = ql_b[:, lo_col:lo_col + 512]
                                E("vector", "tensor_copy", dh_, qb[:],
                                  deps=[tm], tok=True)
                                E("vector", "tensor_tensor", dl_, qb[:],
                                  dh_, ALU.subtract, tok=True)
                            else:
                                dh_ = kh_b[:, lo_col:lo_col + 512]
                                dl_ = kl_b[:, lo_col:lo_col + 512]
                                E("vector", "tensor_scalar", dh_, qb[:],
                                  0.0625, None, ALU.mult, deps=[tm], tok=True)
                                E("vector", "scalar_tensor_tensor", dl_,
                                  qb[:], 0.0625, dh_, ALU.mult, ALU.subtract,
                                  tok=True)
                        if proj == 2:
                            for grp in range(4):
                                bank = psb[2 + grp % 2]
                                ltp = None
                                for kk in range(4):
                                    jt = 4 * grp + kk
                                    ltp = E("tensor", "transpose",
                                            bank[:, 128 * kk:128 * (kk + 1)],
                                            vfm[:, 128 * jt:128 * (jt + 1)],
                                            identt[:], tok=(kk == 3))
                                dsth = vtmh.rearrange("p (j c) -> p j c", c=257) \
                                    [:, 4 * grp:4 * grp + 4, 128 * fc:128 * (fc + 1)]
                                dstl = vtml.rearrange("p (j c) -> p j c", c=257) \
                                    [:, 4 * grp:4 * grp + 4, 128 * fc:128 * (fc + 1)]
                                E("vector", "tensor_copy", dsth,
                                  bank[:].rearrange("p (k c) -> p k c", c=128),
                                  deps=[ltp], tok=True)
                                E("vector", "tensor_tensor", dstl,
                                  bank[:].rearrange("p (k c) -> p k c", c=128),
                                  dsth, ALU.subtract, tok=True)
                E("vector", "memset",
                  vtmh.rearrange("p (j c) -> p j c", c=257)[:, :, 256:257], 1.0,
                  tok=True)
                E("vector", "memset",
                  vtml.rearrange("p (j c) -> p j c", c=257)[:, :, 256:257], 0.0,
                  tok=True)
                qh3 = qh_b.rearrange("p (f t) -> p f t", t=2048)
                ql3 = ql_b.rearrange("p (f t) -> p f t", t=2048)
                kh3 = kh_b.rearrange("p (f t) -> p f t", t=2048)
                kl3 = kl_b.rearrange("p (f t) -> p f t", t=2048)

                def emit_ST(it, jc):
                    stb = psb[6 + (jc % 2)]
                    first = True
                    lmm = None
                    for fc in range(2):
                        lh = kh3[:, fc, 128 * jc:128 * (jc + 1)]
                        ll = kl3[:, fc, 128 * jc:128 * (jc + 1)]
                        rh = qh3[:, fc, 512 * it:512 * (it + 1)]
                        rl = ql3[:, fc, 512 * it:512 * (it + 1)]
                        E("tensor", "matmul", stb[:], lh, rh,
                          start=first, stop=False)
                        E("tensor", "matmul", stb[:], lh, rl,
                          start=False, stop=False)
                        lmm = E("tensor", "matmul", stb[:], ll, rh,
                                start=False, stop=(fc == 1), tok=(fc == 1))
                        first = False
                    return stb, lmm

                def emit_D_CT(it, jc, stb, lmm):
                    dk = jc - 4 * it
                    esrc = mm_slots[dk] if dk >= 0 else arA[:, 5632:6144]
                    Dsb = dsb_s[jc % 2]
                    E("scalar", "activation", Dsb, esrc, ACT.Exp,
                      scale=-1.0, bias=gcols[:, 4 * jc + h:4 * jc + h + 1],
                      tok=True)
                    E("vector", "tensor_tensor", Dsb, stb[:], Dsb,
                      ALU.mult, deps=[lmm], tok=True)
                    cth = cth_b[jc % 2]
                    ctl = ctl_b[jc % 2]
                    E("gpsimd", "tensor_copy", cth, Dsb, tok=True)
                    E("vector", "tensor_tensor", ctl, Dsb, cth,
                      ALU.subtract, tok=True)
                    return cth, ctl

                def emit_CV(it, jc, cth, ctl):
                    vh = vtmh.rearrange("p (j c) -> p j c", c=257)[:, jc, :]
                    vl = vtml.rearrange("p (j c) -> p j c", c=257)[:, jc, :]
                    for i1 in range(4):
                        if jc > 4 * it + i1:
                            continue
                        cvb = psb[2 + i1]
                        first = (jc == 0)
                        last = (jc == 4 * it + i1)
                        E("tensor", "matmul", cvb[:, 0:257],
                          cth[:, 128 * i1:128 * (i1 + 1)], vh,
                          start=first, stop=False)
                        E("tensor", "matmul", cvb[:, 0:257],
                          cth[:, 128 * i1:128 * (i1 + 1)], vl,
                          start=False, stop=False)
                        E("tensor", "matmul", cvb[:, 0:257],
                          ctl[:, 128 * i1:128 * (i1 + 1)], vh,
                          start=False, stop=last, tok=True)

                def emit_phase1(it):
                    for i1 in range(4):
                        cvb = psb[2 + i1]
                        i16 = 4 * it + i1
                        c0, c1 = 60 + 2 * i1, 61 + 2 * i1
                        E("vector", "reduce_max", cA(c0),
                          cvb[:, 256:257], axis=AX.X,
                          apply_absolute_value=True, tok=True)
                        E("vector", "tensor_tensor", cA(c0), cA(c0),
                          gcols[:, 64 + 4 * i16 + h:64 + 4 * i16 + h + 1],
                          ALU.max, tok=True)
                        E("vector", "tensor_scalar", cA(c0), cA(c0), 1e-8,
                          None, ALU.add, tok=True)
                        E("vector", "reciprocal", cA(c1), cA(c0), tok=True)
                        E("vector", "tensor_scalar", htm4[i1], cvb[:, 0:256],
                          cA(c1), None, ALU.mult, tok=True)

                def defer_mhln(it):
                    for i1 in range(4):
                        c0, c1 = 60 + 2 * i1, 61 + 2 * i1
                        hh = htm4[i1]
                        defq.extend([
                            (lambda hh=hh, c0=c0: E("vector", "reduce_sum",
                             cA(c0), hh, axis=AX.X, tok=True)),
                            (lambda c0=c0: E("vector", "tensor_scalar",
                             cA(c0), cA(c0), -1.0 / DH, None, ALU.mult,
                             tok=True)),
                            (lambda hh=hh, c0=c0: E("vector", "tensor_scalar",
                             hh, hh, cA(c0), None, ALU.add, tok=True)),
                            (lambda hh=hh: E("vector", "tensor_tensor",
                             junk, hh, hh, ALU.mult, tok=True)),
                            (lambda c1=c1: E("vector", "reduce_sum", cA(c1),
                             junk, axis=AX.X, tok=True)),
                        ])
                    def sqrt4():
                        for i1 in range(4):
                            E("scalar", "activation", cA(61 + 2 * i1),
                              cA(61 + 2 * i1), ACT.Sqrt, scale=1.0 / DH,
                              bias=cA(79), tok=True)
                    defq.append(sqrt4)
                    for i1 in range(4):
                        i16 = 4 * it + i1
                        c1 = 61 + 2 * i1
                        hh = htm4[i1]
                        xsl = xres[:, 1024 * i16 + 256 * h:
                                   1024 * i16 + 256 * (h + 1)]
                        defq.extend([
                            (lambda c1=c1: E("vector", "reciprocal", cA(c1),
                             cA(c1), tok=True)),
                            (lambda hh=hh, c1=c1: E("vector",
                             "scalar_tensor_tensor", hh, hh, cA(c1),
                             gamma[:, 256 * h:256 * (h + 1)], ALU.mult,
                             ALU.mult, tok=True)),
                            (lambda hh=hh, xsl=xsl: E("vector",
                             "tensor_tensor", xsl, xsl, hh, ALU.add,
                             tok=True)),
                        ])

                def defer_masks(it):
                    for dk in range(4):
                        defq.append(lambda dk=dk: E(
                            "vector", "tensor_tensor", mm_slots[dk],
                            arA[:, 5632:6144],
                            masks[:, 512 * dk:512 * (dk + 1)], ALU.add,
                            tok=True))

                for it in range(4):
                    tm = E("tensor", "matmul", psb[0][:],
                           sel4t[:, 128 * h:128 * (h + 1)],
                           M_sb[:, 512 * it:512 * (it + 1)],
                           start=True, stop=True, tok=True)
                    E("vector", "tensor_copy", arA[:, 5632:6144], psb[0][:],
                      deps=[tm], tok=True)
                    if it == 0:
                        drain(len(defq))
                        defer_masks(it)
                        drain(4)
                    else:
                        drain(len(defq))
                        emit_phase1(it - 1)
                        defer_masks(it)
                        defer_mhln(it - 1)
                    njc = 4 * it + 4
                    stA = emit_ST(it, 0)
                    pend = (0, stA)
                    for jc in range(1, njc + 1):
                        pj, (stb, lmm) = pend
                        cth, ctl = emit_D_CT(it, pj, stb, lmm)
                        drain(3)
                        if jc <= njc - 1:
                            pend = (jc, emit_ST(it, jc))
                        emit_CV(it, pj, cth, ctl)
                    if it == 3:
                        drain(len(defq))
                        emit_phase1(3)
                        defer_mhln(3)
                        if h == NH - 1:
                            drain(len(defq))

        if N_PHASES >= 2:
            ffn_phase(True)
        if N_PHASES >= 3:
            attn_phase()
        if N_PHASES >= 4:
            ffn_phase(False)

        p.dma("gpsimd", 0, [(out_d[128 * c:128 * (c + 1), :],
                             xres[:, 1024 * c:1024 * (c + 1)])
                            for c in range(NTC)])
        E("gpsimd", "nop", deps=p.qtot("gpsimd"))

        p.finalize()
        block = es.enter_context(nc.Block())

        @block.sync
        def _(e):
            for fn in p.ops["sync"]:
                fn(e)

        @block.vector
        def _(e):
            for fn in p.ops["vector"]:
                fn(e)

        @block.scalar
        def _(e):
            for fn in p.ops["scalar"]:
                fn(e)

        @block.tensor
        def _(e):
            for fn in p.ops["tensor"]:
                fn(e)

        @block.gpsimd
        def _(e):
            for fn in p.ops["gpsimd"]:
                fn(e)

    return nc



def _bd_chunks(w):
    bd = np.zeros((NFC, 128, 128), dtype=np.float32)
    for c in range(NFC):
        for g in range(32):
            blk = w[32 * c + g]
            bd[c, 4 * g:4 * g + 4, 4 * g:4 * g + 4] = blk.T
    return bd


def _host_consts():
    ident = np.eye(128, dtype=np.float32)
    maskd = np.zeros((128, 2048), dtype=np.float32)
    pp = np.arange(128)[:, None]
    cc = np.arange(512)[None, :]
    for k in range(4):
        maskd[:, 512 * k:512 * (k + 1)] = np.where(128 * k + pp > cc, 1e30, 0.0)
    onesk = np.ones((1, 128), dtype=np.float32)
    sel4 = np.zeros((4, 512), dtype=np.float32)
    for h in range(4):
        sel4[h, 128 * h:128 * (h + 1)] = 1.0
    return ident, maskd, onesk, sel4


def kernel(L0_attn_x, L0_attn_wq, L0_attn_wk, L0_attn_wv, L0_attn_wi,
           L0_attn_bi, L0_attn_wf, L0_attn_bf, L0_attn_ln_w,
           L0_ffn_w_gate, L0_ffn_w_up, L0_ffn_w_down):
    x = np.ascontiguousarray(np.asarray(L0_attn_x, dtype=np.float32))
    ident, maskd, onesk, sel4 = _host_consts()
    common = {
        "wup": np.ascontiguousarray(np.asarray(L0_ffn_w_up, np.float32)),
        "wgate": np.ascontiguousarray(np.asarray(L0_ffn_w_gate, np.float32)),
        "wdown": np.ascontiguousarray(np.asarray(L0_ffn_w_down, np.float32)),
        "bdq": _bd_chunks(np.asarray(L0_attn_wq, np.float32)),
        "bdk": _bd_chunks(np.asarray(L0_attn_wk, np.float32)),
        "bdv": _bd_chunks(np.asarray(L0_attn_wv, np.float32)),
        "wit": np.ascontiguousarray(np.asarray(L0_attn_wi, np.float32).T),
        "wft": np.ascontiguousarray(np.asarray(L0_attn_wf, np.float32).T),
        "bi": np.ascontiguousarray(np.asarray(L0_attn_bi, np.float32).reshape(4, 1)),
        "bf4": np.ascontiguousarray(np.asarray(L0_attn_bf, np.float32).reshape(4, 1)),
        "lnw": np.ascontiguousarray(np.asarray(L0_attn_ln_w, np.float32).reshape(1, D)),
        "ident": ident, "maskd": maskd, "onesk": onesk, "sel4": sel4,
    }
    nc = build_program()
    in_maps = [dict(common, x=np.ascontiguousarray(x[c])) for c in range(8)]
    res = run_bass_kernel_spmd(nc, in_maps, core_ids=list(range(8)))
    out = np.stack([res.results[c]["out"] for c in range(8)], axis=0)
    return out.astype(np.float32)
